# version 03
# baseline (speedup 1.0000x reference)
import numpy as np
from contextlib import ExitStack
import concourse.bass as bass
import concourse.mybir as mybir
from concourse.bass_utils import run_bass_kernel_spmd
import ml_dtypes

F32 = mybir.dt.float32
BF16 = mybir.dt.bfloat16
I32 = mybir.dt.int32
AF = mybir.ActivationFunctionType
ALU = mybir.AluOpType
AX = mybir.AxisListType.X

NCORE = 8
T = 2048
NT = 16
DM = 1024
NDC = 8
SEQ = 8192
CAP = 1280
NE = 16
DFF = 4096
NFC = 32
EO_STRIDE = 2 * CAP + 64
FM = T + 256
LN_EPS = 1e-5
ATT_EPS = 1e-4
GROUPS_ALL = [list(range(NCORE))]
GROUPS_BATCH = [[0, 1, 2, 3], [4, 5, 6, 7]]


def shard_inputs(inp):
    x = np.ascontiguousarray(np.asarray(inp["L0_attn_a_x"], dtype=np.float32).reshape(2 * SEQ, DM))
    w1 = np.asarray(inp["L0_ffn_w1"], dtype=np.float32)
    w2 = np.asarray(inp["L0_ffn_w2"], dtype=np.float32)
    w1r = np.ascontiguousarray(w1.reshape(NE, NDC, 128, NFC, 128).transpose(0, 3, 2, 1, 4)
                               .reshape(NE, NFC, 128, NDC * 128))
    gate_wT = np.ascontiguousarray(np.asarray(inp["L0_ffn_gate_w"], dtype=np.float32).T)
    wqkv_b_hm = np.ascontiguousarray(
        np.asarray(inp["L0_attn_b_w_qkv"], dtype=np.float32)
        .reshape(NDC, 128, 3, NDC, 128).transpose(3, 1, 2, 0, 4).reshape(NDC, 128, 3 * DM))
    maps = []
    for c in range(NCORE):
        coremask8 = np.zeros((16, 8), np.float32)
        coremask8[:, :c] = 1.0
        myexp = np.zeros((16, 2), np.float32)
        myexp[2 * c, 0] = 1.0
        myexp[2 * c + 1, 1] = 1.0
        li = np.arange(128, dtype=np.int32)
        hleft = ((c - 1) * 256 + 128 + li) if c % 4 > 0 else (c * 256 + li)
        hright = ((c + 1) * 256 + li) if c % 4 < 3 else (c * 256 + 128 + li)
        maps.append(dict(
            x_in=np.ascontiguousarray(x[c * T:(c + 1) * T]),
            wqkv_a=np.asarray(inp["L0_attn_a_W_qkv"], dtype=np.float32),
            bqkv_a=np.ascontiguousarray(np.asarray(inp["L0_attn_a_b_qkv"], dtype=np.float32).reshape(1, 3 * DM)),
            bq_a_in=np.ascontiguousarray(np.asarray(inp["L0_attn_a_b_qkv"], dtype=np.float32)[:DM].reshape(8, 128).T),
            wout_a=np.asarray(inp["L0_attn_a_W_out"], dtype=np.float32),
            bout_a=np.ascontiguousarray(np.asarray(inp["L0_attn_a_b_out"], dtype=np.float32).reshape(8, 128).T),
            wqkv_b=wqkv_b_hm,
            wproj_b=np.asarray(inp["L0_attn_b_w_proj"], dtype=np.float32),
            bproj_b=np.ascontiguousarray(np.asarray(inp["L0_attn_b_b_proj"], dtype=np.float32).reshape(8, 128).T),
            gate_wT=gate_wT,
            w1_m=np.ascontiguousarray(w1r[2 * c:2 * c + 2]),
            w2_m=np.ascontiguousarray(w2[2 * c:2 * c + 2]),
            coremask8=coremask8,
            myexp=myexp,
            p0col=np.full((128, 1), float((c % 4) * T), np.float32),
            hleft_idx=hleft.reshape(128, 1).astype(np.int32),
            hright_idx=hright.reshape(128, 1).astype(np.int32),
        ))
    return maps


class Sem:
    def __init__(self, h):
        self.h = h
        self.v = 0

    def at(self):
        return (self, self.v)


class MK:

    def __init__(self, nc, n_sub=4):
        self.nc = nc
        self.n_sub = n_sub
        self.ops = {k: [] for k in ("sync", "tensor", "vector", "scalar", "gpsimd")}
        self.bfree = [[] for _ in range(8)]

    def op(self, engine, fn, waits=(), inc=None, inc_n=1):
        if inc is not None:
            inc.v += inc_n
        ws = tuple((s.h, v) for (s, v) in waits if v > 0)

        def run(eng, fn=fn, ws=ws, inc=inc, inc_n=inc_n):
            for h, v in ws:
                eng.wait_ge(h, v)
            ins = fn(eng)
            if inc is not None:
                ins.then_inc(inc.h, inc_n)
        self.ops[engine].append(run)

    def snap(self):
        return [s.at() for s in self.allsems]

    def fence(self):
        sn = self.snap()
        for e in ("sync", "tensor", "vector", "scalar", "gpsimd"):
            self.op(e, lambda eng: None, waits=sn)
        for i in range(8):
            self.bfree[i] = []

    def arb(self, c0, n):
        assert c0 >= 0 and c0 + n <= 70656, (c0, n)
        return self.AR[:, c0:c0 + n]

    def arf(self, c0, n):
        assert c0 % 2 == 0
        return self.arb(c0, 2 * n).bitcast(F32)

    def declare(self, ctx):
        nc = self.nc
        dt = nc.dram_tensor
        self.x_in = dt("x_in", [T, DM], F32, kind="ExternalInput")
        self.wqkv_a = dt("wqkv_a", [DM, 3 * DM], F32, kind="ExternalInput")
        self.bqkv_a = dt("bqkv_a", [1, 3 * DM], F32, kind="ExternalInput")
        self.bq_a_in = dt("bq_a_in", [128, 8], F32, kind="ExternalInput")
        self.wout_a = dt("wout_a", [DM, DM], F32, kind="ExternalInput")
        self.bout_a = dt("bout_a", [128, 8], F32, kind="ExternalInput")
        self.wqkv_b = dt("wqkv_b", [NDC, 128, 3 * DM], F32, kind="ExternalInput")
        self.wproj_b = dt("wproj_b", [DM, DM], F32, kind="ExternalInput")
        self.bproj_b = dt("bproj_b", [128, 8], F32, kind="ExternalInput")
        self.gate_wT_d = dt("gate_wT", [DM, NE], F32, kind="ExternalInput")
        self.w1_m = dt("w1_m", [2, NFC, 128, NDC * 128], F32, kind="ExternalInput")
        self.w2_m = dt("w2_m", [2, DFF, DM], F32, kind="ExternalInput")
        self.coremask8_d = dt("coremask8", [16, 8], F32, kind="ExternalInput")
        self.myexp_d = dt("myexp", [16, 2], F32, kind="ExternalInput")
        self.p0col_d = dt("p0col", [128, 1], F32, kind="ExternalInput")
        self.hleft_d = dt("hleft_idx", [128, 1], I32, kind="ExternalInput")
        self.hright_d = dt("hright_idx", [128, 1], I32, kind="ExternalInput")
        self.y_out = dt("y_out", [T, DM], F32, kind="ExternalOutput")
        import os
        self.dbg = bool(os.environ.get("MKDEBUG"))
        self.moe_calls = 0
        if self.dbg:
            self.dbg_dgi = dt("dbg_dgi", [128, 20], I32, kind="ExternalOutput")
            self.dbg_ieo = dt("dbg_ieo", [128, NT], I32, kind="ExternalOutput")
            self.dbg_isc = dt("dbg_isc", [128, NT], I32, kind="ExternalOutput")
            self.dbg_gk = dt("dbg_gk", [128, NT], F32, kind="ExternalOutput")
            self.dbg_cnt = dt("dbg_cnt", [16, 8], F32, kind="ExternalOutput")
            self.dbg_seg = dt("dbg_seg", [16, 8], F32, kind="ExternalOutput")
            self.dbg_cmp = dt("dbg_cmp", [T, DM], BF16, kind="ExternalOutput")
            self.dbg_cmpo = dt("dbg_cmpo", [NCORE * T, DM], BF16, kind="ExternalOutput")
            self.dbg_eoin = dt("dbg_eoin", [EO_STRIDE, DM], BF16, kind="ExternalOutput")
            self.dbg_disp = dt("dbg_disp", [128, NDC * CAP], BF16, kind="ExternalOutput")
            self.dbg_h = dt("dbg_h", [128, NFC * CAP], BF16, kind="ExternalOutput")
        self.kv_ar_in = dt("kv_ar_in", [128, 2080], F32)
        self.kv_ar_out = dt("kv_ar_out", [128, 2080], F32)
        self.cnt_in = dt("cnt_in", [2, 16], F32)
        self.cnt_out = dt("cnt_out", [16, 16], F32)
        self.cmp_in = dt("cmp_in", [T, DM], BF16)
        self.cmp_out = dt("cmp_out", [NCORE * T, DM], BF16)
        self.eo_in = dt("eo_in", [EO_STRIDE, DM], BF16)
        self.eo_out = dt("eo_out", [NCORE * EO_STRIDE, DM], BF16)
        self.halo_in = dt("halo_in", [256, DM], F32)
        self.halo_out = dt("halo_out", [NCORE * 256, DM], F32)
        it = nc.inline_tensor
        self.c_idf = it(np.eye(128, dtype=np.float32), name="c_idf")
        self.c_idb = it(np.eye(128, dtype=np.float32).astype(ml_dtypes.bfloat16), name="c_idb")
        self.c_1b = it(np.ones((128, 1), dtype=ml_dtypes.bfloat16), name="c_1b")
        self.c_1f = it(np.ones((128, 1), dtype=np.float32), name="c_1f")
        self.c_1r = it(np.ones((1, 128), dtype=np.float32), name="c_1r")
        base_e = np.array([(e // 2) * EO_STRIDE + (e % 2) * CAP for e in range(NE)], np.float32)
        trash_e = np.array([(e // 2) * EO_STRIDE + 2 * CAP for e in range(NE)], np.float32)
        self.c_base = it(np.broadcast_to(base_e, (128, NE)).copy(), name="c_base")
        self.c_trash = it(np.broadcast_to(trash_e, (128, NE)).copy(), name="c_trash")
        self.c_io16 = it(np.broadcast_to(np.arange(16, dtype=np.float32), (128, 16)).copy(), name="c_io16")
        self.c_io8 = it(np.broadcast_to(np.arange(8, dtype=np.float32), (128, 8)).copy(), name="c_io8")
        b2 = np.zeros((2, 128), np.float32)
        b2[0, :64] = 1.0
        b2[1, 64:] = 1.0
        self.c_b2 = it(b2.astype(ml_dtypes.bfloat16), name="c_b2")

        en = ctx.enter_context
        sbt = lambda name, shape, d=F32: en(nc.sbuf_tensor(name, shape, d))
        self.xb = sbt("xb", [128, NT * DM])
        self.AR = sbt("AR", [128, 70656], BF16)
        self.csm = sbt("csm", [128, 64])
        self.idt_f = sbt("idt_f", [128, 128])
        self.idt_b = sbt("idt_b", [128, 128], BF16)
        self.ones_b = sbt("ones_b", [128, 1], BF16)
        self.ones_f = sbt("ones_f", [128, 1])
        self.ones1r = sbt("ones1r", [1, 128])
        self.b2h = sbt("b2h", [2, 128], BF16)
        self.cst = sbt("cst", [128, 64])
        self.cmask8 = sbt("cmask8", [16, 8])
        self.myexp_s = sbt("myexp_s", [16, 2])
        self.cntT = sbt("cntT", [16, 8])
        self.segT = sbt("segT", [16, 8])
        self.cumT = sbt("cumT", [16, 8])
        self.zrow16 = sbt("zrow16", [16, 8])
        self.gwT = sbt("gwT", [128, NDC * NE], BF16)
        self.masks = sbt("masks", [128, NT * NE])
        self.gcol = sbt("gcol", [128, NT])
        self.idx_sc = sbt("idx_sc", [128, NT], I32)
        self.idx_eo = sbt("idx_eo", [128, NT], I32)
        self.gk = sbt("gk", [128, NT])
        self.hl_idx = sbt("hl_idx", [128, 1], I32)
        self.hr_idx = sbt("hr_idx", [128, 1], I32)
        self.bq_a = sbt("bq_a", [128, 8])
        self.bo_a = sbt("bo_a", [128, 8])
        self.bp_b = sbt("bp_b", [128, 8])
        self.dgi = sbt("dgi", [128, 20], I32)
        self.psum = [en(nc.psum_tensor(f"pb{i}", [128, 512], F32)) for i in range(8)]
        self.allsems = []
        for nm in ("sL", "sG", "sT", "sV", "sS", "sC", "sP"):
            s = Sem(en(nc.semaphore(nm)))
            setattr(self, nm, s)
            self.allsems.append(s)
        self.C_EPSLN, self.C_P0, self.C_HPI = 0, 2, 3
        self.C_BASEE, self.C_TRASHE, self.C_IOTA16, self.C_IOTA8 = 8, 24, 40, 56
        self.LNT0 = 0
        self.LNF0 = 16384
        self.WB0 = 34816
        self.SC0 = 51200
        self.HS0 = 0
        self.W2G0 = 40960
        self.EOS0 = 49152

    def bank(self, i, extra=()):
        w = list(self.bfree[i]) + list(extra)
        self.bfree[i] = []
        return self.psum[i], w

    def bdone(self, i, waits):
        self.bfree[i] = list(waits)

    def p0(self):
        s = self
        sL, sG, sV = s.sL, s.sG, s.sV

        def d(dst, src):
            s.op("sync", lambda sy, dst=dst, src=src: sy.dma_start(out=dst, in_=src), inc=sL, inc_n=16)
        d(s.idt_f[:], s.c_idf[:, :])
        d(s.idt_b[:], s.c_idb[:, :])
        d(s.ones_b[:], s.c_1b[:, :])
        d(s.ones_f[:], s.c_1f[:, :])
        d(s.ones1r[:], s.c_1r[:, :])
        d(s.b2h[:], s.c_b2[:, :])
        d(s.cst[:, s.C_P0:s.C_P0 + 1], s.p0col_d[:, :])
        d(s.cst[:, s.C_BASEE:s.C_BASEE + 16], s.c_base[:, :])
        d(s.cst[:, s.C_TRASHE:s.C_TRASHE + 16], s.c_trash[:, :])
        d(s.cst[:, s.C_IOTA16:s.C_IOTA16 + 16], s.c_io16[:, :])
        d(s.cst[:, s.C_IOTA8:s.C_IOTA8 + 8], s.c_io8[:, :])
        d(s.cmask8[:], s.coremask8_d[:, :])
        d(s.myexp_s[:], s.myexp_d[:, :])
        d(s.hl_idx[:], s.hleft_d[:, :])
        d(s.hr_idx[:], s.hright_d[:, :])
        d(s.bq_a[:], s.bq_a_in[:, :])
        d(s.bo_a[:], s.bout_a[:, :])
        d(s.bp_b[:], s.bproj_b[:, :])
        for t in range(NT):
            d(s.xb[:, t * DM:(t + 1) * DM], s.x_in[t * 128:(t + 1) * 128, :])
        for dc in range(NDC):
            s.op("gpsimd", lambda g, dc=dc: g.dma_start(out=s.gwT[:, dc * NE:(dc + 1) * NE],
                                                        in_=s.gate_wT_d[dc * 128:(dc + 1) * 128, :]),
                 inc=sG, inc_n=16)
        s.op("vector", lambda v: v.memset(s.cst[:, s.C_EPSLN:s.C_EPSLN + 1], LN_EPS), inc=sV)
        s.op("vector", lambda v: v.memset(s.cst[:, s.C_HPI:s.C_HPI + 1], float(np.pi / 2)), inc=sV)
        s.op("vector", lambda v: v.memset(s.zrow16[:], 0.0), inc=sV)
        self.fence()

    def lnt(self, halo=False):
        s = self
        sV, sS, sT = s.sV, s.sS, s.sT
        HXL = s.SC0 + 4096
        HXR = s.SC0 + 8192
        tiles = [("own", t) for t in range(NT)]
        if halo:
            tiles += [("hl", 0), ("hr", 0)]
        for kind, t in tiles:
            cc = [s.csm[:, i:i + 1] for i in range(7)]
            c_sum, c_mean, c_m2, c_var, c_sd, c_rstd, c_sumsq = cc
            if kind == "own":
                xt = s.xb[:, t * DM:(t + 1) * DM]
                tmdst = s.arb(s.LNT0 + t * DM, DM)
                fmcol = 128 + t * 128
            else:
                xt = s.arf(HXL if kind == "hl" else HXR, DM)
                tmdst = s.arb(s.SC0 + 12288 + (0 if kind == "hl" else DM), DM)
                fmcol = 0 if kind == "hl" else (T + 128)
            trash = s.arf(s.SC0, 1024)
            sn = [sV.at(), sS.at(), s.sL.at(), s.sG.at()]
            s.op("vector", lambda v, xt=xt, c=c_sum: v.reduce_sum(c, xt, axis=AX), waits=sn, inc=sV)
            s.op("scalar", lambda sc, xt=xt, o=trash, a=c_sumsq: sc.activation(o, xt, AF.Square, accum_out=a),
                 waits=sn, inc=sS)
            s.op("vector", lambda v, i=c_sum, o=c_mean: v.tensor_scalar(o, i, 1.0 / DM, None, ALU.mult),
                 waits=[sV.at()], inc=sV)
            s.op("vector", lambda v, i=c_mean, o=c_m2: v.tensor_mul(o, i, i), waits=[sV.at()], inc=sV)
            s.op("vector", lambda v, i=c_sumsq, o=c_var: v.tensor_scalar(o, i, 1.0 / DM, None, ALU.mult),
                 waits=[sS.at()], inc=sV)
            s.op("vector", lambda v, o=c_var, a=c_m2: v.tensor_sub(o, o, a), waits=[sV.at()], inc=sV)
            s.op("scalar", lambda sc, i=c_var, o=c_sd:
                 sc.activation(o, i, AF.Sqrt, bias=s.cst[:, s.C_EPSLN:s.C_EPSLN + 1]),
                 waits=[sV.at()], inc=sS)
            s.op("vector", lambda v, i=c_sd, o=c_rstd: v.reciprocal(o, i), waits=[sS.at()], inc=sV)
            s.op("vector", lambda v, xt=xt, m=c_mean, r=c_rstd, o=tmdst:
                 v.tensor_scalar(o, xt, m, r, ALU.subtract, ALU.mult), waits=[sV.at()], inc=sV)
            lnt_done = sV.at()
            for dc in range(NDC):
                bi = 6 + (dc % 2)
                pb, bw = s.bank(bi, extra=[lnt_done])
                s.op("tensor", lambda te, pb=pb, src=tmdst[:, dc * 128:(dc + 1) * 128]:
                     te.transpose(pb[:, 0:64].bitcast(BF16), src, s.idt_b[:]), waits=bw, inc=sT)
                s.op("scalar", lambda sc, pb=pb, dst=s.arb(s.LNF0 + dc * FM + fmcol, 128):
                     sc.copy(dst, pb[:, 0:64].bitcast(BF16)), waits=[sT.at()], inc=sS)
                s.bdone(bi, [sS.at()])
        self.fence()

    def attn_a(self):
        s = self
        sL, sG, sT, sV, sS, sC = s.sL, s.sG, s.sT, s.sV, s.sS, s.sC
        SC = s.SC0
        COS0, SIN0 = SC, SC + 2048
        KC0, KS0, VT0 = SC + 4096, SC + 5120, SC + 6144
        KT_F = SC + 7168
        KVS_F = SC + 9216
        POSR_F = SC + 13376
        self.lnt()

        SCALE = float(np.pi / 2 / SEQ)
        posr = s.arf(POSR_F, 2048)
        s.op("gpsimd", lambda g: g.iota(posr, pattern=[[1, 2048]], base=0, channel_multiplier=0,
                                        allow_small_or_imprecise_dtypes=True), inc=s.sP)
        s.op("vector", lambda v: v.tensor_scalar(posr, posr, s.cst[:, s.C_P0:s.C_P0 + 1], None, ALU.add),
             waits=[s.sP.at()], inc=sV)
        s.op("scalar", lambda sc: sc.activation(s.arb(SIN0, 2048), posr, AF.Sin, scale=SCALE),
             waits=[sV.at()], inc=sS)
        s.op("scalar", lambda sc: sc.activation(s.arb(COS0, 2048), posr, AF.Sin, scale=-SCALE,
                                                bias=s.cst[:, s.C_HPI:s.C_HPI + 1]),
             waits=[sV.at()], inc=sS)
        pc = s.csm[:, 8:24]
        cc = s.csm[:, 24:40]
        scc = s.csm[:, 40:56]
        s.op("gpsimd", lambda g: g.iota(pc, pattern=[[128, 16]], base=0, channel_multiplier=1,
                                        allow_small_or_imprecise_dtypes=True), inc=s.sP)
        s.op("vector", lambda v: v.tensor_scalar(pc, pc, s.cst[:, s.C_P0:s.C_P0 + 1], None, ALU.add),
             waits=[s.sP.at()], inc=sV)
        s.op("scalar", lambda sc: sc.activation(scc, pc, AF.Sin, scale=SCALE), waits=[sV.at()], inc=sS)
        s.op("scalar", lambda sc: sc.activation(cc, pc, AF.Sin, scale=-SCALE,
                                                bias=s.cst[:, s.C_HPI:s.C_HPI + 1]), waits=[sV.at()], inc=sS)
        trig_done = [sS.at()]
        bkv = s.arf(POSR_F, 2048)
        s.op("sync", lambda sy: sy.dma_start(out=bkv[:, 0:1024],
                                             in_=s.bqkv_a.ap()[0:1, DM:2 * DM].partition_broadcast(128)),
             waits=trig_done, inc=sL, inc_n=16)
        s.op("sync", lambda sy: sy.dma_start(out=bkv[:, 1024:2048],
                                             in_=s.bqkv_a.ap()[0:1, 2 * DM:3 * DM].partition_broadcast(128)),
             waits=trig_done, inc=sL, inc_n=16)
        bkv_ld = [sL.at()]

        for dc in range(NDC):
            s.op("gpsimd", lambda g, dc=dc: g.dma_start(out=s.arb(s.WB0 + dc * 2048, 1024),
                                                        in_=s.wqkv_a[dc * 128:(dc + 1) * 128, DM:2 * DM]),
                 inc=sG, inc_n=16)
            s.op("gpsimd", lambda g, dc=dc: g.dma_start(out=s.arb(s.WB0 + dc * 2048 + 1024, 1024),
                                                        in_=s.wqkv_a[dc * 128:(dc + 1) * 128, 2 * DM:3 * DM]),
                 inc=sG, inc_n=16)
        wkv_ld = [sG.at()]

        kvs = s.arf(KVS_F, 2080)
        s.op("vector", lambda v: v.memset(kvs, 0.0), inc=sV)
        for t in range(NT):
            mm_w = wkv_ld + trig_done
            for j, wcol in enumerate((0, 512, 1024, 1536)):
                pb, bw = s.bank(j, extra=mm_w)
                for dc in range(NDC):
                    ap = s.arb(s.LNF0 + dc * FM + 128 + t * 128, 128)
                    s.op("tensor", lambda te, pb=pb, ap=ap, dc=dc, wcol=wcol:
                         te.matmul(pb[:, 0:512], ap, s.arb(s.WB0 + dc * 2048 + wcol, 512),
                                   start=(dc == 0), stop=(dc == NDC - 1)),
                         waits=bw if dc == 0 else (), inc=sT if dc == NDC - 1 else None)
            mm_done = [sT.at()]
            kt = s.arf(KT_F, 1024)
            kc = s.arb(KC0, 1024)
            ks = s.arb(KS0, 1024)
            vt = s.arb(VT0, 1024)
            s.op("vector", lambda v, kt=kt: v.tensor_add(kt[:, 0:512], s.psum[0][:, 0:512], bkv[:, 0:512]),
                 waits=mm_done + bkv_ld, inc=sV)
            s.op("vector", lambda v, kt=kt: v.tensor_add(kt[:, 512:1024], s.psum[1][:, 0:512],
                                                         bkv[:, 512:1024]), inc=sV)
            s.op("scalar", lambda sc, kt=kt: sc.activation(kt, kt, AF.Relu), waits=[sV.at()], inc=sS)
            s.op("vector", lambda v, kt=kt, kc=kc, t=t: v.tensor_scalar(kc, kt, cc[:, t:t + 1], None, ALU.mult),
                 waits=[sS.at()], inc=sV)
            s.op("vector", lambda v, kt=kt, ks=ks, t=t: v.tensor_scalar(ks, kt, scc[:, t:t + 1], None, ALU.mult),
                 inc=sV)
            s.op("vector", lambda v, vt=vt: v.tensor_add(vt[:, 0:512], s.psum[2][:, 0:512],
                                                         bkv[:, 1024:1536]), waits=mm_done, inc=sV)
            s.op("vector", lambda v, vt=vt: v.tensor_add(vt[:, 512:1024], s.psum[3][:, 0:512],
                                                         bkv[:, 1536:2048]), inc=sV)
            kcv = [sV.at()]
            for j in range(4):
                s.bdone(j, kcv)
            for hh in range(NDC):
                bi = 4 + (hh % 2)
                pb, bw = s.bank(bi, extra=kcv)
                kcs_ = kc[:, hh * 128:(hh + 1) * 128]
                kss_ = ks[:, hh * 128:(hh + 1) * 128]
                vs_ = vt[:, hh * 128:(hh + 1) * 128]
                s.op("tensor", lambda te, pb=pb, a=kcs_, b=vs_:
                     te.matmul(pb[:, 0:128], a, b, start=True, stop=False), waits=bw)
                s.op("tensor", lambda te, pb=pb, a=kcs_:
                     te.matmul(pb[:, 128:129], a, s.ones_b[:], start=False, stop=False))
                s.op("tensor", lambda te, pb=pb, a=kss_, b=vs_:
                     te.matmul(pb[:, 132:260], a, b, start=False, stop=False))
                s.op("tensor", lambda te, pb=pb, a=kss_:
                     te.matmul(pb[:, 260:261], a, s.ones_b[:], start=False, stop=True), inc=sT)
                kvmm = [sT.at()]
                dstc = kvs[:, hh * 130:hh * 130 + 129]
                dsts = kvs[:, 1040 + hh * 130:1040 + hh * 130 + 129]
                if t == 0:
                    s.op("vector", lambda v, pb=pb, d=dstc: v.tensor_copy(d, pb[:, 0:129]), waits=kvmm, inc=sV)
                    s.op("vector", lambda v, pb=pb, d=dsts: v.tensor_copy(d, pb[:, 132:261]), inc=sV)
                else:
                    s.op("vector", lambda v, pb=pb, d=dstc: v.tensor_add(d, d, pb[:, 0:129]), waits=kvmm, inc=sV)
                    s.op("vector", lambda v, pb=pb, d=dsts: v.tensor_add(d, d, pb[:, 132:261]), inc=sV)
                s.bdone(bi, [sV.at()])

        s.op("sync", lambda sy: sy.dma_start(out=s.kv_ar_in[:, :], in_=kvs), waits=[sV.at()], inc=sL, inc_n=16)
        s.op("gpsimd", lambda g: g.collective_compute(
            "AllReduce", ALU.add, replica_groups=GROUPS_BATCH,
            ins=[s.kv_ar_in.ap().opt()], outs=[s.kv_ar_out.ap().opt()]), waits=[sL.at()], inc=sC)
        for dc in range(NDC):
            s.op("gpsimd", lambda g, dc=dc: g.dma_start(out=s.arb(s.WB0 + dc * 1024, 1024),
                                                        in_=s.wqkv_a[dc * 128:(dc + 1) * 128, 0:DM]),
                 waits=[sT.at()] if dc == 0 else (), inc=sG, inc_n=16)
            s.op("gpsimd", lambda g, dc=dc: g.dma_start(out=s.arb(s.WB0 + 8192 + dc * 1024, 1024),
                                                        in_=s.wout_a[dc * 128:(dc + 1) * 128, :]),
                 inc=sG, inc_n=16)
        w_ld = [sG.at()]
        s.op("gpsimd", lambda g: None, waits=[w_ld[0]])
        s.op("sync", lambda sy: sy.dma_start(out=kvs, in_=s.kv_ar_out[:, :]), waits=[sC.at()], inc=sL, inc_n=16)
        kvs_ld = [sL.at()]

        SBD0 = SC + 4096
        NBD0 = SC + 6144
        CTXN0 = SC + 6400
        QR_F = SC + 10496
        QC0 = SC + 12544
        QS0 = SC + 13568
        NSB_F = SC + 14592
        RNB0 = SC + 16640
        BCB0 = SC + 17664
        OSB_F = SC + 18688
        for hh in range(NDC):
            for which, srcoff in ((0, 0), (1, 1040)):
                d_ = s.arb(SBD0 + hh * 256 + which * 128, 128)
                s.op("vector", lambda v, d=d_: v.memset(d, 0.0), waits=kvs_ld if hh == 0 and which == 0 else (),
                     inc=sV)
                s.op("vector", lambda v, d=d_, so=srcoff, hh=hh:
                     v.tensor_copy(d[0:64, 0:64], kvs[0:64, so + hh * 130:so + hh * 130 + 64]),
                     waits=[sV.at()], inc=sV)
                s.op("vector", lambda v, d=d_, so=srcoff, hh=hh:
                     v.tensor_copy(d[64:128, 64:128], kvs[64:128, so + hh * 130 + 64:so + hh * 130 + 128]),
                     waits=[sV.at()], inc=sV)
                n_ = s.arb(NBD0 + hh * 4 + which * 2, 2)
                s.op("vector", lambda v, d=n_: v.memset(d, 0.0), inc=sV)
                s.op("vector", lambda v, d=n_, so=srcoff, hh=hh:
                     v.tensor_copy(d[0:64, 0:1], kvs[0:64, so + hh * 130 + 128:so + hh * 130 + 129]),
                     waits=[sV.at()], inc=sV)
                s.op("vector", lambda v, d=n_, so=srcoff, hh=hh:
                     v.tensor_copy(d[64:128, 1:2], kvs[64:128, so + hh * 130 + 128:so + hh * 130 + 129]),
                     waits=[sV.at()], inc=sV)
        sbd_done = [sV.at()]

        for tg in range(4):
            for hh in range(NDC):
                par = hh % 2
                pbq, bwq = s.bank(0 if par == 0 else 4, extra=w_ld + sbd_done)
                for dc in range(NDC):
                    s.op("tensor", lambda te, pbq=pbq, dc=dc, hh=hh, tg=tg:
                         te.matmul(pbq[:, 0:512], s.arb(s.WB0 + dc * 1024 + hh * 128, 128),
                                   s.arb(s.LNF0 + dc * FM + 128 + tg * 512, 512),
                                   start=(dc == 0), stop=(dc == NDC - 1)),
                         waits=bwq if dc == 0 else (), inc=sT if dc == NDC - 1 else None)
                qr = s.arf(QR_F + par * 1024, 512)
                s.op("scalar", lambda sc, pbq=pbq, qr=qr, hh=hh:
                     sc.activation(qr, pbq[:, 0:512], AF.Relu, bias=s.bq_a[:, hh:hh + 1]),
                     waits=[sT.at()], inc=sS)
                s.bdone(0 if par == 0 else 4, [sS.at()])
                qc = s.arb(QC0 + par * 512, 512)
                qs = s.arb(QS0 + par * 512, 512)
                s.op("vector", lambda v, qr=qr, qc=qc, tg=tg: v.tensor_mul(qc, qr, s.arb(COS0 + tg * 512, 512)),
                     waits=[sS.at()], inc=sV)
                s.op("vector", lambda v, qr=qr, qs=qs, tg=tg: v.tensor_mul(qs, qr, s.arb(SIN0 + tg * 512, 512)),
                     inc=sV)
                qcs = [sV.at()]
                pbc, bwc = s.bank(1 if par == 0 else 5, extra=qcs)
                s.op("tensor", lambda te, pbc=pbc, hh=hh, qc=qc:
                     te.matmul(pbc[:, 0:512], s.arb(SBD0 + hh * 256, 128), qc, start=True, stop=False),
                     waits=bwc)
                s.op("tensor", lambda te, pbc=pbc, hh=hh, qs=qs:
                     te.matmul(pbc[:, 0:512], s.arb(SBD0 + hh * 256 + 128, 128), qs, start=False, stop=True),
                     inc=sT)
                pbn, bwn = s.bank(2 if par == 0 else 6, extra=qcs)
                s.op("tensor", lambda te, pbn=pbn, hh=hh, qc=qc:
                     te.matmul(pbn[0:2, 0:512], s.arb(NBD0 + hh * 4, 2), qc, start=True, stop=False),
                     waits=bwn)
                s.op("tensor", lambda te, pbn=pbn, hh=hh, qs=qs:
                     te.matmul(pbn[0:2, 0:512], s.arb(NBD0 + hh * 4 + 2, 2), qs, start=False, stop=True),
                     inc=sT)
                ctxmm = [sT.at()]
                nsb = s.arf(NSB_F + par * 1024, 512)
                s.op("vector", lambda v, pbn=pbn, nsb=nsb:
                     v.tensor_scalar(nsb[0:2, :], pbn[0:2, 0:512], ATT_EPS, None, ALU.add),
                     waits=ctxmm, inc=sV)
                s.bdone(2 if par == 0 else 6, [sV.at()])
                s.op("vector", lambda v, nsb=nsb: v.reciprocal(nsb[0:2, :], nsb[0:2, :]), waits=[sV.at()], inc=sV)
                rnb = s.arb(RNB0 + par * 512, 512)
                s.op("vector", lambda v, nsb=nsb, rnb=rnb: v.tensor_copy(rnb[0:2, :], nsb[0:2, :]),
                     waits=[sV.at()], inc=sV)
                pbb, bwb = s.bank(3 if par == 0 else 7, extra=[sV.at()])
                s.op("tensor", lambda te, pbb=pbb, rnb=rnb:
                     te.matmul(pbb[:, 0:512], s.b2h[:, :], rnb[0:2, :], start=True, stop=True),
                     waits=bwb, inc=sT)
                bcb = s.arb(BCB0 + par * 512, 512)
                s.op("scalar", lambda sc, pbb=pbb, bcb=bcb: sc.copy(bcb, pbb[:, 0:512]), waits=[sT.at()], inc=sS)
                s.bdone(3 if par == 0 else 7, [sS.at()])
                s.op("vector", lambda v, pbc=pbc, hh=hh, bcb=bcb:
                     v.tensor_mul(s.arb(CTXN0 + hh * 512, 512), pbc[:, 0:512], bcb),
                     waits=[sS.at()], inc=sV)
                s.bdone(1 if par == 0 else 5, [sV.at()])
            ctxn_done = [sV.at()]
            banks = []
            for fj in range(NDC):
                pb, bw = s.bank(fj, extra=ctxn_done)
                banks.append(pb)
                for hh in range(NDC):
                    s.op("tensor", lambda te, pb=pb, fj=fj, hh=hh:
                         te.matmul(pb[:, 0:512], s.arb(s.WB0 + 8192 + hh * 1024 + fj * 128, 128),
                                   s.arb(CTXN0 + hh * 512, 512), start=(hh == 0), stop=(hh == NDC - 1)),
                         waits=bw if hh == 0 else (), inc=sT if hh == NDC - 1 else None)
            wout_mm = [sT.at()]
            for fj in range(NDC):
                osb = s.arf(QR_F, 512)
                s.op("scalar", lambda sc, fj=fj, osb=osb:
                     sc.activation(osb, banks[fj][:, 0:512], AF.Identity, bias=s.bo_a[:, fj:fj + 1]),
                     waits=wout_mm if fj == 0 else [sV.at()], inc=sS)
                ev = [sS.at()]
                for i in range(4):
                    t2 = tg * 4 + i
                    pb2, bw2 = s.bank(fj, extra=ev)
                    s.op("tensor", lambda te, pb2=pb2, osb=osb, i=i:
                         te.transpose(pb2[:, 0:128], osb[:, i * 128:(i + 1) * 128], s.idt_f[:]),
                         waits=bw2, inc=sT)
                    dst = s.xb[:, t2 * DM + fj * 128:t2 * DM + (fj + 1) * 128]
                    s.op("vector", lambda v, pb2=pb2, dst=dst: v.tensor_add(dst, dst, pb2[:, 0:128]),
                         waits=[sT.at()], inc=sV)
                    s.bdone(fj, [sV.at()])
        self.fence()


    def moe(self):
        s = self
        sL, sG, sT, sV, sS, sC = s.sL, s.sG, s.sT, s.sV, s.sS, s.sC
        SC = s.SC0
        self.lnt()
        masket = s.arf(SC, 2048)
        cumet = s.arf(SC + 4096, 2048)
        zros = s.arf(SC + 8192, 2048)
        GT0 = SC + 12288
        W1B = SC + 4096
        ECST = SC + 16384
        Z = SC + 17408
        tmp16 = s.arf(Z, 16)
        rcol = s.arf(Z + 32, 16)
        crow = s.arf(Z + 64, 16)
        segi = s.arf(Z + 96, 16)
        segr = s.arf(Z + 128, 16)
        expt = s.arf(Z + 160, 16)
        sume = s.arf(Z + 192, 2)
        mxc = s.arf(Z + 196, 2)
        lrt = s.arf(Z + 200, 16)
        prefb = s.arf(Z + 232, 16)
        c1 = s.arf(Z + 264, 1)
        c2 = s.arf(Z + 266, 1)
        c3 = s.arf(Z + 268, 1)
        c4 = s.arf(Z + 270, 1)
        c5 = s.arf(Z + 272, 1)
        mcum = s.arf(Z + 274, 16)
        mseg = s.arf(Z + 306, 16)
        cumb = s.arf(Z + 338, 8)
        segb = s.arf(Z + 354, 8)
        tmp8 = s.arf(Z + 370, 8)
        pcol16 = s.arf(Z + 386, 1)
        prow = s.arf(Z + 388, 16)
        DSP0 = 40960
        HS0 = 0
        W2G0 = 40960
        EOSF = 57344

        s.op("vector", lambda v: v.memset(zros, 0.0), inc=sV)
        for t in range(NT):
            pb, bw = s.bank(0)
            for dc in range(NDC):
                s.op("tensor", lambda te, pb=pb, dc=dc, t=t:
                     te.matmul(pb[:, 0:16], s.arb(s.LNF0 + dc * FM + 128 + t * 128, 128),
                               s.gwT[:, dc * NE:(dc + 1) * NE], start=(dc == 0), stop=(dc == NDC - 1)),
                     waits=bw if dc == 0 else (), inc=sT if dc == NDC - 1 else None)
            lg = [sT.at()]
            mx = mxc[:, 0:1]
            nmx = mxc[:, 1:2]
            s.op("vector", lambda v, pb=pb, mx=mx: v.reduce_max(mx, pb[:, 0:16], axis=AX), waits=lg, inc=sV)
            s.op("vector", lambda v, mx=mx, nmx=nmx: v.tensor_scalar(nmx, mx, -1.0, None, ALU.mult),
                 waits=[sV.at()], inc=sV)
            s.op("scalar", lambda sc, pb=pb, nmx=nmx: sc.activation(expt, pb[:, 0:16], AF.Exp, bias=nmx,
                                                                   accum_out=sume[:, 0:1]),
                 waits=[sV.at()], inc=sS)
            s.op("vector", lambda v, t=t: v.reciprocal(s.gcol[:, t:t + 1], sume[:, 0:1]), waits=[sS.at()], inc=sV)
            s.op("vector", lambda v, pb=pb, mx=mx, t=t:
                 v.tensor_scalar(s.masks[:, t * NE:(t + 1) * NE], pb[:, 0:16], mx, None, ALU.is_equal),
                 waits=[sV.at()], inc=sV)
            s.bdone(0, [sV.at(), sS.at()])
        pb1, bw1 = s.bank(1)
        for t in range(NT):
            s.op("tensor", lambda te, pb1=pb1, t=t:
                 te.matmul(pb1[0:1, 0:16], s.ones_f[:], s.masks[:, t * NE:(t + 1) * NE],
                           start=(t == 0), stop=(t == NT - 1)),
                 waits=(bw1 + [sV.at()]) if t == 0 else (), inc=sT if t == NT - 1 else None)
        s.op("vector", lambda v: v.tensor_copy(crow[0:1, :], pb1[0:1, 0:16]), waits=[sT.at()], inc=sV)
        s.bdone(1, [sV.at()])
        s.op("vector", lambda v: v.tensor_tensor_scan(segi[0:1, :], crow[0:1, :], zros[0:1, 0:16], 0.0,
                                                      ALU.add, ALU.add), waits=[sV.at()], inc=sV)
        s.op("vector", lambda v: v.tensor_sub(segr[0:1, :], segi[0:1, :], crow[0:1, :]), waits=[sV.at()], inc=sV)
        s.op("sync", lambda sy: sy.dma_start(out=s.cnt_in[0:1, :], in_=crow[0:1, :]), waits=[sV.at()],
             inc=sL, inc_n=16)
        s.op("sync", lambda sy: sy.dma_start(out=s.cnt_in[1:2, :], in_=segr[0:1, :]), waits=[sV.at()],
             inc=sL, inc_n=16)
        s.op("gpsimd", lambda g: g.collective_compute(
            "AllGather", ALU.bypass, replica_groups=GROUPS_ALL,
            ins=[s.cnt_in.ap().opt()], outs=[s.cnt_out.ap().opt()]), waits=[sL.at()], inc=sC)
        cnt_ag = [sC.at()]

        for t in range(NT):
            bi = 6 + t % 2
            pb, bw = s.bank(bi, extra=[sV.at()])
            s.op("tensor", lambda te, pb=pb, t=t:
                 te.transpose(pb[0:16, 0:128], s.masks[:, t * NE:(t + 1) * NE], s.idt_f[:]), waits=bw, inc=sT)
            s.op("vector", lambda v, pb=pb, t=t: v.tensor_copy(masket[0:16, t * 128:(t + 1) * 128],
                                                               pb[0:16, 0:128]), waits=[sT.at()], inc=sV)
            s.bdone(bi, [sV.at()])
        s.op("vector", lambda v: v.tensor_tensor_scan(cumet[0:16, :], masket[0:16, :], zros[0:16, :], 0.0,
                                                      ALU.add, ALU.add), waits=[sV.at()], inc=sV)
        cum_done = [sV.at()]
        pb2, bw2 = s.bank(2, extra=[sV.at()])
        s.op("tensor", lambda te, pb2=pb2: te.matmul(pb2[:, 0:16], s.ones1r[:], segr[0:1, :],
                                                     start=True, stop=True), waits=bw2, inc=sT)
        segb_mm = [sT.at()]
        for t in range(NT):
            bi = 6 + t % 2
            pb, bw = s.bank(bi, extra=cum_done)
            s.op("tensor", lambda te, pb=pb, t=t:
                 te.transpose(pb[:, 0:16], cumet[0:16, t * 128:(t + 1) * 128], s.idt_f[0:16, 0:16]),
                 waits=bw, inc=sT)
            trm = [sT.at()]
            s.op("vector", lambda v, pb=pb, t=t: v.tensor_mul(tmp16, s.masks[:, t * NE:(t + 1) * NE],
                                                              pb[:, 0:16]), waits=trm, inc=sV)
            s.bdone(bi, [sV.at()])
            s.op("vector", lambda v, t=t: v.reduce_sum(lrt[:, t:t + 1], tmp16, axis=AX), waits=[sV.at()], inc=sV)
            s.op("vector", lambda v, t=t: v.tensor_scalar(lrt[:, t:t + 1], lrt[:, t:t + 1], -1.0, None, ALU.add),
                 waits=[sV.at()], inc=sV)
            s.op("vector", lambda v, t=t: v.tensor_mul(tmp16, s.masks[:, t * NE:(t + 1) * NE], pb2[:, 0:16]),
                 waits=segb_mm + [sV.at()], inc=sV)
            s.op("vector", lambda v, t=t: v.reduce_sum(c1, tmp16, axis=AX), waits=[sV.at()], inc=sV)
            s.op("vector", lambda v, t=t: v.tensor_add(c2, c1, lrt[:, t:t + 1]), waits=[sV.at()], inc=sV)
            s.op("vector", lambda v, t=t: v.tensor_copy(s.idx_sc[:, t:t + 1], c2), waits=[sV.at()], inc=sV)
            s.op("gpsimd", lambda g, t=t: g.indirect_dma_start(
                out=s.cmp_in[:, :], out_offset=bass.IndirectOffsetOnAxis(ap=s.idx_sc[:, t:t + 1], axis=0),
                in_=s.arb(s.LNT0 + t * DM, DM), in_offset=None),
                waits=[sV.at()], inc=sG, inc_n=16)
        s.bdone(2, [sV.at()])
        s.op("gpsimd", lambda g: g.collective_compute(
            "AllGather", ALU.bypass, replica_groups=GROUPS_ALL,
            ins=[s.cmp_in.ap().opt()], outs=[s.cmp_out.ap().opt()]), waits=[sG.at()], inc=sC)
        tok_ag = [sC.at()]

        s.op("sync", lambda sy: sy.dma_start(out=s.cntT[:, :],
                                             in_=s.cnt_out.ap()[0:16:2, :].rearrange("r e -> e r"),
                                             allow_slow_non_contiguous=True),
             waits=cnt_ag, inc=sL, inc_n=16)
        s.op("sync", lambda sy: sy.dma_start(out=s.segT[:, :],
                                             in_=s.cnt_out.ap()[1:16:2, :].rearrange("r e -> e r"),
                                             allow_slow_non_contiguous=True),
             waits=cnt_ag, inc=sL, inc_n=16)
        ld2 = [sL.at()]
        s.op("vector", lambda v: v.tensor_tensor_scan(s.cumT[:, :], s.cntT[:, :], s.zrow16[:, :], 0.0,
                                                      ALU.add, ALU.add), waits=ld2, inc=sV)
        s.op("vector", lambda v: v.tensor_mul(tmp8[0:16, :], s.cntT[:, :], s.cmask8[:, :]), waits=ld2, inc=sV)
        s.op("vector", lambda v: v.reduce_sum(pcol16[0:16, :], tmp8[0:16, 0:8], axis=AX), waits=[sV.at()], inc=sV)
        pcol_done = [sV.at()]
        bi = 6
        pb, bw = s.bank(bi, extra=pcol_done)
        s.op("tensor", lambda te, pb=pb: te.transpose(pb[0:1, 0:16], pcol16[0:16, 0:1], s.idt_f[0:16, 0:16]),
             waits=bw, inc=sT)
        s.op("vector", lambda v, pb=pb: v.tensor_copy(prow[0:1, :], pb[0:1, 0:16]), waits=[sT.at()], inc=sV)
        s.bdone(bi, [sV.at()])
        pb3, bw3 = s.bank(3, extra=[sV.at()])
        s.op("tensor", lambda te, pb3=pb3: te.matmul(pb3[:, 0:16], s.ones1r[:], prow[0:1, :],
                                                     start=True, stop=True), waits=bw3, inc=sT)
        s.op("vector", lambda v, pb3=pb3: v.tensor_copy(prefb, pb3[:, 0:16]), waits=[sT.at()], inc=sV)
        s.bdone(3, [sV.at()])
        for t in range(NT):
            s.op("vector", lambda v, t=t: v.tensor_mul(tmp16, s.masks[:, t * NE:(t + 1) * NE], prefb),
                 waits=[sV.at()], inc=sV)
            s.op("vector", lambda v: v.reduce_sum(c1, tmp16, axis=AX), waits=[sV.at()], inc=sV)
            s.op("vector", lambda v, t=t: v.tensor_add(c2, c1, lrt[:, t:t + 1]), waits=[sV.at()], inc=sV)
            s.op("vector", lambda v: v.tensor_scalar(c3, c2, float(CAP), None, ALU.is_lt), waits=[sV.at()],
                 inc=sV)
            s.op("vector", lambda v, t=t: v.tensor_mul(s.gk[:, t:t + 1], s.gcol[:, t:t + 1], c3),
                 waits=[sV.at()], inc=sV)
            s.op("vector", lambda v, t=t: v.tensor_mul(tmp16, s.masks[:, t * NE:(t + 1) * NE],
                                                       s.cst[:, s.C_BASEE:s.C_BASEE + 16]), inc=sV)
            s.op("vector", lambda v: v.reduce_sum(c4, tmp16, axis=AX), waits=[sV.at()], inc=sV)
            s.op("vector", lambda v, t=t: v.tensor_mul(tmp16, s.masks[:, t * NE:(t + 1) * NE],
                                                       s.cst[:, s.C_TRASHE:s.C_TRASHE + 16]), waits=[sV.at()],
                 inc=sV)
            s.op("vector", lambda v: v.reduce_sum(c5, tmp16, axis=AX), waits=[sV.at()], inc=sV)
            s.op("vector", lambda v: v.tensor_add(c4, c4, c2), waits=[sV.at()], inc=sV)
            s.op("vector", lambda v: v.tensor_sub(c4, c4, c5), waits=[sV.at()], inc=sV)
            s.op("vector", lambda v: v.scalar_tensor_tensor(c4, c4, c3, c5, ALU.mult, ALU.add),
                 waits=[sV.at()], inc=sV)
            s.op("vector", lambda v, t=t: v.tensor_copy(s.idx_eo[:, t:t + 1], c4), waits=[sV.at()], inc=sV)

        s.op("gpsimd", lambda g: g.iota(rcol[:, 0:10], pattern=[[128, 10]], base=0, channel_multiplier=1,
                                        allow_small_or_imprecise_dtypes=True), inc=s.sP)
        rdy = [s.sP.at(), sV.at()]
        pb4, bw4 = s.bank(4, extra=rdy)
        for el in range(2):
            s.op("tensor", lambda te, pb4=pb4, el=el:
                 te.matmul(pb4[0:1, el * 8:el * 8 + 8], s.myexp_s[:, el:el + 1], s.cumT[:, :],
                           start=(el == 0), stop=False), waits=bw4 if el == 0 else (), inc=None)
            s.op("tensor", lambda te, pb4=pb4, el=el:
                 te.matmul(pb4[0:1, 16 + el * 8:16 + el * 8 + 8], s.myexp_s[:, el:el + 1], s.segT[:, :],
                           start=False, stop=(el == 1)), inc=sT)
        s.op("vector", lambda v, pb4=pb4: v.tensor_copy(mcum[0:1, 0:8], pb4[0:1, 0:8]), waits=[sT.at()], inc=sV)
        s.op("vector", lambda v, pb4=pb4: v.tensor_copy(mcum[0:1, 8:16], pb4[0:1, 8:16]), inc=sV)
        s.op("vector", lambda v, pb4=pb4: v.tensor_copy(mseg[0:1, 0:8], pb4[0:1, 16:24]), inc=sV)
        s.op("vector", lambda v, pb4=pb4: v.tensor_copy(mseg[0:1, 8:16], pb4[0:1, 24:32]), inc=sV)
        s.bdone(4, [sV.at()])
        for el in range(2):
            pb5, bw5 = s.bank(5, extra=[sV.at()])
            s.op("tensor", lambda te, pb5=pb5, el=el:
                 te.matmul(pb5[:, 0:8], s.ones1r[:], mcum[0:1, el * 8:el * 8 + 8],
                           start=True, stop=False), waits=bw5, inc=sT)
            s.op("tensor", lambda te, pb5=pb5, el=el:
                 te.matmul(pb5[:, 8:16], s.ones1r[:], mseg[0:1, el * 8:el * 8 + 8],
                           start=False, stop=True), inc=sT)
            s.op("vector", lambda v, pb5=pb5: v.tensor_copy(cumb, pb5[:, 0:8]), waits=[sT.at()], inc=sV)
            s.op("vector", lambda v, pb5=pb5: v.tensor_copy(segb, pb5[:, 8:16]), waits=[sT.at()], inc=sV)
            s.bdone(5, [sV.at()])
            for st in range(10):
                r_ = rcol[:, st:st + 1]
                s.op("vector", lambda v, r_=r_: v.tensor_scalar(tmp8, cumb, r_, None, ALU.is_le),
                     waits=[sV.at()], inc=sV)
                s.op("vector", lambda v: v.reduce_sum(c1, tmp8, axis=AX), waits=[sV.at()], inc=sV)
                s.op("vector", lambda v: v.tensor_mul(tmp8, tmp8, cumb), waits=[sV.at()], inc=sV)
                s.op("vector", lambda v: v.reduce_max(c2, tmp8, axis=AX), waits=[sV.at()], inc=sV)
                s.op("vector", lambda v: v.tensor_scalar(tmp8, s.cst[:, s.C_IOTA8:s.C_IOTA8 + 8], c1, None,
                                                         ALU.is_equal), waits=[sV.at()], inc=sV)
                s.op("vector", lambda v: v.tensor_mul(tmp8, tmp8, segb), waits=[sV.at()], inc=sV)
                s.op("vector", lambda v: v.reduce_sum(c3, tmp8, axis=AX), waits=[sV.at()], inc=sV)
                s.op("vector", lambda v: v.tensor_scalar(c4, c1, float(T), None, ALU.mult), waits=[sV.at()], inc=sV)
                s.op("vector", lambda v: v.tensor_add(c4, c4, c3), waits=[sV.at()], inc=sV)
                s.op("vector", lambda v, r_=r_: v.tensor_add(c4, c4, r_), waits=[sV.at()], inc=sV)
                s.op("vector", lambda v: v.tensor_sub(c4, c4, c2), waits=[sV.at()], inc=sV)
                s.op("vector", lambda v: v.tensor_scalar_min(c4, c4, float(NCORE * T - 1)), waits=[sV.at()], inc=sV)
                s.op("vector", lambda v, el=el, st=st: v.tensor_copy(s.dgi[:, el * 10 + st:el * 10 + st + 1], c4),
                     waits=[sV.at()], inc=sV)
        dgi_done = [sV.at()]
        self.fence()

        for el in range(2):
            for st in range(10):
                gt = s.arb(GT0 + (st % 2) * 1024, 1024)
                s.op("gpsimd", lambda g, el=el, st=st, gt=gt: g.indirect_dma_start(
                    out=gt, out_offset=None, in_=s.cmp_out[:, :],
                    in_offset=bass.IndirectOffsetOnAxis(ap=s.dgi[:, el * 10 + st:el * 10 + st + 1], axis=0)),
                    waits=[sS.at()], inc=sG, inc_n=16)
                gw_ = [sG.at()]
                for dc in range(NDC):
                    bi = 6 + dc % 2
                    pb, bw = s.bank(bi, extra=gw_)
                    s.op("tensor", lambda te, pb=pb, gt=gt, dc=dc:
                         te.transpose(pb[:, 0:64].bitcast(BF16), gt[:, dc * 128:(dc + 1) * 128], s.idt_b[:]),
                         waits=bw, inc=sT)
                    s.op("scalar", lambda sc, pb=pb, dc=dc, st=st:
                         sc.copy(s.arb(DSP0 + dc * CAP + st * 128, 128), pb[:, 0:64].bitcast(BF16)),
                         waits=[sT.at()], inc=sS)
                    s.bdone(bi, [sS.at()])
            disp_done = [sS.at()]
            if self.dbg and self.moe_calls == 0 and el == 0:
                s.op("sync", lambda sy: sy.dma_start(out=s.dbg_disp[:, :], in_=s.arb(DSP0, NDC * CAP)),
                     waits=disp_done, inc=sL, inc_n=16)
                s.op("gpsimd", lambda g: None, waits=[sL.at()])
            burst_end = {}
            mmdone = {}

            def w1_burst(b):
                for fc in range(4 * b, 4 * b + 4):
                    prev = [mmdone[b - 2]] if (b >= 2 and fc == 4 * b) else []
                    s.op("gpsimd", lambda g, el=el, fc=fc:
                         g.dma_start(out=s.arb(W1B + (fc % 8) * 1024, 1024), in_=s.w1_m[el, fc, :, :]),
                         waits=(disp_done if b == 0 and fc == 0 else []) + prev, inc=sG, inc_n=16)
                burst_end[b] = sG.at()
                s.op("gpsimd", lambda g: None, waits=[burst_end[b]])
            w1_burst(0)
            w1_burst(1)
            for b in range(8):
                for fc in range(4 * b, 4 * b + 4):
                    wb = s.arb(W1B + (fc % 8) * 1024, 1024)
                    for sl, (c0, w) in enumerate(((0, 512), (512, 512), (1024, 256))):
                        bi = sl % 2
                        pb, bw = s.bank(bi, extra=[burst_end[b]] + (disp_done if fc == 0 else []))
                        for dc in range(NDC):
                            s.op("tensor", lambda te, pb=pb, wb=wb, dc=dc, c0=c0, w=w:
                                 te.matmul(pb[:, 0:w], wb[:, dc * 128:(dc + 1) * 128],
                                           s.arb(DSP0 + dc * CAP + c0, w), start=(dc == 0), stop=(dc == NDC - 1)),
                                 waits=bw if dc == 0 else (), inc=sT if dc == NDC - 1 else None)
                        s.op("scalar", lambda sc, pb=pb, fc=fc, c0=c0, w=w:
                             sc.activation(s.arb(HS0 + fc * CAP + c0, w), pb[:, 0:w], AF.Gelu_apprx_tanh),
                             waits=[sT.at()], inc=sS)
                        s.bdone(bi, [sS.at()])
                mmdone[b] = sT.at()
                if b + 2 <= 7:
                    w1_burst(b + 2)
            h_done = [sS.at()]
            if self.dbg and self.moe_calls == 0 and el == 0:
                s.op("sync", lambda sy: sy.dma_start(out=s.dbg_h[:, :], in_=s.arb(HS0, NFC * CAP)),
                     waits=h_done, inc=sL, inc_n=16)
                s.op("tensor", lambda te: None, waits=[sL.at()])
                s.op("gpsimd", lambda g: None, waits=[sL.at()])
            eosb = s.arf(EOSF, 5120)
            w2done = {}
            w2l = {}

            def w2_load(k):
                buf = W2G0 + (k % 2) * 8192
                fg = k % 4
                for j in range(8):
                    prevw = [w2done[k - 2]] if (k >= 2 and j == 0) else []
                    s.op("gpsimd", lambda g, el=el, fg=fg, j=j, buf=buf:
                         g.dma_start(out=s.arb(buf + j * 1024, 1024),
                                     in_=s.w2_m[el, (fg * 8 + j) * 128:(fg * 8 + j + 1) * 128, :]),
                         waits=(h_done if k == 0 and j == 0 else []) + prevw, inc=sG, inc_n=16)
                w2l[k] = sG.at()
                s.op("gpsimd", lambda g: None, waits=[w2l[k]])
            w2_load(0)
            w2_load(1)
            for half in range(2):
                for fg in range(4):
                    k = half * 4 + fg
                    buf = W2G0 + (k % 2) * 8192
                    for si in range(5):
                        ss = half * 5 + si
                        for dh in range(2):
                            bi = (si * 2 + dh) % 2
                            pb, bw = s.bank(bi, extra=[w2l[k]] + (h_done if k == 0 else []))
                            for j in range(8):
                                s.op("tensor", lambda te, pb=pb, fg=fg, j=j, ss=ss, dh=dh, buf=buf:
                                     te.matmul(pb[:, 0:512], s.arb(HS0 + (fg * 8 + j) * CAP + ss * 128, 128),
                                               s.arb(buf + j * 1024 + dh * 512, 512),
                                               start=(j == 0), stop=(j == 7)),
                                     waits=bw if j == 0 else (), inc=sT if j == 7 else None)
                            dst = eosb[:, si * 1024 + dh * 512:si * 1024 + dh * 512 + 512]
                            if fg == 0:
                                s.op("vector", lambda v, pb=pb, dst=dst: v.tensor_copy(dst, pb[:, 0:512]),
                                     waits=[sT.at()], inc=sV)
                            else:
                                s.op("vector", lambda v, pb=pb, dst=dst: v.tensor_add(dst, dst, pb[:, 0:512]),
                                     waits=[sT.at()], inc=sV)
                            s.bdone(bi, [sV.at()])
                    w2done[k] = sT.at()
                    if k + 2 <= 7:
                        w2_load(k + 2)
                for si in range(5):
                    ss = half * 5 + si
                    ec = s.arb(ECST + (si % 2) * 1024, 1024)
                    s.op("vector", lambda v, si=si, ec=ec: v.tensor_copy(ec, eosb[:, si * 1024:(si + 1) * 1024]),
                         waits=[sV.at(), sL.at()], inc=sV)
                    s.op("sync", lambda sy, el=el, ss=ss, ec=ec:
                         sy.dma_start(out=s.eo_in[el * CAP + ss * 128:el * CAP + (ss + 1) * 128, :], in_=ec),
                         waits=[sV.at()], inc=sL, inc_n=16)
            self.fence()
        zb = s.arb(GT0, 1024)
        s.op("vector", lambda v, zb=zb: v.memset(zb, 0.0), inc=sV)
        s.op("sync", lambda sy, zb=zb: sy.dma_start(out=s.eo_in[2 * CAP:EO_STRIDE, :], in_=zb[0:64, :]),
             waits=[sV.at()], inc=sL, inc_n=16)
        s.op("gpsimd", lambda g: g.collective_compute(
            "AllGather", ALU.bypass, replica_groups=GROUPS_ALL,
            ins=[s.eo_in.ap().opt()], outs=[s.eo_out.ap().opt()]), waits=[sL.at()], inc=sC)
        eo_ag = [sC.at()]
        for t in range(NT):
            gt = s.arb(GT0 + (t % 2) * 1024, 1024)
            s.op("gpsimd", lambda g, t=t, gt=gt: g.indirect_dma_start(
                out=gt, out_offset=None, in_=s.eo_out[:, :],
                in_offset=bass.IndirectOffsetOnAxis(ap=s.idx_eo[:, t:t + 1], axis=0)),
                waits=eo_ag + [sV.at()], inc=sG, inc_n=16)
            xt = s.xb[:, t * DM:(t + 1) * DM]
            s.op("vector", lambda v, gt=gt, xt=xt, t=t:
                 v.scalar_tensor_tensor(xt, gt, s.gk[:, t:t + 1], xt, ALU.mult, ALU.add),
                 waits=[sG.at()], inc=sV)
        self.fence()
        self.moe_calls += 1
        if self.dbg and self.moe_calls == 1:
            def dmp(dst, srcap):
                s.op("sync", lambda sy, dst=dst, srcap=srcap: sy.dma_start(out=dst, in_=srcap),
                     inc=sL, inc_n=16)
            dmp(s.dbg_dgi[:, :], s.dgi[:, :])
            dmp(s.dbg_ieo[:, :], s.idx_eo[:, :])
            dmp(s.dbg_isc[:, :], s.idx_sc[:, :])
            dmp(s.dbg_gk[:, :], s.gk[:, :])
            dmp(s.dbg_cnt[:, :], s.cntT[:, :])
            dmp(s.dbg_seg[:, :], s.segT[:, :])
            dmp(s.dbg_cmp[:, :], s.cmp_in.ap())
            dmp(s.dbg_cmpo[:, :], s.cmp_out.ap())
            dmp(s.dbg_eoin[:, :], s.eo_in.ap())
            self.fence()


    def attn_b(self):
        s = self
        sL, sG, sT, sV, sS, sC = s.sL, s.sG, s.sT, s.sV, s.sS, s.sC
        SC = s.SC0
        QF, KF, VF, VTM = SC, SC + 2304, SC + 4608, SC + 6912
        MB0 = SC + 9216
        SCB_F = SC + 15360
        EXPT_F = SC + 16128
        ATT0 = SC + 16896
        ATTT0 = SC + 17280
        IOT_F = SC + 17664
        KP_F = SC + 18432
        HXL = SC + 4096
        HXR = SC + 8192

        s.op("sync", lambda sy: sy.dma_start(out=s.halo_in[0:128, :], in_=s.xb[:, 0:DM]), inc=sL, inc_n=16)
        s.op("sync", lambda sy: sy.dma_start(out=s.halo_in[128:256, :], in_=s.xb[:, (NT - 1) * DM:NT * DM]),
             inc=sL, inc_n=16)
        s.op("gpsimd", lambda g: g.collective_compute(
            "AllGather", ALU.bypass, replica_groups=GROUPS_ALL,
            ins=[s.halo_in.ap().opt()], outs=[s.halo_out.ap().opt()]), waits=[sL.at()], inc=sC)
        halo_ag = [sC.at()]
        s.op("gpsimd", lambda g: g.indirect_dma_start(
            out=s.arf(HXL, DM), out_offset=None, in_=s.halo_out[:, :],
            in_offset=bass.IndirectOffsetOnAxis(ap=s.hl_idx[:, 0:1], axis=0)), waits=halo_ag, inc=sG, inc_n=16)
        s.op("gpsimd", lambda g: g.indirect_dma_start(
            out=s.arf(HXR, DM), out_offset=None, in_=s.halo_out[:, :],
            in_offset=bass.IndirectOffsetOnAxis(ap=s.hr_idx[:, 0:1], axis=0)), waits=halo_ag, inc=sG, inc_n=16)
        self.lnt(halo=True)

        iot = s.arf(IOT_F, 384)
        kp = s.arf(KP_F, 384)
        ma = s.arf(SCB_F, 384)
        mb_ = s.arf(EXPT_F, 384)
        s.op("gpsimd", lambda g: g.iota(iot, pattern=[[1, 384]], base=-128, channel_multiplier=-1,
                                        allow_small_or_imprecise_dtypes=True), inc=s.sP)
        iota_done = [s.sP.at()]
        for t in range(NT):
            s.op("gpsimd", lambda g, t=t: g.iota(kp, pattern=[[1, 384]], base=t * 128 - 128, channel_multiplier=0,
                                                 allow_small_or_imprecise_dtypes=True),
                 waits=[sV.at()], inc=s.sP)
            s.op("vector", lambda v: v.tensor_scalar(kp, kp, s.cst[:, s.C_P0:s.C_P0 + 1], None, ALU.add),
                 waits=[s.sP.at()], inc=sV)
            s.op("vector", lambda v: v.tensor_scalar(ma, iot, -128.0, None, ALU.is_ge),
                 waits=iota_done + [sV.at()], inc=sV)
            s.op("vector", lambda v: v.tensor_scalar(mb_, iot, 128.0, None, ALU.is_le), inc=sV)
            s.op("vector", lambda v: v.tensor_mul(ma, ma, mb_), waits=[sV.at()], inc=sV)
            s.op("vector", lambda v: v.tensor_scalar(mb_, kp, 0.0, None, ALU.is_ge), waits=[sV.at()], inc=sV)
            s.op("vector", lambda v: v.tensor_mul(ma, ma, mb_), waits=[sV.at()], inc=sV)
            s.op("vector", lambda v: v.tensor_scalar(mb_, kp, float(SEQ), None, ALU.is_lt), waits=[sV.at()], inc=sV)
            s.op("vector", lambda v: v.tensor_mul(ma, ma, mb_), waits=[sV.at()], inc=sV)
            s.op("vector", lambda v, t=t: v.tensor_scalar(s.arb(MB0 + t * 384, 384), ma, 1.0e30, -1.0e30,
                                                          ALU.mult, ALU.add), waits=[sV.at()], inc=sV)
        mb_done = [sV.at()]

        for hh in range(NDC):
            s.op("gpsimd", lambda g, hh=hh: g.dma_start(out=s.arb(s.WB0 + hh * 1024, 1024),
                                                        in_=s.wproj_b[hh * 128:(hh + 1) * 128, :]),
                 inc=sG, inc_n=16)
        wp_ld = [sG.at()]

        TT5 = ((0, 512), (512, 512), (1024, 512), (1536, 512), (2048, 256))
        for hh in range(NDC):
            s.op("gpsimd", lambda g, hh=hh: g.dma_start(out=s.arb(SC + 19200, 0) if False else
                                                        s.arb(s.WB0 + 8192, 3072), in_=s.wqkv_b[hh, :, :]),
                 waits=[sT.at()], inc=sG, inc_n=16)
            whh_ld = [sG.at()]
            for part, dst0 in ((0, QF), (1, KF), (2, VF)):
                for it, (c0, w) in enumerate(TT5):
                    bi = it % 2
                    pb, bw = s.bank(bi, extra=whh_ld + ([sV.at(), sS.at()] if part == 0 and it == 0 else []))
                    for dc in range(NDC):
                        s.op("tensor", lambda te, pb=pb, part=part, dc=dc, c0=c0, w=w:
                             te.matmul(pb[:, 0:w], s.arb(s.WB0 + 8192 + (part * 8 + dc) * 128, 128),
                                       s.arb(s.LNF0 + dc * FM + c0, w), start=(dc == 0), stop=(dc == NDC - 1)),
                             waits=bw if dc == 0 else (), inc=sT if dc == NDC - 1 else None)
                    s.op("scalar", lambda sc, pb=pb, dst0=dst0, c0=c0, w=w:
                         sc.copy(s.arb(dst0 + c0, w), pb[:, 0:w]), waits=[sT.at()], inc=sS)
                    s.bdone(bi, [sS.at()])
            qkv_done = [sS.at()]
            for j in range(18):
                bi = 6 + j % 2
                pb, bw = s.bank(bi, extra=qkv_done)
                s.op("tensor", lambda te, pb=pb, j=j:
                     te.transpose(pb[:, 0:64].bitcast(BF16), s.arb(VF + j * 128, 128), s.idt_b[:]),
                     waits=bw, inc=sT)
                s.op("scalar", lambda sc, pb=pb, j=j: sc.copy(s.arb(VTM + j * 128, 128),
                                                              pb[:, 0:64].bitcast(BF16)),
                     waits=[sT.at()], inc=sS)
                s.bdone(bi, [sS.at()])
            vtm_done = [sS.at()]
            SCB1, EXPT1, ATT1, ATTT1 = VF, VF + 768, VF + 1536, VF + 1920
            for t in range(NT):
                for head in range(2):
                    q2 = (t * 2 + head) % 2
                    po = head * 64
                    qcol = 128 + t * 128
                    pb3, bw3 = s.bank(3 if q2 == 0 else 5, extra=vtm_done + mb_done)
                    s.op("tensor", lambda te, pb3=pb3, po=po, qcol=qcol, t=t:
                         te.matmul(pb3[:, 0:384], s.arb(QF + qcol, 128)[po:po + 64, :],
                                   s.arb(KF + t * 128, 384)[po:po + 64, :], start=True, stop=True),
                         waits=bw3, inc=sT)
                    scmm = [sT.at()]
                    scb = s.arf(SCB_F if q2 == 0 else SCB1, 384)
                    s.op("vector", lambda v, pb3=pb3, t=t, scb=scb: v.tensor_add(scb, pb3[:, 0:384],
                                                                                s.arb(MB0 + t * 384, 384)),
                         waits=scmm, inc=sV)
                    s.bdone(3 if q2 == 0 else 5, [sV.at()])
                    cb = 60 if q2 == 0 else 56
                    mx = s.csm[:, cb:cb + 1]
                    nmx = s.csm[:, cb + 1:cb + 2]
                    sume = s.csm[:, cb + 2:cb + 3]
                    rc = s.csm[:, cb + 3:cb + 4]
                    s.op("vector", lambda v, mx=mx, scb=scb: v.reduce_max(mx, scb, axis=AX),
                         waits=[sV.at()], inc=sV)
                    s.op("vector", lambda v, mx=mx, nmx=nmx: v.tensor_scalar(nmx, mx, -0.125, None, ALU.mult),
                         waits=[sV.at()], inc=sV)
                    expt = s.arf(EXPT_F if q2 == 0 else EXPT1, 384)
                    s.op("scalar", lambda sc, nmx=nmx, expt=expt, scb=scb, sume=sume:
                         sc.activation(expt, scb, AF.Exp, bias=nmx, scale=0.125, accum_out=sume),
                         waits=[sV.at()], inc=sS)
                    s.op("vector", lambda v, sume=sume, rc=rc: v.reciprocal(rc, sume), waits=[sS.at()],
                         inc=sV)
                    att = s.arb(ATT0 if q2 == 0 else ATT1, 384)
                    s.op("vector", lambda v, att=att, expt=expt, rc=rc:
                         v.tensor_scalar(att, expt, rc, None, ALU.mult), waits=[sV.at()], inc=sV)
                    att_done = [sV.at()]
                    at0 = ATTT0 if q2 == 0 else ATTT1
                    for j3 in range(3):
                        bi = (6 if q2 == 0 else 7)
                        pb, bw = s.bank(bi, extra=att_done)
                        s.op("tensor", lambda te, pb=pb, att=att, j3=j3:
                             te.transpose(pb[:, 0:64].bitcast(BF16), att[:, j3 * 128:(j3 + 1) * 128], s.idt_b[:]),
                             waits=bw, inc=sT)
                        s.op("scalar", lambda sc, pb=pb, j3=j3, at0=at0: sc.copy(s.arb(at0 + j3 * 128, 128),
                                                                                pb[:, 0:64].bitcast(BF16)),
                             waits=[sT.at()], inc=sS)
                        s.bdone(bi, [sS.at()])
                    attT_done = [sS.at()]
                    pb4, bw4 = s.bank(4, extra=attT_done)
                    for j3 in range(3):
                        s.op("tensor", lambda te, pb4=pb4, j3=j3, t=t, po=po, head=head, at0=at0:
                             te.matmul(pb4[po:po + 64, 0:128], s.arb(VTM + (t + j3) * 128 + po, 64),
                                       s.arb(at0 + j3 * 128, 128),
                                       start=(j3 == 0), stop=(j3 == 2),
                                       tile_position=(0, po)),
                             waits=bw4 if j3 == 0 else (), inc=sT if j3 == 2 else None)
                    avmm = [sT.at()]
                    if head == 1:
                        s.op("scalar", lambda sc, pb4=pb4, hh=hh, t=t:
                             sc.copy(s.arb(s.LNT0 + hh * 2048 + t * 128, 128), pb4[:, 0:128]),
                             waits=avmm, inc=sS)
                        s.bdone(4, [sS.at()])
            self.fence()
        for tg in range(4):
            banks = []
            bw_all = [sV.at(), sS.at()]
            for fj in range(NDC):
                pb, bw = s.bank(fj, extra=bw_all)
                banks.append(pb)
                for hh in range(NDC):
                    s.op("tensor", lambda te, pb=pb, fj=fj, hh=hh, tg=tg:
                         te.matmul(pb[:, 0:512], s.arb(s.WB0 + hh * 1024 + fj * 128, 128),
                                   s.arb(s.LNT0 + hh * 2048 + tg * 512, 512), start=(hh == 0), stop=(hh == NDC - 1)),
                         waits=(bw + [s.sG.at()]) if hh == 0 else (), inc=sT if hh == NDC - 1 else None)
            pr_mm = [sT.at()]
            for fj in range(NDC):
                osb = s.arf(SCB_F, 512)
                s.op("scalar", lambda sc, fj=fj, osb=osb:
                     sc.activation(osb, banks[fj][:, 0:512], AF.Identity, bias=s.bp_b[:, fj:fj + 1]),
                     waits=pr_mm if fj == 0 else [sV.at()], inc=sS)
                ev = [sS.at()]
                for i in range(4):
                    t2 = tg * 4 + i
                    pb2, bw2 = s.bank(fj, extra=ev)
                    s.op("tensor", lambda te, pb2=pb2, osb=osb, i=i:
                         te.transpose(pb2[:, 0:128], osb[:, i * 128:(i + 1) * 128], s.idt_f[:]),
                         waits=bw2, inc=sT)
                    dst = s.xb[:, t2 * DM + fj * 128:t2 * DM + (fj + 1) * 128]
                    s.op("vector", lambda v, pb2=pb2, dst=dst: v.tensor_add(dst, dst, pb2[:, 0:128]),
                         waits=[sT.at()], inc=sV)
                    s.bdone(fj, [sV.at()])
        self.fence()

    def store(self):
        s = self
        for t in range(NT):
            s.op("sync", lambda sy, t=t: sy.dma_start(out=s.y_out[t * 128:(t + 1) * 128, :],
                                                      in_=s.xb[:, t * DM:(t + 1) * DM]),
                 inc=s.sL, inc_n=16)
        s.op("sync", lambda sy: None, waits=[s.sL.at()])

    def emit(self, block):
        @block.sync
        def _(sy):
            for f in self.ops["sync"]:
                f(sy)

        @block.tensor
        def _(te):
            for f in self.ops["tensor"]:
                f(te)

        @block.vector
        def _(v):
            for f in self.ops["vector"]:
                f(v)

        @block.scalar
        def _(sc):
            for f in self.ops["scalar"]:
                f(sc)

        @block.gpsimd
        def _(g):
            for f in self.ops["gpsimd"]:
                f(g)


def build(n_sub=4):
    nc = bass.Bass("TRN2", target_bir_lowering=False)
    mk = MK(nc, n_sub)
    with ExitStack() as ctx:
        mk.declare(ctx)
        block = ctx.enter_context(nc.Block())
        mk.p0()
        if n_sub >= 1:
            mk.attn_a()
        if n_sub >= 2:
            mk.moe()
        if n_sub >= 3:
            mk.attn_b()
        if n_sub >= 4:
            mk.moe()
        mk.store()
        mk.emit(block)
    return nc


def kernel(**inputs):
    maps = shard_inputs(inputs)
    nc = build(4)
    res = run_bass_kernel_spmd(nc, maps, core_ids=list(range(NCORE)))
    y = np.concatenate([r["y_out"] for r in res.results], axis=0)
    return y.reshape(2, SEQ, DM)


# version 04
# speedup vs baseline: 1.1140x; 1.1140x over previous
import numpy as np
from contextlib import ExitStack
import concourse.bass as bass
import concourse.mybir as mybir
from concourse.bass_utils import run_bass_kernel_spmd
import ml_dtypes

F32 = mybir.dt.float32
BF16 = mybir.dt.bfloat16
I32 = mybir.dt.int32
AF = mybir.ActivationFunctionType
ALU = mybir.AluOpType
AX = mybir.AxisListType.X

NCORE = 8
T = 2048
NT = 16
DM = 1024
NDC = 8
SEQ = 8192
CAP = 1280
NE = 16
DFF = 4096
NFC = 32
EO_STRIDE = 2 * CAP + 64
FM = T + 256
LN_EPS = 1e-5
ATT_EPS = 1e-4
GROUPS_ALL = [list(range(NCORE))]
GROUPS_BATCH = [[0, 1, 2, 3], [4, 5, 6, 7]]


def shard_inputs(inp):
    x = np.ascontiguousarray(np.asarray(inp["L0_attn_a_x"], dtype=np.float32).reshape(2 * SEQ, DM))
    w1 = np.asarray(inp["L0_ffn_w1"], dtype=np.float32)
    w2 = np.asarray(inp["L0_ffn_w2"], dtype=np.float32)
    w1r = np.ascontiguousarray(w1.reshape(NE, NDC, 128, NFC, 128).transpose(0, 3, 2, 1, 4)
                               .reshape(NE, NFC, 128, NDC * 128))
    gate_wT = np.ascontiguousarray(np.asarray(inp["L0_ffn_gate_w"], dtype=np.float32).T)
    wqkv_b_hm = np.ascontiguousarray(
        np.asarray(inp["L0_attn_b_w_qkv"], dtype=np.float32)
        .reshape(NDC, 128, 3, NDC, 128).transpose(3, 1, 2, 0, 4).reshape(NDC, 128, 3 * DM))
    maps = []
    for c in range(NCORE):
        coremask8 = np.zeros((16, 8), np.float32)
        coremask8[:, :c] = 1.0
        myexp = np.zeros((16, 2), np.float32)
        myexp[2 * c, 0] = 1.0
        myexp[2 * c + 1, 1] = 1.0
        li = np.arange(128, dtype=np.int32)
        hleft = ((c - 1) * 256 + 128 + li) if c % 4 > 0 else (c * 256 + li)
        hright = ((c + 1) * 256 + li) if c % 4 < 3 else (c * 256 + 128 + li)
        maps.append(dict(
            x_in=np.ascontiguousarray(x[c * T:(c + 1) * T]),
            wqkv_a=np.asarray(inp["L0_attn_a_W_qkv"], dtype=np.float32),
            bqkv_a=np.ascontiguousarray(np.asarray(inp["L0_attn_a_b_qkv"], dtype=np.float32).reshape(1, 3 * DM)),
            bq_a_in=np.ascontiguousarray(np.asarray(inp["L0_attn_a_b_qkv"], dtype=np.float32)[:DM].reshape(8, 128).T),
            wout_a=np.asarray(inp["L0_attn_a_W_out"], dtype=np.float32),
            bout_a=np.ascontiguousarray(np.asarray(inp["L0_attn_a_b_out"], dtype=np.float32).reshape(8, 128).T),
            wqkv_b=wqkv_b_hm,
            wproj_b=np.asarray(inp["L0_attn_b_w_proj"], dtype=np.float32),
            bproj_b=np.ascontiguousarray(np.asarray(inp["L0_attn_b_b_proj"], dtype=np.float32).reshape(8, 128).T),
            gate_wT=gate_wT,
            w1_m=np.ascontiguousarray(w1r[2 * c:2 * c + 2]),
            w2_m=np.ascontiguousarray(w2[2 * c:2 * c + 2]),
            coremask8=coremask8,
            myexp=myexp,
            p0col=np.full((128, 1), float((c % 4) * T), np.float32),
            hleft_idx=hleft.reshape(128, 1).astype(np.int32),
            hright_idx=hright.reshape(128, 1).astype(np.int32),
        ))
    return maps


class Sem:
    def __init__(self, h):
        self.h = h
        self.v = 0

    def at(self):
        return (self, self.v)


class MK:

    def __init__(self, nc, n_sub=4):
        self.nc = nc
        self.n_sub = n_sub
        self.ops = {k: [] for k in ("sync", "tensor", "vector", "scalar", "gpsimd")}
        self.bfree = [[] for _ in range(8)]

    def op(self, engine, fn, waits=(), inc=None, inc_n=1):
        if inc is not None:
            inc.v += inc_n
        ws = tuple((s.h, v) for (s, v) in waits if v > 0)

        def run(eng, fn=fn, ws=ws, inc=inc, inc_n=inc_n):
            for h, v in ws:
                eng.wait_ge(h, v)
            ins = fn(eng)
            if inc is not None:
                ins.then_inc(inc.h, inc_n)
        self.ops[engine].append(run)

    def snap(self):
        return [s.at() for s in self.allsems]

    def fence(self):
        sn = self.snap()
        for e in ("sync", "tensor", "vector", "scalar", "gpsimd"):
            self.op(e, lambda eng: None, waits=sn)
        for i in range(8):
            self.bfree[i] = []

    def arb(self, c0, n):
        assert c0 >= 0 and c0 + n <= 70656, (c0, n)
        return self.AR[:, c0:c0 + n]

    def arf(self, c0, n):
        assert c0 % 2 == 0
        return self.arb(c0, 2 * n).bitcast(F32)

    def declare(self, ctx):
        nc = self.nc
        dt = nc.dram_tensor
        self.x_in = dt("x_in", [T, DM], F32, kind="ExternalInput")
        self.wqkv_a = dt("wqkv_a", [DM, 3 * DM], F32, kind="ExternalInput")
        self.bqkv_a = dt("bqkv_a", [1, 3 * DM], F32, kind="ExternalInput")
        self.bq_a_in = dt("bq_a_in", [128, 8], F32, kind="ExternalInput")
        self.wout_a = dt("wout_a", [DM, DM], F32, kind="ExternalInput")
        self.bout_a = dt("bout_a", [128, 8], F32, kind="ExternalInput")
        self.wqkv_b = dt("wqkv_b", [NDC, 128, 3 * DM], F32, kind="ExternalInput")
        self.wproj_b = dt("wproj_b", [DM, DM], F32, kind="ExternalInput")
        self.bproj_b = dt("bproj_b", [128, 8], F32, kind="ExternalInput")
        self.gate_wT_d = dt("gate_wT", [DM, NE], F32, kind="ExternalInput")
        self.w1_m = dt("w1_m", [2, NFC, 128, NDC * 128], F32, kind="ExternalInput")
        self.w2_m = dt("w2_m", [2, DFF, DM], F32, kind="ExternalInput")
        self.coremask8_d = dt("coremask8", [16, 8], F32, kind="ExternalInput")
        self.myexp_d = dt("myexp", [16, 2], F32, kind="ExternalInput")
        self.p0col_d = dt("p0col", [128, 1], F32, kind="ExternalInput")
        self.hleft_d = dt("hleft_idx", [128, 1], I32, kind="ExternalInput")
        self.hright_d = dt("hright_idx", [128, 1], I32, kind="ExternalInput")
        self.y_out = dt("y_out", [T, DM], F32, kind="ExternalOutput")
        import os
        self.dbg = bool(os.environ.get("MKDEBUG"))
        self.moe_calls = 0
        if self.dbg:
            self.dbg_dgi = dt("dbg_dgi", [128, 20], I32, kind="ExternalOutput")
            self.dbg_ieo = dt("dbg_ieo", [128, NT], I32, kind="ExternalOutput")
            self.dbg_isc = dt("dbg_isc", [128, NT], I32, kind="ExternalOutput")
            self.dbg_gk = dt("dbg_gk", [128, NT], F32, kind="ExternalOutput")
            self.dbg_cnt = dt("dbg_cnt", [16, 8], F32, kind="ExternalOutput")
            self.dbg_seg = dt("dbg_seg", [16, 8], F32, kind="ExternalOutput")
            self.dbg_cmp = dt("dbg_cmp", [T, DM], BF16, kind="ExternalOutput")
            self.dbg_cmpo = dt("dbg_cmpo", [NCORE * T, DM], BF16, kind="ExternalOutput")
            self.dbg_eoin = dt("dbg_eoin", [EO_STRIDE, DM], BF16, kind="ExternalOutput")
            self.dbg_disp = dt("dbg_disp", [128, NDC * CAP], BF16, kind="ExternalOutput")
            self.dbg_h = dt("dbg_h", [128, NFC * CAP], BF16, kind="ExternalOutput")
        self.kv_ar_in = dt("kv_ar_in", [128, 2080], F32)
        self.kv_ar_out = dt("kv_ar_out", [128, 2080], F32)
        self.cnt_in = dt("cnt_in", [2, 16], F32)
        self.cnt_out = dt("cnt_out", [16, 16], F32)
        self.cmp_in = dt("cmp_in", [T, DM], BF16)
        self.cmp_out = dt("cmp_out", [NCORE * T, DM], BF16)
        self.eo_in = dt("eo_in", [EO_STRIDE, DM], BF16)
        self.eo_out = dt("eo_out", [NCORE * EO_STRIDE, DM], BF16)
        self.halo_in = dt("halo_in", [256, DM], F32)
        self.halo_out = dt("halo_out", [NCORE * 256, DM], F32)
        it = nc.inline_tensor
        self.c_idf = it(np.eye(128, dtype=np.float32), name="c_idf")
        self.c_idb = it(np.eye(128, dtype=np.float32).astype(ml_dtypes.bfloat16), name="c_idb")
        self.c_1b = it(np.ones((128, 1), dtype=ml_dtypes.bfloat16), name="c_1b")
        self.c_1f = it(np.ones((128, 1), dtype=np.float32), name="c_1f")
        self.c_1r = it(np.ones((1, 128), dtype=np.float32), name="c_1r")
        base_e = np.array([(e // 2) * EO_STRIDE + (e % 2) * CAP for e in range(NE)], np.float32)
        trash_e = np.array([(e // 2) * EO_STRIDE + 2 * CAP for e in range(NE)], np.float32)
        self.c_base = it(np.broadcast_to(base_e, (128, NE)).copy(), name="c_base")
        self.c_trash = it(np.broadcast_to(trash_e, (128, NE)).copy(), name="c_trash")
        self.c_io16 = it(np.broadcast_to(np.arange(16, dtype=np.float32), (128, 16)).copy(), name="c_io16")
        self.c_io8 = it(np.broadcast_to(np.arange(8, dtype=np.float32), (128, 8)).copy(), name="c_io8")
        b2 = np.zeros((2, 128), np.float32)
        b2[0, :64] = 1.0
        b2[1, 64:] = 1.0
        self.c_b2 = it(b2.astype(ml_dtypes.bfloat16), name="c_b2")

        en = ctx.enter_context
        sbt = lambda name, shape, d=F32: en(nc.sbuf_tensor(name, shape, d))
        self.xb = sbt("xb", [128, NT * DM])
        self.AR = sbt("AR", [128, 70656], BF16)
        self.csm = sbt("csm", [128, 64])
        self.idt_f = sbt("idt_f", [128, 128])
        self.idt_b = sbt("idt_b", [128, 128], BF16)
        self.ones_b = sbt("ones_b", [128, 1], BF16)
        self.ones_f = sbt("ones_f", [128, 1])
        self.ones1r = sbt("ones1r", [1, 128])
        self.b2h = sbt("b2h", [2, 128], BF16)
        self.cst = sbt("cst", [128, 64])
        self.cmask8 = sbt("cmask8", [16, 8])
        self.myexp_s = sbt("myexp_s", [16, 2])
        self.cntT = sbt("cntT", [16, 8])
        self.segT = sbt("segT", [16, 8])
        self.cumT = sbt("cumT", [16, 8])
        self.zrow16 = sbt("zrow16", [16, 8])
        self.gwT = sbt("gwT", [128, NDC * NE], BF16)
        self.masks = sbt("masks", [128, NT * NE])
        self.gcol = sbt("gcol", [128, NT])
        self.idx_sc = sbt("idx_sc", [128, NT], I32)
        self.idx_eo = sbt("idx_eo", [128, NT], I32)
        self.gk = sbt("gk", [128, NT])
        self.hl_idx = sbt("hl_idx", [128, 1], I32)
        self.hr_idx = sbt("hr_idx", [128, 1], I32)
        self.bq_a = sbt("bq_a", [128, 8])
        self.bo_a = sbt("bo_a", [128, 8])
        self.bp_b = sbt("bp_b", [128, 8])
        self.dgi = sbt("dgi", [128, 20], I32)
        self.psum = [en(nc.psum_tensor(f"pb{i}", [128, 512], F32)) for i in range(8)]
        self.allsems = []
        for nm in ("sL", "sG", "sT", "sV", "sS", "sC", "sP"):
            s = Sem(en(nc.semaphore(nm)))
            setattr(self, nm, s)
            self.allsems.append(s)
        self.C_EPSLN, self.C_P0, self.C_HPI = 0, 2, 3
        self.C_BASEE, self.C_TRASHE, self.C_IOTA16, self.C_IOTA8 = 8, 24, 40, 56
        self.LNT0 = 0
        self.LNF0 = 16384
        self.WB0 = 34816
        self.SC0 = 51200
        self.HS0 = 0
        self.W2G0 = 40960
        self.EOS0 = 49152

    def bank(self, i, extra=()):
        w = list(self.bfree[i]) + list(extra)
        self.bfree[i] = []
        return self.psum[i], w

    def bdone(self, i, waits):
        self.bfree[i] = list(waits)

    def p0(self):
        s = self
        sL, sG, sV = s.sL, s.sG, s.sV

        def d(dst, src):
            s.op("sync", lambda sy, dst=dst, src=src: sy.dma_start(out=dst, in_=src), inc=sL, inc_n=16)
        d(s.idt_f[:], s.c_idf[:, :])
        d(s.idt_b[:], s.c_idb[:, :])
        d(s.ones_b[:], s.c_1b[:, :])
        d(s.ones_f[:], s.c_1f[:, :])
        d(s.ones1r[:], s.c_1r[:, :])
        d(s.b2h[:], s.c_b2[:, :])
        d(s.cst[:, s.C_P0:s.C_P0 + 1], s.p0col_d[:, :])
        d(s.cst[:, s.C_BASEE:s.C_BASEE + 16], s.c_base[:, :])
        d(s.cst[:, s.C_TRASHE:s.C_TRASHE + 16], s.c_trash[:, :])
        d(s.cst[:, s.C_IOTA16:s.C_IOTA16 + 16], s.c_io16[:, :])
        d(s.cst[:, s.C_IOTA8:s.C_IOTA8 + 8], s.c_io8[:, :])
        d(s.cmask8[:], s.coremask8_d[:, :])
        d(s.myexp_s[:], s.myexp_d[:, :])
        d(s.hl_idx[:], s.hleft_d[:, :])
        d(s.hr_idx[:], s.hright_d[:, :])
        d(s.bq_a[:], s.bq_a_in[:, :])
        d(s.bo_a[:], s.bout_a[:, :])
        d(s.bp_b[:], s.bproj_b[:, :])
        for t in range(NT):
            d(s.xb[:, t * DM:(t + 1) * DM], s.x_in[t * 128:(t + 1) * 128, :])
        for dc in range(NDC):
            s.op("gpsimd", lambda g, dc=dc: g.dma_start(out=s.gwT[:, dc * NE:(dc + 1) * NE],
                                                        in_=s.gate_wT_d[dc * 128:(dc + 1) * 128, :]),
                 inc=sG, inc_n=16)
        s.op("vector", lambda v: v.memset(s.cst[:, s.C_EPSLN:s.C_EPSLN + 1], LN_EPS), inc=sV)
        s.op("vector", lambda v: v.memset(s.cst[:, s.C_HPI:s.C_HPI + 1], float(np.pi / 2)), inc=sV)
        s.op("vector", lambda v: v.memset(s.zrow16[:], 0.0), inc=sV)
        self.fence()

    def lnt(self, halo=False):
        s = self
        sV, sS, sT = s.sV, s.sS, s.sT
        HXL = s.SC0 + 4096
        HXR = s.SC0 + 8192
        tiles = [("own", t) for t in range(NT)]
        if halo:
            tiles += [("hl", 0), ("hr", 0)]
        for kind, t in tiles:
            cc = [s.csm[:, i:i + 1] for i in range(7)]
            c_sum, c_mean, c_m2, c_var, c_sd, c_rstd, c_sumsq = cc
            if kind == "own":
                xt = s.xb[:, t * DM:(t + 1) * DM]
                tmdst = s.arb(s.LNT0 + t * DM, DM)
                fmcol = 128 + t * 128
            else:
                xt = s.arf(HXL if kind == "hl" else HXR, DM)
                tmdst = s.arb(s.SC0 + 12288 + (0 if kind == "hl" else DM), DM)
                fmcol = 0 if kind == "hl" else (T + 128)
            trash = s.arf(s.SC0, 1024)
            sn = [sV.at(), sS.at(), s.sL.at(), s.sG.at()]
            s.op("vector", lambda v, xt=xt, c=c_sum: v.reduce_sum(c, xt, axis=AX), waits=sn, inc=sV)
            s.op("scalar", lambda sc, xt=xt, o=trash, a=c_sumsq: sc.activation(o, xt, AF.Square, accum_out=a),
                 waits=sn, inc=sS)
            s.op("vector", lambda v, i=c_sum, o=c_mean: v.tensor_scalar(o, i, 1.0 / DM, None, ALU.mult),
                 waits=[sV.at()], inc=sV)
            s.op("vector", lambda v, i=c_mean, o=c_m2: v.tensor_mul(o, i, i), waits=[sV.at()], inc=sV)
            s.op("vector", lambda v, i=c_sumsq, o=c_var: v.tensor_scalar(o, i, 1.0 / DM, None, ALU.mult),
                 waits=[sS.at()], inc=sV)
            s.op("vector", lambda v, o=c_var, a=c_m2: v.tensor_sub(o, o, a), waits=[sV.at()], inc=sV)
            s.op("scalar", lambda sc, i=c_var, o=c_sd:
                 sc.activation(o, i, AF.Sqrt, bias=s.cst[:, s.C_EPSLN:s.C_EPSLN + 1]),
                 waits=[sV.at()], inc=sS)
            s.op("vector", lambda v, i=c_sd, o=c_rstd: v.reciprocal(o, i), waits=[sS.at()], inc=sV)
            s.op("vector", lambda v, xt=xt, m=c_mean, r=c_rstd, o=tmdst:
                 v.tensor_scalar(o, xt, m, r, ALU.subtract, ALU.mult), waits=[sV.at()], inc=sV)
            lnt_done = sV.at()
            for dc in range(NDC):
                bi = 6 + (dc % 2)
                pb, bw = s.bank(bi, extra=[lnt_done])
                s.op("tensor", lambda te, pb=pb, src=tmdst[:, dc * 128:(dc + 1) * 128]:
                     te.transpose(pb[:, 0:64].bitcast(BF16), src, s.idt_b[:]), waits=bw, inc=sT)
                s.op("scalar", lambda sc, pb=pb, dst=s.arb(s.LNF0 + dc * FM + fmcol, 128):
                     sc.copy(dst, pb[:, 0:64].bitcast(BF16)), waits=[sT.at()], inc=sS)
                s.bdone(bi, [sS.at()])
        self.fence()

    def attn_a(self):
        s = self
        sL, sG, sT, sV, sS, sC = s.sL, s.sG, s.sT, s.sV, s.sS, s.sC
        SC = s.SC0
        COS0, SIN0 = SC, SC + 2048
        KC0, KS0, VT0 = SC + 4096, SC + 5120, SC + 6144
        KT_F = SC + 7168
        KVS_F = SC + 9216
        POSR_F = SC + 13376
        self.lnt()

        SCALE = float(np.pi / 2 / SEQ)
        posr = s.arf(POSR_F, 2048)
        s.op("gpsimd", lambda g: g.iota(posr, pattern=[[1, 2048]], base=0, channel_multiplier=0,
                                        allow_small_or_imprecise_dtypes=True), inc=s.sP)
        s.op("vector", lambda v: v.tensor_scalar(posr, posr, s.cst[:, s.C_P0:s.C_P0 + 1], None, ALU.add),
             waits=[s.sP.at()], inc=sV)
        s.op("scalar", lambda sc: sc.activation(s.arb(SIN0, 2048), posr, AF.Sin, scale=SCALE),
             waits=[sV.at()], inc=sS)
        s.op("scalar", lambda sc: sc.activation(s.arb(COS0, 2048), posr, AF.Sin, scale=-SCALE,
                                                bias=s.cst[:, s.C_HPI:s.C_HPI + 1]),
             waits=[sV.at()], inc=sS)
        pc = s.csm[:, 8:24]
        cc = s.csm[:, 24:40]
        scc = s.csm[:, 40:56]
        s.op("gpsimd", lambda g: g.iota(pc, pattern=[[128, 16]], base=0, channel_multiplier=1,
                                        allow_small_or_imprecise_dtypes=True), inc=s.sP)
        s.op("vector", lambda v: v.tensor_scalar(pc, pc, s.cst[:, s.C_P0:s.C_P0 + 1], None, ALU.add),
             waits=[s.sP.at()], inc=sV)
        s.op("scalar", lambda sc: sc.activation(scc, pc, AF.Sin, scale=SCALE), waits=[sV.at()], inc=sS)
        s.op("scalar", lambda sc: sc.activation(cc, pc, AF.Sin, scale=-SCALE,
                                                bias=s.cst[:, s.C_HPI:s.C_HPI + 1]), waits=[sV.at()], inc=sS)
        trig_done = [sS.at()]
        bkv = s.arf(POSR_F, 2048)
        s.op("sync", lambda sy: sy.dma_start(out=bkv[:, 0:1024],
                                             in_=s.bqkv_a.ap()[0:1, DM:2 * DM].partition_broadcast(128)),
             waits=trig_done, inc=sL, inc_n=16)
        s.op("sync", lambda sy: sy.dma_start(out=bkv[:, 1024:2048],
                                             in_=s.bqkv_a.ap()[0:1, 2 * DM:3 * DM].partition_broadcast(128)),
             waits=trig_done, inc=sL, inc_n=16)
        bkv_ld = [sL.at()]

        for dc in range(NDC):
            s.op("gpsimd", lambda g, dc=dc: g.dma_start(out=s.arb(s.WB0 + dc * 2048, 1024),
                                                        in_=s.wqkv_a[dc * 128:(dc + 1) * 128, DM:2 * DM]),
                 inc=sG, inc_n=16)
            s.op("gpsimd", lambda g, dc=dc: g.dma_start(out=s.arb(s.WB0 + dc * 2048 + 1024, 1024),
                                                        in_=s.wqkv_a[dc * 128:(dc + 1) * 128, 2 * DM:3 * DM]),
                 inc=sG, inc_n=16)
        wkv_ld = [sG.at()]

        kvs = s.arf(KVS_F, 2080)
        s.op("vector", lambda v: v.memset(kvs, 0.0), inc=sV)
        for t in range(NT):
            mm_w = wkv_ld + trig_done
            for j, wcol in enumerate((0, 512, 1024, 1536)):
                pb, bw = s.bank(j, extra=mm_w)
                for dc in range(NDC):
                    ap = s.arb(s.LNF0 + dc * FM + 128 + t * 128, 128)
                    s.op("tensor", lambda te, pb=pb, ap=ap, dc=dc, wcol=wcol:
                         te.matmul(pb[:, 0:512], ap, s.arb(s.WB0 + dc * 2048 + wcol, 512),
                                   start=(dc == 0), stop=(dc == NDC - 1)),
                         waits=bw if dc == 0 else (), inc=sT if dc == NDC - 1 else None)
            mm_done = [sT.at()]
            kt = s.arf(KT_F, 1024)
            kc = s.arb(KC0, 1024)
            ks = s.arb(KS0, 1024)
            vt = s.arb(VT0, 1024)
            s.op("vector", lambda v, kt=kt: v.tensor_add(kt[:, 0:512], s.psum[0][:, 0:512], bkv[:, 0:512]),
                 waits=mm_done + bkv_ld, inc=sV)
            s.op("vector", lambda v, kt=kt: v.tensor_add(kt[:, 512:1024], s.psum[1][:, 0:512],
                                                         bkv[:, 512:1024]), inc=sV)
            s.op("scalar", lambda sc, kt=kt: sc.activation(kt, kt, AF.Relu), waits=[sV.at()], inc=sS)
            s.op("vector", lambda v, kt=kt, kc=kc, t=t: v.tensor_scalar(kc, kt, cc[:, t:t + 1], None, ALU.mult),
                 waits=[sS.at()], inc=sV)
            s.op("vector", lambda v, kt=kt, ks=ks, t=t: v.tensor_scalar(ks, kt, scc[:, t:t + 1], None, ALU.mult),
                 inc=sV)
            s.op("vector", lambda v, vt=vt: v.tensor_add(vt[:, 0:512], s.psum[2][:, 0:512],
                                                         bkv[:, 1024:1536]), waits=mm_done, inc=sV)
            s.op("vector", lambda v, vt=vt: v.tensor_add(vt[:, 512:1024], s.psum[3][:, 0:512],
                                                         bkv[:, 1536:2048]), inc=sV)
            kcv = [sV.at()]
            for j in range(4):
                s.bdone(j, kcv)
            for hh in range(NDC):
                bi = 4 + (hh % 2)
                pb, bw = s.bank(bi, extra=kcv)
                kcs_ = kc[:, hh * 128:(hh + 1) * 128]
                kss_ = ks[:, hh * 128:(hh + 1) * 128]
                vs_ = vt[:, hh * 128:(hh + 1) * 128]
                s.op("tensor", lambda te, pb=pb, a=kcs_, b=vs_:
                     te.matmul(pb[:, 0:128], a, b, start=True, stop=False), waits=bw)
                s.op("tensor", lambda te, pb=pb, a=kcs_:
                     te.matmul(pb[:, 128:129], a, s.ones_b[:], start=False, stop=False))
                s.op("tensor", lambda te, pb=pb, a=kss_, b=vs_:
                     te.matmul(pb[:, 132:260], a, b, start=False, stop=False))
                s.op("tensor", lambda te, pb=pb, a=kss_:
                     te.matmul(pb[:, 260:261], a, s.ones_b[:], start=False, stop=True), inc=sT)
                kvmm = [sT.at()]
                dstc = kvs[:, hh * 130:hh * 130 + 129]
                dsts = kvs[:, 1040 + hh * 130:1040 + hh * 130 + 129]
                if t == 0:
                    s.op("vector", lambda v, pb=pb, d=dstc: v.tensor_copy(d, pb[:, 0:129]), waits=kvmm, inc=sV)
                    s.op("vector", lambda v, pb=pb, d=dsts: v.tensor_copy(d, pb[:, 132:261]), inc=sV)
                else:
                    s.op("vector", lambda v, pb=pb, d=dstc: v.tensor_add(d, d, pb[:, 0:129]), waits=kvmm, inc=sV)
                    s.op("vector", lambda v, pb=pb, d=dsts: v.tensor_add(d, d, pb[:, 132:261]), inc=sV)
                s.bdone(bi, [sV.at()])

        for dc in range(NDC):
            s.op("gpsimd", lambda g, dc=dc: g.dma_start(out=s.arb(s.WB0 + dc * 1024, 1024),
                                                        in_=s.wqkv_a[dc * 128:(dc + 1) * 128, 0:DM]),
                 waits=[sT.at()] if dc == 0 else (), inc=sG, inc_n=16)
            s.op("gpsimd", lambda g, dc=dc: g.dma_start(out=s.arb(s.WB0 + 8192 + dc * 1024, 1024),
                                                        in_=s.wout_a[dc * 128:(dc + 1) * 128, :]),
                 inc=sG, inc_n=16)
        w_ld = [sG.at()]
        s.op("gpsimd", lambda g: None, waits=[w_ld[0]])
        s.op("sync", lambda sy: sy.dma_start(out=s.kv_ar_in[:, :], in_=kvs), waits=[sV.at()], inc=sL, inc_n=16)
        s.op("gpsimd", lambda g: g.collective_compute(
            "AllReduce", ALU.add, replica_groups=GROUPS_BATCH,
            ins=[s.kv_ar_in.ap().opt()], outs=[s.kv_ar_out.ap().opt()]), waits=[sL.at(), w_ld[0]], inc=sC)
        s.op("sync", lambda sy: sy.dma_start(out=kvs, in_=s.kv_ar_out[:, :]), waits=[sC.at()], inc=sL, inc_n=16)
        kvs_ld = [sL.at()]

        SBD0 = SC + 4096
        NBD0 = SC + 6144
        CTXN0 = SC + 6400
        QR_F = SC + 10496
        QC0 = SC + 12544
        QS0 = SC + 13568
        NSB_F = SC + 14592
        RNB0 = SC + 16640
        BCB0 = SC + 17664
        OSB_F = SC + 18688
        for hh in range(NDC):
            for which, srcoff in ((0, 0), (1, 1040)):
                d_ = s.arb(SBD0 + hh * 256 + which * 128, 128)
                s.op("vector", lambda v, d=d_: v.memset(d, 0.0), waits=kvs_ld if hh == 0 and which == 0 else (),
                     inc=sV)
                s.op("vector", lambda v, d=d_, so=srcoff, hh=hh:
                     v.tensor_copy(d[0:64, 0:64], kvs[0:64, so + hh * 130:so + hh * 130 + 64]),
                     waits=[sV.at()], inc=sV)
                s.op("vector", lambda v, d=d_, so=srcoff, hh=hh:
                     v.tensor_copy(d[64:128, 64:128], kvs[64:128, so + hh * 130 + 64:so + hh * 130 + 128]),
                     waits=[sV.at()], inc=sV)
                n_ = s.arb(NBD0 + hh * 4 + which * 2, 2)
                s.op("vector", lambda v, d=n_: v.memset(d, 0.0), inc=sV)
                s.op("vector", lambda v, d=n_, so=srcoff, hh=hh:
                     v.tensor_copy(d[0:64, 0:1], kvs[0:64, so + hh * 130 + 128:so + hh * 130 + 129]),
                     waits=[sV.at()], inc=sV)
                s.op("vector", lambda v, d=n_, so=srcoff, hh=hh:
                     v.tensor_copy(d[64:128, 1:2], kvs[64:128, so + hh * 130 + 128:so + hh * 130 + 129]),
                     waits=[sV.at()], inc=sV)
        sbd_done = [sV.at()]

        for tg in range(4):
            for hh in range(NDC):
                par = hh % 2
                pbq, bwq = s.bank(0 if par == 0 else 4, extra=w_ld + sbd_done)
                for dc in range(NDC):
                    s.op("tensor", lambda te, pbq=pbq, dc=dc, hh=hh, tg=tg:
                         te.matmul(pbq[:, 0:512], s.arb(s.WB0 + dc * 1024 + hh * 128, 128),
                                   s.arb(s.LNF0 + dc * FM + 128 + tg * 512, 512),
                                   start=(dc == 0), stop=(dc == NDC - 1)),
                         waits=bwq if dc == 0 else (), inc=sT if dc == NDC - 1 else None)
                qr = s.arf(QR_F + par * 1024, 512)
                s.op("scalar", lambda sc, pbq=pbq, qr=qr, hh=hh:
                     sc.activation(qr, pbq[:, 0:512], AF.Relu, bias=s.bq_a[:, hh:hh + 1]),
                     waits=[sT.at()], inc=sS)
                s.bdone(0 if par == 0 else 4, [sS.at()])
                qc = s.arb(QC0 + par * 512, 512)
                qs = s.arb(QS0 + par * 512, 512)
                s.op("vector", lambda v, qr=qr, qc=qc, tg=tg: v.tensor_mul(qc, qr, s.arb(COS0 + tg * 512, 512)),
                     waits=[sS.at()], inc=sV)
                s.op("vector", lambda v, qr=qr, qs=qs, tg=tg: v.tensor_mul(qs, qr, s.arb(SIN0 + tg * 512, 512)),
                     inc=sV)
                qcs = [sV.at()]
                pbc, bwc = s.bank(1 if par == 0 else 5, extra=qcs)
                s.op("tensor", lambda te, pbc=pbc, hh=hh, qc=qc:
                     te.matmul(pbc[:, 0:512], s.arb(SBD0 + hh * 256, 128), qc, start=True, stop=False),
                     waits=bwc)
                s.op("tensor", lambda te, pbc=pbc, hh=hh, qs=qs:
                     te.matmul(pbc[:, 0:512], s.arb(SBD0 + hh * 256 + 128, 128), qs, start=False, stop=True),
                     inc=sT)
                pbn, bwn = s.bank(2 if par == 0 else 6, extra=qcs)
                s.op("tensor", lambda te, pbn=pbn, hh=hh, qc=qc:
                     te.matmul(pbn[0:2, 0:512], s.arb(NBD0 + hh * 4, 2), qc, start=True, stop=False),
                     waits=bwn)
                s.op("tensor", lambda te, pbn=pbn, hh=hh, qs=qs:
                     te.matmul(pbn[0:2, 0:512], s.arb(NBD0 + hh * 4 + 2, 2), qs, start=False, stop=True),
                     inc=sT)
                ctxmm = [sT.at()]
                nsb = s.arf(NSB_F + par * 1024, 512)
                s.op("vector", lambda v, pbn=pbn, nsb=nsb:
                     v.tensor_scalar(nsb[0:2, :], pbn[0:2, 0:512], ATT_EPS, None, ALU.add),
                     waits=ctxmm, inc=sV)
                s.bdone(2 if par == 0 else 6, [sV.at()])
                s.op("vector", lambda v, nsb=nsb: v.reciprocal(nsb[0:2, :], nsb[0:2, :]), waits=[sV.at()], inc=sV)
                rnb = s.arb(RNB0 + par * 512, 512)
                s.op("vector", lambda v, nsb=nsb, rnb=rnb: v.tensor_copy(rnb[0:2, :], nsb[0:2, :]),
                     waits=[sV.at()], inc=sV)
                pbb, bwb = s.bank(3 if par == 0 else 7, extra=[sV.at()])
                s.op("tensor", lambda te, pbb=pbb, rnb=rnb:
                     te.matmul(pbb[:, 0:512], s.b2h[:, :], rnb[0:2, :], start=True, stop=True),
                     waits=bwb, inc=sT)
                bcb = s.arb(BCB0 + par * 512, 512)
                s.op("scalar", lambda sc, pbb=pbb, bcb=bcb: sc.copy(bcb, pbb[:, 0:512]), waits=[sT.at()], inc=sS)
                s.bdone(3 if par == 0 else 7, [sS.at()])
                s.op("vector", lambda v, pbc=pbc, hh=hh, bcb=bcb:
                     v.tensor_mul(s.arb(CTXN0 + hh * 512, 512), pbc[:, 0:512], bcb),
                     waits=[sS.at()], inc=sV)
                s.bdone(1 if par == 0 else 5, [sV.at()])
            ctxn_done = [sV.at()]
            banks = []
            for fj in range(NDC):
                pb, bw = s.bank(fj, extra=ctxn_done)
                banks.append(pb)
                for hh in range(NDC):
                    s.op("tensor", lambda te, pb=pb, fj=fj, hh=hh:
                         te.matmul(pb[:, 0:512], s.arb(s.WB0 + 8192 + hh * 1024 + fj * 128, 128),
                                   s.arb(CTXN0 + hh * 512, 512), start=(hh == 0), stop=(hh == NDC - 1)),
                         waits=bw if hh == 0 else (), inc=sT if hh == NDC - 1 else None)
            wout_mm = [sT.at()]
            for fj in range(NDC):
                osb = s.arf(QR_F, 512)
                s.op("scalar", lambda sc, fj=fj, osb=osb:
                     sc.activation(osb, banks[fj][:, 0:512], AF.Identity, bias=s.bo_a[:, fj:fj + 1]),
                     waits=wout_mm if fj == 0 else [sV.at()], inc=sS)
                ev = [sS.at()]
                for i in range(4):
                    t2 = tg * 4 + i
                    pb2, bw2 = s.bank(fj, extra=ev)
                    s.op("tensor", lambda te, pb2=pb2, osb=osb, i=i:
                         te.transpose(pb2[:, 0:128], osb[:, i * 128:(i + 1) * 128], s.idt_f[:]),
                         waits=bw2, inc=sT)
                    dst = s.xb[:, t2 * DM + fj * 128:t2 * DM + (fj + 1) * 128]
                    s.op("vector", lambda v, pb2=pb2, dst=dst: v.tensor_add(dst, dst, pb2[:, 0:128]),
                         waits=[sT.at()], inc=sV)
                    s.bdone(fj, [sV.at()])
        self.fence()


    def moe(self):
        s = self
        sL, sG, sT, sV, sS, sC = s.sL, s.sG, s.sT, s.sV, s.sS, s.sC
        SC = s.SC0
        self.lnt()
        masket = s.arf(SC, 2048)
        cumet = s.arf(SC + 4096, 2048)
        zros = s.arf(SC + 8192, 2048)
        GT0 = SC + 12288
        W1B = SC + 4096
        ECST = SC + 16384
        Z = SC + 17408
        tmp16 = s.arf(Z, 16)
        rcol = s.arf(Z + 32, 16)
        crow = s.arf(Z + 64, 16)
        segi = s.arf(Z + 96, 16)
        segr = s.arf(Z + 128, 16)
        expt = s.arf(Z + 160, 16)
        sume = s.arf(Z + 192, 2)
        mxc = s.arf(Z + 196, 2)
        lrt = s.arf(Z + 200, 16)
        prefb = s.arf(Z + 232, 16)
        c1 = s.arf(Z + 264, 1)
        c2 = s.arf(Z + 266, 1)
        c3 = s.arf(Z + 268, 1)
        c4 = s.arf(Z + 270, 1)
        c5 = s.arf(Z + 272, 1)
        mcum = s.arf(Z + 274, 16)
        mseg = s.arf(Z + 306, 16)
        cumb = s.arf(Z + 338, 8)
        segb = s.arf(Z + 354, 8)
        tmp8 = s.arf(Z + 370, 8)
        pcol16 = s.arf(Z + 386, 1)
        prow = s.arf(Z + 388, 16)
        DSP0 = 40960
        HS0 = 0
        W2G0 = 40960
        EOSF = 57344

        s.op("vector", lambda v: v.memset(zros, 0.0), inc=sV)
        for t in range(NT):
            pb, bw = s.bank(0)
            for dc in range(NDC):
                s.op("tensor", lambda te, pb=pb, dc=dc, t=t:
                     te.matmul(pb[:, 0:16], s.arb(s.LNF0 + dc * FM + 128 + t * 128, 128),
                               s.gwT[:, dc * NE:(dc + 1) * NE], start=(dc == 0), stop=(dc == NDC - 1)),
                     waits=bw if dc == 0 else (), inc=sT if dc == NDC - 1 else None)
            lg = [sT.at()]
            mx = mxc[:, 0:1]
            nmx = mxc[:, 1:2]
            s.op("vector", lambda v, pb=pb, mx=mx: v.reduce_max(mx, pb[:, 0:16], axis=AX), waits=lg, inc=sV)
            s.op("vector", lambda v, mx=mx, nmx=nmx: v.tensor_scalar(nmx, mx, -1.0, None, ALU.mult),
                 waits=[sV.at()], inc=sV)
            s.op("scalar", lambda sc, pb=pb, nmx=nmx: sc.activation(expt, pb[:, 0:16], AF.Exp, bias=nmx,
                                                                   accum_out=sume[:, 0:1]),
                 waits=[sV.at()], inc=sS)
            s.op("vector", lambda v, t=t: v.reciprocal(s.gcol[:, t:t + 1], sume[:, 0:1]), waits=[sS.at()], inc=sV)
            s.op("vector", lambda v, pb=pb, mx=mx, t=t:
                 v.tensor_scalar(s.masks[:, t * NE:(t + 1) * NE], pb[:, 0:16], mx, None, ALU.is_equal),
                 waits=[sV.at()], inc=sV)
            s.bdone(0, [sV.at(), sS.at()])
        pb1, bw1 = s.bank(1)
        for t in range(NT):
            s.op("tensor", lambda te, pb1=pb1, t=t:
                 te.matmul(pb1[0:1, 0:16], s.ones_f[:], s.masks[:, t * NE:(t + 1) * NE],
                           start=(t == 0), stop=(t == NT - 1)),
                 waits=(bw1 + [sV.at()]) if t == 0 else (), inc=sT if t == NT - 1 else None)
        s.op("vector", lambda v: v.tensor_copy(crow[0:1, :], pb1[0:1, 0:16]), waits=[sT.at()], inc=sV)
        s.bdone(1, [sV.at()])
        s.op("vector", lambda v: v.tensor_tensor_scan(segi[0:1, :], crow[0:1, :], zros[0:1, 0:16], 0.0,
                                                      ALU.add, ALU.add), waits=[sV.at()], inc=sV)
        s.op("vector", lambda v: v.tensor_sub(segr[0:1, :], segi[0:1, :], crow[0:1, :]), waits=[sV.at()], inc=sV)
        s.op("sync", lambda sy: sy.dma_start(out=s.cnt_in[0:1, :], in_=crow[0:1, :]), waits=[sV.at()],
             inc=sL, inc_n=16)
        s.op("sync", lambda sy: sy.dma_start(out=s.cnt_in[1:2, :], in_=segr[0:1, :]), waits=[sV.at()],
             inc=sL, inc_n=16)
        s.op("gpsimd", lambda g: g.collective_compute(
            "AllGather", ALU.bypass, replica_groups=GROUPS_ALL,
            ins=[s.cnt_in.ap().opt()], outs=[s.cnt_out.ap().opt()]), waits=[sL.at()], inc=sC)
        cnt_ag = [sC.at()]

        for t in range(NT):
            bi = 6 + t % 2
            pb, bw = s.bank(bi, extra=[sV.at()])
            s.op("tensor", lambda te, pb=pb, t=t:
                 te.transpose(pb[0:16, 0:128], s.masks[:, t * NE:(t + 1) * NE], s.idt_f[:]), waits=bw, inc=sT)
            s.op("vector", lambda v, pb=pb, t=t: v.tensor_copy(masket[0:16, t * 128:(t + 1) * 128],
                                                               pb[0:16, 0:128]), waits=[sT.at()], inc=sV)
            s.bdone(bi, [sV.at()])
        s.op("vector", lambda v: v.tensor_tensor_scan(cumet[0:16, :], masket[0:16, :], zros[0:16, :], 0.0,
                                                      ALU.add, ALU.add), waits=[sV.at()], inc=sV)
        cum_done = [sV.at()]
        pb2, bw2 = s.bank(2, extra=[sV.at()])
        s.op("tensor", lambda te, pb2=pb2: te.matmul(pb2[:, 0:16], s.ones1r[:], segr[0:1, :],
                                                     start=True, stop=True), waits=bw2, inc=sT)
        segb_mm = [sT.at()]
        for t in range(NT):
            bi = 6 + t % 2
            pb, bw = s.bank(bi, extra=cum_done)
            s.op("tensor", lambda te, pb=pb, t=t:
                 te.transpose(pb[:, 0:16], cumet[0:16, t * 128:(t + 1) * 128], s.idt_f[0:16, 0:16]),
                 waits=bw, inc=sT)
            trm = [sT.at()]
            s.op("vector", lambda v, pb=pb, t=t: v.tensor_mul(tmp16, s.masks[:, t * NE:(t + 1) * NE],
                                                              pb[:, 0:16]), waits=trm, inc=sV)
            s.bdone(bi, [sV.at()])
            s.op("vector", lambda v, t=t: v.reduce_sum(lrt[:, t:t + 1], tmp16, axis=AX), waits=[sV.at()], inc=sV)
            s.op("vector", lambda v, t=t: v.tensor_scalar(lrt[:, t:t + 1], lrt[:, t:t + 1], -1.0, None, ALU.add),
                 waits=[sV.at()], inc=sV)
            s.op("vector", lambda v, t=t: v.tensor_mul(tmp16, s.masks[:, t * NE:(t + 1) * NE], pb2[:, 0:16]),
                 waits=segb_mm + [sV.at()], inc=sV)
            s.op("vector", lambda v, t=t: v.reduce_sum(c1, tmp16, axis=AX), waits=[sV.at()], inc=sV)
            s.op("vector", lambda v, t=t: v.tensor_add(c2, c1, lrt[:, t:t + 1]), waits=[sV.at()], inc=sV)
            s.op("vector", lambda v, t=t: v.tensor_copy(s.idx_sc[:, t:t + 1], c2), waits=[sV.at()], inc=sV)
            s.op("gpsimd", lambda g, t=t: g.indirect_dma_start(
                out=s.cmp_in[:, :], out_offset=bass.IndirectOffsetOnAxis(ap=s.idx_sc[:, t:t + 1], axis=0),
                in_=s.arb(s.LNT0 + t * DM, DM), in_offset=None),
                waits=[sV.at()], inc=sG, inc_n=16)
        s.bdone(2, [sV.at()])
        s.op("gpsimd", lambda g: g.collective_compute(
            "AllGather", ALU.bypass, replica_groups=GROUPS_ALL,
            ins=[s.cmp_in.ap().opt()], outs=[s.cmp_out.ap().opt()]), waits=[sG.at()], inc=sC)
        tok_ag = [sC.at()]

        s.op("sync", lambda sy: sy.dma_start(out=s.cntT[:, :],
                                             in_=s.cnt_out.ap()[0:16:2, :].rearrange("r e -> e r"),
                                             allow_slow_non_contiguous=True),
             waits=cnt_ag, inc=sL, inc_n=16)
        s.op("sync", lambda sy: sy.dma_start(out=s.segT[:, :],
                                             in_=s.cnt_out.ap()[1:16:2, :].rearrange("r e -> e r"),
                                             allow_slow_non_contiguous=True),
             waits=cnt_ag, inc=sL, inc_n=16)
        ld2 = [sL.at()]
        s.op("vector", lambda v: v.tensor_tensor_scan(s.cumT[:, :], s.cntT[:, :], s.zrow16[:, :], 0.0,
                                                      ALU.add, ALU.add), waits=ld2, inc=sV)
        s.op("vector", lambda v: v.tensor_mul(tmp8[0:16, :], s.cntT[:, :], s.cmask8[:, :]), waits=ld2, inc=sV)
        s.op("vector", lambda v: v.reduce_sum(pcol16[0:16, :], tmp8[0:16, 0:8], axis=AX), waits=[sV.at()], inc=sV)
        pcol_done = [sV.at()]
        bi = 6
        pb, bw = s.bank(bi, extra=pcol_done)
        s.op("tensor", lambda te, pb=pb: te.transpose(pb[0:1, 0:16], pcol16[0:16, 0:1], s.idt_f[0:16, 0:16]),
             waits=bw, inc=sT)
        s.op("vector", lambda v, pb=pb: v.tensor_copy(prow[0:1, :], pb[0:1, 0:16]), waits=[sT.at()], inc=sV)
        s.bdone(bi, [sV.at()])
        pb3, bw3 = s.bank(3, extra=[sV.at()])
        s.op("tensor", lambda te, pb3=pb3: te.matmul(pb3[:, 0:16], s.ones1r[:], prow[0:1, :],
                                                     start=True, stop=True), waits=bw3, inc=sT)
        s.op("vector", lambda v, pb3=pb3: v.tensor_copy(prefb, pb3[:, 0:16]), waits=[sT.at()], inc=sV)
        s.bdone(3, [sV.at()])
        for t in range(NT):
            s.op("vector", lambda v, t=t: v.tensor_mul(tmp16, s.masks[:, t * NE:(t + 1) * NE], prefb),
                 waits=[sV.at()], inc=sV)
            s.op("vector", lambda v: v.reduce_sum(c1, tmp16, axis=AX), waits=[sV.at()], inc=sV)
            s.op("vector", lambda v, t=t: v.tensor_add(c2, c1, lrt[:, t:t + 1]), waits=[sV.at()], inc=sV)
            s.op("vector", lambda v: v.tensor_scalar(c3, c2, float(CAP), None, ALU.is_lt), waits=[sV.at()],
                 inc=sV)
            s.op("vector", lambda v, t=t: v.tensor_mul(s.gk[:, t:t + 1], s.gcol[:, t:t + 1], c3),
                 waits=[sV.at()], inc=sV)
            s.op("vector", lambda v, t=t: v.tensor_mul(tmp16, s.masks[:, t * NE:(t + 1) * NE],
                                                       s.cst[:, s.C_BASEE:s.C_BASEE + 16]), inc=sV)
            s.op("vector", lambda v: v.reduce_sum(c4, tmp16, axis=AX), waits=[sV.at()], inc=sV)
            s.op("vector", lambda v, t=t: v.tensor_mul(tmp16, s.masks[:, t * NE:(t + 1) * NE],
                                                       s.cst[:, s.C_TRASHE:s.C_TRASHE + 16]), waits=[sV.at()],
                 inc=sV)
            s.op("vector", lambda v: v.reduce_sum(c5, tmp16, axis=AX), waits=[sV.at()], inc=sV)
            s.op("vector", lambda v: v.tensor_add(c4, c4, c2), waits=[sV.at()], inc=sV)
            s.op("vector", lambda v: v.tensor_sub(c4, c4, c5), waits=[sV.at()], inc=sV)
            s.op("vector", lambda v: v.scalar_tensor_tensor(c4, c4, c3, c5, ALU.mult, ALU.add),
                 waits=[sV.at()], inc=sV)
            s.op("vector", lambda v, t=t: v.tensor_copy(s.idx_eo[:, t:t + 1], c4), waits=[sV.at()], inc=sV)

        s.op("gpsimd", lambda g: g.iota(rcol[:, 0:10], pattern=[[128, 10]], base=0, channel_multiplier=1,
                                        allow_small_or_imprecise_dtypes=True), inc=s.sP)
        rdy = [s.sP.at(), sV.at()]
        pb4, bw4 = s.bank(4, extra=rdy)
        for el in range(2):
            s.op("tensor", lambda te, pb4=pb4, el=el:
                 te.matmul(pb4[0:1, el * 8:el * 8 + 8], s.myexp_s[:, el:el + 1], s.cumT[:, :],
                           start=(el == 0), stop=False), waits=bw4 if el == 0 else (), inc=None)
            s.op("tensor", lambda te, pb4=pb4, el=el:
                 te.matmul(pb4[0:1, 16 + el * 8:16 + el * 8 + 8], s.myexp_s[:, el:el + 1], s.segT[:, :],
                           start=False, stop=(el == 1)), inc=sT)
        s.op("vector", lambda v, pb4=pb4: v.tensor_copy(mcum[0:1, 0:8], pb4[0:1, 0:8]), waits=[sT.at()], inc=sV)
        s.op("vector", lambda v, pb4=pb4: v.tensor_copy(mcum[0:1, 8:16], pb4[0:1, 8:16]), inc=sV)
        s.op("vector", lambda v, pb4=pb4: v.tensor_copy(mseg[0:1, 0:8], pb4[0:1, 16:24]), inc=sV)
        s.op("vector", lambda v, pb4=pb4: v.tensor_copy(mseg[0:1, 8:16], pb4[0:1, 24:32]), inc=sV)
        s.bdone(4, [sV.at()])
        for el in range(2):
            pb5, bw5 = s.bank(5, extra=[sV.at()])
            s.op("tensor", lambda te, pb5=pb5, el=el:
                 te.matmul(pb5[:, 0:8], s.ones1r[:], mcum[0:1, el * 8:el * 8 + 8],
                           start=True, stop=False), waits=bw5, inc=sT)
            s.op("tensor", lambda te, pb5=pb5, el=el:
                 te.matmul(pb5[:, 8:16], s.ones1r[:], mseg[0:1, el * 8:el * 8 + 8],
                           start=False, stop=True), inc=sT)
            s.op("vector", lambda v, pb5=pb5: v.tensor_copy(cumb, pb5[:, 0:8]), waits=[sT.at()], inc=sV)
            s.op("vector", lambda v, pb5=pb5: v.tensor_copy(segb, pb5[:, 8:16]), waits=[sT.at()], inc=sV)
            s.bdone(5, [sV.at()])
            for st in range(10):
                r_ = rcol[:, st:st + 1]
                s.op("vector", lambda v, r_=r_: v.tensor_scalar(tmp8, cumb, r_, None, ALU.is_le),
                     waits=[sV.at()], inc=sV)
                s.op("vector", lambda v: v.reduce_sum(c1, tmp8, axis=AX), waits=[sV.at()], inc=sV)
                s.op("vector", lambda v: v.tensor_mul(tmp8, tmp8, cumb), waits=[sV.at()], inc=sV)
                s.op("vector", lambda v: v.reduce_max(c2, tmp8, axis=AX), waits=[sV.at()], inc=sV)
                s.op("vector", lambda v: v.tensor_scalar(tmp8, s.cst[:, s.C_IOTA8:s.C_IOTA8 + 8], c1, None,
                                                         ALU.is_equal), waits=[sV.at()], inc=sV)
                s.op("vector", lambda v: v.tensor_mul(tmp8, tmp8, segb), waits=[sV.at()], inc=sV)
                s.op("vector", lambda v: v.reduce_sum(c3, tmp8, axis=AX), waits=[sV.at()], inc=sV)
                s.op("vector", lambda v: v.tensor_scalar(c4, c1, float(T), None, ALU.mult), waits=[sV.at()], inc=sV)
                s.op("vector", lambda v: v.tensor_add(c4, c4, c3), waits=[sV.at()], inc=sV)
                s.op("vector", lambda v, r_=r_: v.tensor_add(c4, c4, r_), waits=[sV.at()], inc=sV)
                s.op("vector", lambda v: v.tensor_sub(c4, c4, c2), waits=[sV.at()], inc=sV)
                s.op("vector", lambda v: v.tensor_scalar_min(c4, c4, float(NCORE * T - 1)), waits=[sV.at()], inc=sV)
                s.op("vector", lambda v, el=el, st=st: v.tensor_copy(s.dgi[:, el * 10 + st:el * 10 + st + 1], c4),
                     waits=[sV.at()], inc=sV)
        dgi_done = [sV.at()]
        self.fence()

        for el in range(2):
            for st in range(10):
                gt = s.arb(GT0 + (st % 2) * 1024, 1024)
                s.op("gpsimd", lambda g, el=el, st=st, gt=gt: g.indirect_dma_start(
                    out=gt, out_offset=None, in_=s.cmp_out[:, :],
                    in_offset=bass.IndirectOffsetOnAxis(ap=s.dgi[:, el * 10 + st:el * 10 + st + 1], axis=0)),
                    waits=[sS.at()], inc=sG, inc_n=16)
                gw_ = [sG.at()]
                for dc in range(NDC):
                    bi = 6 + dc % 2
                    pb, bw = s.bank(bi, extra=gw_)
                    s.op("tensor", lambda te, pb=pb, gt=gt, dc=dc:
                         te.transpose(pb[:, 0:64].bitcast(BF16), gt[:, dc * 128:(dc + 1) * 128], s.idt_b[:]),
                         waits=bw, inc=sT)
                    s.op("scalar", lambda sc, pb=pb, dc=dc, st=st:
                         sc.copy(s.arb(DSP0 + dc * CAP + st * 128, 128), pb[:, 0:64].bitcast(BF16)),
                         waits=[sT.at()], inc=sS)
                    s.bdone(bi, [sS.at()])
            disp_done = [sS.at()]
            if self.dbg and self.moe_calls == 0 and el == 0:
                s.op("sync", lambda sy: sy.dma_start(out=s.dbg_disp[:, :], in_=s.arb(DSP0, NDC * CAP)),
                     waits=disp_done, inc=sL, inc_n=16)
                s.op("gpsimd", lambda g: None, waits=[sL.at()])
            burst_end = {}
            mmdone = {}

            def w1_burst(b):
                for fc in range(4 * b, 4 * b + 4):
                    prev = [mmdone[b - 2]] if (b >= 2 and fc == 4 * b) else []
                    s.op("gpsimd", lambda g, el=el, fc=fc:
                         g.dma_start(out=s.arb(W1B + (fc % 8) * 1024, 1024), in_=s.w1_m[el, fc, :, :]),
                         waits=(disp_done if b == 0 and fc == 0 else []) + prev, inc=sG, inc_n=16)
                burst_end[b] = sG.at()
                s.op("gpsimd", lambda g: None, waits=[burst_end[b]])
            w1_burst(0)
            w1_burst(1)
            for b in range(8):
                for fc in range(4 * b, 4 * b + 4):
                    wb = s.arb(W1B + (fc % 8) * 1024, 1024)
                    for sl, (c0, w) in enumerate(((0, 512), (512, 512), (1024, 256))):
                        bi = sl % 2
                        pb, bw = s.bank(bi, extra=[burst_end[b]] + (disp_done if fc == 0 else []))
                        for dc in range(NDC):
                            s.op("tensor", lambda te, pb=pb, wb=wb, dc=dc, c0=c0, w=w:
                                 te.matmul(pb[:, 0:w], wb[:, dc * 128:(dc + 1) * 128],
                                           s.arb(DSP0 + dc * CAP + c0, w), start=(dc == 0), stop=(dc == NDC - 1)),
                                 waits=bw if dc == 0 else (), inc=sT if dc == NDC - 1 else None)
                        s.op("scalar", lambda sc, pb=pb, fc=fc, c0=c0, w=w:
                             sc.activation(s.arb(HS0 + fc * CAP + c0, w), pb[:, 0:w], AF.Gelu_apprx_tanh),
                             waits=[sT.at()], inc=sS)
                        s.bdone(bi, [sS.at()])
                mmdone[b] = sT.at()
                if b + 2 <= 7:
                    w1_burst(b + 2)
            h_done = [sS.at()]
            if self.dbg and self.moe_calls == 0 and el == 0:
                s.op("sync", lambda sy: sy.dma_start(out=s.dbg_h[:, :], in_=s.arb(HS0, NFC * CAP)),
                     waits=h_done, inc=sL, inc_n=16)
                s.op("tensor", lambda te: None, waits=[sL.at()])
                s.op("gpsimd", lambda g: None, waits=[sL.at()])
            eosb = s.arf(EOSF, 5120)
            w2done = {}
            w2l = {}

            def w2_load(k):
                buf = W2G0 + (k % 2) * 8192
                fg = k % 4
                for j in range(8):
                    prevw = [w2done[k - 2]] if (k >= 2 and j == 0) else []
                    s.op("gpsimd", lambda g, el=el, fg=fg, j=j, buf=buf:
                         g.dma_start(out=s.arb(buf + j * 1024, 1024),
                                     in_=s.w2_m[el, (fg * 8 + j) * 128:(fg * 8 + j + 1) * 128, :]),
                         waits=(h_done if k == 0 and j == 0 else []) + prevw, inc=sG, inc_n=16)
                w2l[k] = sG.at()
                s.op("gpsimd", lambda g: None, waits=[w2l[k]])
            w2_load(0)
            w2_load(1)
            for half in range(2):
                for fg in range(4):
                    k = half * 4 + fg
                    buf = W2G0 + (k % 2) * 8192
                    for si in range(5):
                        ss = half * 5 + si
                        for dh in range(2):
                            bi = (si * 2 + dh) % 2
                            pb, bw = s.bank(bi, extra=[w2l[k]] + (h_done if k == 0 else []))
                            for j in range(8):
                                s.op("tensor", lambda te, pb=pb, fg=fg, j=j, ss=ss, dh=dh, buf=buf:
                                     te.matmul(pb[:, 0:512], s.arb(HS0 + (fg * 8 + j) * CAP + ss * 128, 128),
                                               s.arb(buf + j * 1024 + dh * 512, 512),
                                               start=(j == 0), stop=(j == 7)),
                                     waits=bw if j == 0 else (), inc=sT if j == 7 else None)
                            dst = eosb[:, si * 1024 + dh * 512:si * 1024 + dh * 512 + 512]
                            if fg == 0:
                                s.op("vector", lambda v, pb=pb, dst=dst: v.tensor_copy(dst, pb[:, 0:512]),
                                     waits=[sT.at()], inc=sV)
                            else:
                                s.op("vector", lambda v, pb=pb, dst=dst: v.tensor_add(dst, dst, pb[:, 0:512]),
                                     waits=[sT.at()], inc=sV)
                            s.bdone(bi, [sV.at()])
                    w2done[k] = sT.at()
                    if k + 2 <= 7:
                        w2_load(k + 2)
                for si in range(5):
                    ss = half * 5 + si
                    ec = s.arb(ECST + (si % 2) * 1024, 1024)
                    s.op("vector", lambda v, si=si, ec=ec: v.tensor_copy(ec, eosb[:, si * 1024:(si + 1) * 1024]),
                         waits=[sV.at(), sL.at()], inc=sV)
                    s.op("sync", lambda sy, el=el, ss=ss, ec=ec:
                         sy.dma_start(out=s.eo_in[el * CAP + ss * 128:el * CAP + (ss + 1) * 128, :], in_=ec),
                         waits=[sV.at()], inc=sL, inc_n=16)
            self.fence()
        zb = s.arb(GT0, 1024)
        s.op("vector", lambda v, zb=zb: v.memset(zb, 0.0), inc=sV)
        s.op("sync", lambda sy, zb=zb: sy.dma_start(out=s.eo_in[2 * CAP:EO_STRIDE, :], in_=zb[0:64, :]),
             waits=[sV.at()], inc=sL, inc_n=16)
        s.op("gpsimd", lambda g: g.collective_compute(
            "AllGather", ALU.bypass, replica_groups=GROUPS_ALL,
            ins=[s.eo_in.ap().opt()], outs=[s.eo_out.ap().opt()]), waits=[sL.at()], inc=sC)
        eo_ag = [sC.at()]
        for t in range(NT):
            gt = s.arb(GT0 + (t % 2) * 1024, 1024)
            s.op("gpsimd", lambda g, t=t, gt=gt: g.indirect_dma_start(
                out=gt, out_offset=None, in_=s.eo_out[:, :],
                in_offset=bass.IndirectOffsetOnAxis(ap=s.idx_eo[:, t:t + 1], axis=0)),
                waits=eo_ag + [sV.at()], inc=sG, inc_n=16)
            xt = s.xb[:, t * DM:(t + 1) * DM]
            s.op("vector", lambda v, gt=gt, xt=xt, t=t:
                 v.scalar_tensor_tensor(xt, gt, s.gk[:, t:t + 1], xt, ALU.mult, ALU.add),
                 waits=[sG.at()], inc=sV)
        self.fence()
        self.moe_calls += 1
        if self.dbg and self.moe_calls == 1:
            def dmp(dst, srcap):
                s.op("sync", lambda sy, dst=dst, srcap=srcap: sy.dma_start(out=dst, in_=srcap),
                     inc=sL, inc_n=16)
            dmp(s.dbg_dgi[:, :], s.dgi[:, :])
            dmp(s.dbg_ieo[:, :], s.idx_eo[:, :])
            dmp(s.dbg_isc[:, :], s.idx_sc[:, :])
            dmp(s.dbg_gk[:, :], s.gk[:, :])
            dmp(s.dbg_cnt[:, :], s.cntT[:, :])
            dmp(s.dbg_seg[:, :], s.segT[:, :])
            dmp(s.dbg_cmp[:, :], s.cmp_in.ap())
            dmp(s.dbg_cmpo[:, :], s.cmp_out.ap())
            dmp(s.dbg_eoin[:, :], s.eo_in.ap())
            self.fence()


    def attn_b(self):
        s = self
        sL, sG, sT, sV, sS, sC = s.sL, s.sG, s.sT, s.sV, s.sS, s.sC
        SC = s.SC0
        QF, KF, VF, VTM = SC, SC + 2304, SC + 4608, SC + 6912
        MB0 = SC + 9216
        SCB_F = SC + 15360
        EXPT_F = SC + 16128
        ATT0 = SC + 16896
        ATTT0 = SC + 17280
        IOT_F = SC + 17664
        KP_F = SC + 18432
        HXL = SC + 4096
        HXR = SC + 8192

        s.op("sync", lambda sy: sy.dma_start(out=s.halo_in[0:128, :], in_=s.xb[:, 0:DM]), inc=sL, inc_n=16)
        s.op("sync", lambda sy: sy.dma_start(out=s.halo_in[128:256, :], in_=s.xb[:, (NT - 1) * DM:NT * DM]),
             inc=sL, inc_n=16)
        s.op("gpsimd", lambda g: g.collective_compute(
            "AllGather", ALU.bypass, replica_groups=GROUPS_ALL,
            ins=[s.halo_in.ap().opt()], outs=[s.halo_out.ap().opt()]), waits=[sL.at()], inc=sC)
        halo_ag = [sC.at()]
        s.op("gpsimd", lambda g: g.indirect_dma_start(
            out=s.arf(HXL, DM), out_offset=None, in_=s.halo_out[:, :],
            in_offset=bass.IndirectOffsetOnAxis(ap=s.hl_idx[:, 0:1], axis=0)), waits=halo_ag, inc=sG, inc_n=16)
        s.op("gpsimd", lambda g: g.indirect_dma_start(
            out=s.arf(HXR, DM), out_offset=None, in_=s.halo_out[:, :],
            in_offset=bass.IndirectOffsetOnAxis(ap=s.hr_idx[:, 0:1], axis=0)), waits=halo_ag, inc=sG, inc_n=16)
        self.lnt(halo=True)

        iot = s.arf(IOT_F, 384)
        kp = s.arf(KP_F, 384)
        ma = s.arf(SCB_F, 384)
        mb_ = s.arf(EXPT_F, 384)
        s.op("gpsimd", lambda g: g.iota(iot, pattern=[[1, 384]], base=-128, channel_multiplier=-1,
                                        allow_small_or_imprecise_dtypes=True), inc=s.sP)
        iota_done = [s.sP.at()]
        for t in range(NT):
            s.op("gpsimd", lambda g, t=t: g.iota(kp, pattern=[[1, 384]], base=t * 128 - 128, channel_multiplier=0,
                                                 allow_small_or_imprecise_dtypes=True),
                 waits=[sV.at()], inc=s.sP)
            s.op("vector", lambda v: v.tensor_scalar(kp, kp, s.cst[:, s.C_P0:s.C_P0 + 1], None, ALU.add),
                 waits=[s.sP.at()], inc=sV)
            s.op("vector", lambda v: v.tensor_scalar(ma, iot, -128.0, None, ALU.is_ge),
                 waits=iota_done + [sV.at()], inc=sV)
            s.op("vector", lambda v: v.tensor_scalar(mb_, iot, 128.0, None, ALU.is_le), inc=sV)
            s.op("vector", lambda v: v.tensor_mul(ma, ma, mb_), waits=[sV.at()], inc=sV)
            s.op("vector", lambda v: v.tensor_scalar(mb_, kp, 0.0, None, ALU.is_ge), waits=[sV.at()], inc=sV)
            s.op("vector", lambda v: v.tensor_mul(ma, ma, mb_), waits=[sV.at()], inc=sV)
            s.op("vector", lambda v: v.tensor_scalar(mb_, kp, float(SEQ), None, ALU.is_lt), waits=[sV.at()], inc=sV)
            s.op("vector", lambda v: v.tensor_mul(ma, ma, mb_), waits=[sV.at()], inc=sV)
            s.op("vector", lambda v, t=t: v.tensor_scalar(s.arb(MB0 + t * 384, 384), ma, 1.0e30, -1.0e30,
                                                          ALU.mult, ALU.add), waits=[sV.at()], inc=sV)
        mb_done = [sV.at()]

        for hh in range(NDC):
            s.op("gpsimd", lambda g, hh=hh: g.dma_start(out=s.arb(s.WB0 + hh * 1024, 1024),
                                                        in_=s.wproj_b[hh * 128:(hh + 1) * 128, :]),
                 inc=sG, inc_n=16)
        wp_ld = [sG.at()]

        TT5 = ((0, 512), (512, 512), (1024, 512), (1536, 512), (2048, 256))
        for hh in range(NDC):
            s.op("gpsimd", lambda g, hh=hh: g.dma_start(out=s.arb(SC + 19200, 0) if False else
                                                        s.arb(s.WB0 + 8192, 3072), in_=s.wqkv_b[hh, :, :]),
                 waits=[sT.at()], inc=sG, inc_n=16)
            whh_ld = [sG.at()]
            for part, dst0 in ((0, QF), (1, KF), (2, VF)):
                for it, (c0, w) in enumerate(TT5):
                    bi = it % 2
                    pb, bw = s.bank(bi, extra=whh_ld + ([sV.at(), sS.at()] if part == 0 and it == 0 else []))
                    for dc in range(NDC):
                        s.op("tensor", lambda te, pb=pb, part=part, dc=dc, c0=c0, w=w:
                             te.matmul(pb[:, 0:w], s.arb(s.WB0 + 8192 + (part * 8 + dc) * 128, 128),
                                       s.arb(s.LNF0 + dc * FM + c0, w), start=(dc == 0), stop=(dc == NDC - 1)),
                             waits=bw if dc == 0 else (), inc=sT if dc == NDC - 1 else None)
                    s.op("scalar", lambda sc, pb=pb, dst0=dst0, c0=c0, w=w:
                         sc.copy(s.arb(dst0 + c0, w), pb[:, 0:w]), waits=[sT.at()], inc=sS)
                    s.bdone(bi, [sS.at()])
            qkv_done = [sS.at()]
            for j in range(18):
                bi = 6 + j % 2
                pb, bw = s.bank(bi, extra=qkv_done)
                s.op("tensor", lambda te, pb=pb, j=j:
                     te.transpose(pb[:, 0:64].bitcast(BF16), s.arb(VF + j * 128, 128), s.idt_b[:]),
                     waits=bw, inc=sT)
                s.op("scalar", lambda sc, pb=pb, j=j: sc.copy(s.arb(VTM + j * 128, 128),
                                                              pb[:, 0:64].bitcast(BF16)),
                     waits=[sT.at()], inc=sS)
                s.bdone(bi, [sS.at()])
            vtm_done = [sS.at()]
            SCB1, EXPT1, ATT1, ATTT1 = VF, VF + 768, VF + 1536, VF + 1920
            for t in range(NT):
                for head in range(2):
                    q2 = (t * 2 + head) % 2
                    po = head * 64
                    qcol = 128 + t * 128
                    pb3, bw3 = s.bank(3 if q2 == 0 else 5, extra=vtm_done + mb_done)
                    s.op("tensor", lambda te, pb3=pb3, po=po, qcol=qcol, t=t:
                         te.matmul(pb3[:, 0:384], s.arb(QF + qcol, 128)[po:po + 64, :],
                                   s.arb(KF + t * 128, 384)[po:po + 64, :], start=True, stop=True),
                         waits=bw3, inc=sT)
                    scmm = [sT.at()]
                    scb = s.arf(SCB_F if q2 == 0 else SCB1, 384)
                    s.op("vector", lambda v, pb3=pb3, t=t, scb=scb: v.tensor_add(scb, pb3[:, 0:384],
                                                                                s.arb(MB0 + t * 384, 384)),
                         waits=scmm, inc=sV)
                    s.bdone(3 if q2 == 0 else 5, [sV.at()])
                    cb = 60 if q2 == 0 else 56
                    mx = s.csm[:, cb:cb + 1]
                    nmx = s.csm[:, cb + 1:cb + 2]
                    sume = s.csm[:, cb + 2:cb + 3]
                    rc = s.csm[:, cb + 3:cb + 4]
                    s.op("vector", lambda v, mx=mx, scb=scb: v.reduce_max(mx, scb, axis=AX),
                         waits=[sV.at()], inc=sV)
                    s.op("vector", lambda v, mx=mx, nmx=nmx: v.tensor_scalar(nmx, mx, -0.125, None, ALU.mult),
                         waits=[sV.at()], inc=sV)
                    expt = s.arf(EXPT_F if q2 == 0 else EXPT1, 384)
                    s.op("scalar", lambda sc, nmx=nmx, expt=expt, scb=scb, sume=sume:
                         sc.activation(expt, scb, AF.Exp, bias=nmx, scale=0.125, accum_out=sume),
                         waits=[sV.at()], inc=sS)
                    s.op("vector", lambda v, sume=sume, rc=rc: v.reciprocal(rc, sume), waits=[sS.at()],
                         inc=sV)
                    att = s.arb(ATT0 if q2 == 0 else ATT1, 384)
                    s.op("vector", lambda v, att=att, expt=expt, rc=rc:
                         v.tensor_scalar(att, expt, rc, None, ALU.mult), waits=[sV.at()], inc=sV)
                    att_done = [sV.at()]
                    at0 = ATTT0 if q2 == 0 else ATTT1
                    for j3 in range(3):
                        bi = (6 if q2 == 0 else 7)
                        pb, bw = s.bank(bi, extra=att_done)
                        s.op("tensor", lambda te, pb=pb, att=att, j3=j3:
                             te.transpose(pb[:, 0:64].bitcast(BF16), att[:, j3 * 128:(j3 + 1) * 128], s.idt_b[:]),
                             waits=bw, inc=sT)
                        s.op("scalar", lambda sc, pb=pb, j3=j3, at0=at0: sc.copy(s.arb(at0 + j3 * 128, 128),
                                                                                pb[:, 0:64].bitcast(BF16)),
                             waits=[sT.at()], inc=sS)
                        s.bdone(bi, [sS.at()])
                    attT_done = [sS.at()]
                    pb4, bw4 = s.bank(4, extra=attT_done)
                    for j3 in range(3):
                        s.op("tensor", lambda te, pb4=pb4, j3=j3, t=t, po=po, head=head, at0=at0:
                             te.matmul(pb4[po:po + 64, 0:128], s.arb(VTM + (t + j3) * 128 + po, 64),
                                       s.arb(at0 + j3 * 128, 128),
                                       start=(j3 == 0), stop=(j3 == 2),
                                       tile_position=(0, po)),
                             waits=bw4 if j3 == 0 else (), inc=sT if j3 == 2 else None)
                    avmm = [sT.at()]
                    if head == 1:
                        s.op("scalar", lambda sc, pb4=pb4, hh=hh, t=t:
                             sc.copy(s.arb(s.LNT0 + hh * 2048 + t * 128, 128), pb4[:, 0:128]),
                             waits=avmm, inc=sS)
                        s.bdone(4, [sS.at()])
            self.fence()
        for tg in range(4):
            banks = []
            bw_all = [sV.at(), sS.at()]
            for fj in range(NDC):
                pb, bw = s.bank(fj, extra=bw_all)
                banks.append(pb)
                for hh in range(NDC):
                    s.op("tensor", lambda te, pb=pb, fj=fj, hh=hh, tg=tg:
                         te.matmul(pb[:, 0:512], s.arb(s.WB0 + hh * 1024 + fj * 128, 128),
                                   s.arb(s.LNT0 + hh * 2048 + tg * 512, 512), start=(hh == 0), stop=(hh == NDC - 1)),
                         waits=(bw + [s.sG.at()]) if hh == 0 else (), inc=sT if hh == NDC - 1 else None)
            pr_mm = [sT.at()]
            for fj in range(NDC):
                osb = s.arf(SCB_F, 512)
                s.op("scalar", lambda sc, fj=fj, osb=osb:
                     sc.activation(osb, banks[fj][:, 0:512], AF.Identity, bias=s.bp_b[:, fj:fj + 1]),
                     waits=pr_mm if fj == 0 else [sV.at()], inc=sS)
                ev = [sS.at()]
                for i in range(4):
                    t2 = tg * 4 + i
                    pb2, bw2 = s.bank(fj, extra=ev)
                    s.op("tensor", lambda te, pb2=pb2, osb=osb, i=i:
                         te.transpose(pb2[:, 0:128], osb[:, i * 128:(i + 1) * 128], s.idt_f[:]),
                         waits=bw2, inc=sT)
                    dst = s.xb[:, t2 * DM + fj * 128:t2 * DM + (fj + 1) * 128]
                    s.op("vector", lambda v, pb2=pb2, dst=dst: v.tensor_add(dst, dst, pb2[:, 0:128]),
                         waits=[sT.at()], inc=sV)
                    s.bdone(fj, [sV.at()])
        self.fence()

    def store(self):
        s = self
        for t in range(NT):
            s.op("sync", lambda sy, t=t: sy.dma_start(out=s.y_out[t * 128:(t + 1) * 128, :],
                                                      in_=s.xb[:, t * DM:(t + 1) * DM]),
                 inc=s.sL, inc_n=16)
        s.op("sync", lambda sy: None, waits=[s.sL.at()])

    def emit(self, block):
        @block.sync
        def _(sy):
            for f in self.ops["sync"]:
                f(sy)

        @block.tensor
        def _(te):
            for f in self.ops["tensor"]:
                f(te)

        @block.vector
        def _(v):
            for f in self.ops["vector"]:
                f(v)

        @block.scalar
        def _(sc):
            for f in self.ops["scalar"]:
                f(sc)

        @block.gpsimd
        def _(g):
            for f in self.ops["gpsimd"]:
                f(g)


def build(n_sub=4):
    nc = bass.Bass("TRN2", target_bir_lowering=False)
    mk = MK(nc, n_sub)
    with ExitStack() as ctx:
        mk.declare(ctx)
        block = ctx.enter_context(nc.Block())
        mk.p0()
        if n_sub >= 1:
            mk.attn_a()
        if n_sub >= 2:
            mk.moe()
        if n_sub >= 3:
            mk.attn_b()
        if n_sub >= 4:
            mk.moe()
        mk.store()
        mk.emit(block)
    return nc


def kernel(**inputs):
    maps = shard_inputs(inputs)
    nc = build(4)
    res = run_bass_kernel_spmd(nc, maps, core_ids=list(range(NCORE)))
    y = np.concatenate([r["y_out"] for r in res.results], axis=0)
    return y.reshape(2, SEQ, DM)


# version 05
# speedup vs baseline: 1.1785x; 1.0579x over previous
import numpy as np
from contextlib import ExitStack
import concourse.bass as bass
import concourse.mybir as mybir
from concourse.bass_utils import run_bass_kernel_spmd
import ml_dtypes

F32 = mybir.dt.float32
BF16 = mybir.dt.bfloat16
I32 = mybir.dt.int32
AF = mybir.ActivationFunctionType
ALU = mybir.AluOpType
AX = mybir.AxisListType.X

NCORE = 8
T = 2048
NT = 16
DM = 1024
NDC = 8
SEQ = 8192
CAP = 1280
NE = 16
DFF = 4096
NFC = 32
EO_STRIDE = 2 * CAP + 64
FM = T + 256
LN_EPS = 1e-5
ATT_EPS = 1e-4
GROUPS_ALL = [list(range(NCORE))]
GROUPS_BATCH = [[0, 1, 2, 3], [4, 5, 6, 7]]


def shard_inputs(inp):
    x = np.ascontiguousarray(np.asarray(inp["L0_attn_a_x"], dtype=np.float32).reshape(2 * SEQ, DM))
    w1 = np.asarray(inp["L0_ffn_w1"], dtype=np.float32)
    w2 = np.asarray(inp["L0_ffn_w2"], dtype=np.float32)
    w1r = np.ascontiguousarray(w1.reshape(NE, NDC, 128, NFC, 128).transpose(0, 3, 2, 1, 4)
                               .reshape(NE, NFC, 128, NDC * 128))
    gate_wT = np.ascontiguousarray(np.asarray(inp["L0_ffn_gate_w"], dtype=np.float32).T)
    wqkv_b_hm = np.ascontiguousarray(
        np.asarray(inp["L0_attn_b_w_qkv"], dtype=np.float32)
        .reshape(NDC, 128, 3, NDC, 128).transpose(3, 1, 2, 0, 4).reshape(NDC, 128, 3 * DM))
    maps = []
    for c in range(NCORE):
        coremask8 = np.zeros((16, 8), np.float32)
        coremask8[:, :c] = 1.0
        myexp = np.zeros((16, 2), np.float32)
        myexp[2 * c, 0] = 1.0
        myexp[2 * c + 1, 1] = 1.0
        li = np.arange(128, dtype=np.int32)
        hleft = ((c - 1) * 256 + 128 + li) if c % 4 > 0 else (c * 256 + li)
        hright = ((c + 1) * 256 + li) if c % 4 < 3 else (c * 256 + 128 + li)
        maps.append(dict(
            x_in=np.ascontiguousarray(x[c * T:(c + 1) * T]),
            wqkv_a=np.asarray(inp["L0_attn_a_W_qkv"], dtype=np.float32),
            bqkv_a=np.ascontiguousarray(np.asarray(inp["L0_attn_a_b_qkv"], dtype=np.float32).reshape(1, 3 * DM)),
            bq_a_in=np.ascontiguousarray(np.asarray(inp["L0_attn_a_b_qkv"], dtype=np.float32)[:DM].reshape(8, 128).T),
            wout_a=np.asarray(inp["L0_attn_a_W_out"], dtype=np.float32),
            bout_a=np.ascontiguousarray(np.asarray(inp["L0_attn_a_b_out"], dtype=np.float32).reshape(8, 128).T),
            wqkv_b=wqkv_b_hm,
            wproj_b=np.asarray(inp["L0_attn_b_w_proj"], dtype=np.float32),
            bproj_b=np.ascontiguousarray(np.asarray(inp["L0_attn_b_b_proj"], dtype=np.float32).reshape(8, 128).T),
            gate_wT=gate_wT,
            w1_m=np.ascontiguousarray(w1r[2 * c:2 * c + 2]),
            w2_m=np.ascontiguousarray(w2[2 * c:2 * c + 2]),
            coremask8=coremask8,
            myexp=myexp,
            p0col=np.full((128, 1), float((c % 4) * T), np.float32),
            hleft_idx=hleft.reshape(128, 1).astype(np.int32),
            hright_idx=hright.reshape(128, 1).astype(np.int32),
        ))
    return maps


class Sem:
    def __init__(self, h):
        self.h = h
        self.v = 0

    def at(self):
        return (self, self.v)


class MK:

    def __init__(self, nc, n_sub=4):
        self.nc = nc
        self.n_sub = n_sub
        self.ops = {k: [] for k in ("sync", "tensor", "vector", "scalar", "gpsimd")}
        self.bfree = [[] for _ in range(8)]

    def op(self, engine, fn, waits=(), inc=None, inc_n=1):
        if inc is not None:
            inc.v += inc_n
        ws = tuple((s.h, v) for (s, v) in waits if v > 0)

        def run(eng, fn=fn, ws=ws, inc=inc, inc_n=inc_n):
            for h, v in ws:
                eng.wait_ge(h, v)
            ins = fn(eng)
            if inc is not None:
                ins.then_inc(inc.h, inc_n)
        self.ops[engine].append(run)

    def snap(self):
        return [s.at() for s in self.allsems]

    def fence(self):
        sn = self.snap()
        for e in ("sync", "tensor", "vector", "scalar", "gpsimd"):
            self.op(e, lambda eng: None, waits=sn)
        for i in range(8):
            self.bfree[i] = []

    def arb(self, c0, n):
        assert c0 >= 0 and c0 + n <= 70656, (c0, n)
        return self.AR[:, c0:c0 + n]

    def arf(self, c0, n):
        assert c0 % 2 == 0
        return self.arb(c0, 2 * n).bitcast(F32)

    def declare(self, ctx):
        nc = self.nc
        dt = nc.dram_tensor
        self.x_in = dt("x_in", [T, DM], F32, kind="ExternalInput")
        self.wqkv_a = dt("wqkv_a", [DM, 3 * DM], F32, kind="ExternalInput")
        self.bqkv_a = dt("bqkv_a", [1, 3 * DM], F32, kind="ExternalInput")
        self.bq_a_in = dt("bq_a_in", [128, 8], F32, kind="ExternalInput")
        self.wout_a = dt("wout_a", [DM, DM], F32, kind="ExternalInput")
        self.bout_a = dt("bout_a", [128, 8], F32, kind="ExternalInput")
        self.wqkv_b = dt("wqkv_b", [NDC, 128, 3 * DM], F32, kind="ExternalInput")
        self.wproj_b = dt("wproj_b", [DM, DM], F32, kind="ExternalInput")
        self.bproj_b = dt("bproj_b", [128, 8], F32, kind="ExternalInput")
        self.gate_wT_d = dt("gate_wT", [DM, NE], F32, kind="ExternalInput")
        self.w1_m = dt("w1_m", [2, NFC, 128, NDC * 128], F32, kind="ExternalInput")
        self.w2_m = dt("w2_m", [2, DFF, DM], F32, kind="ExternalInput")
        self.coremask8_d = dt("coremask8", [16, 8], F32, kind="ExternalInput")
        self.myexp_d = dt("myexp", [16, 2], F32, kind="ExternalInput")
        self.p0col_d = dt("p0col", [128, 1], F32, kind="ExternalInput")
        self.hleft_d = dt("hleft_idx", [128, 1], I32, kind="ExternalInput")
        self.hright_d = dt("hright_idx", [128, 1], I32, kind="ExternalInput")
        self.y_out = dt("y_out", [T, DM], F32, kind="ExternalOutput")
        import os
        self.dbg = bool(os.environ.get("MKDEBUG"))
        self.moe_calls = 0
        if self.dbg:
            self.dbg_dgi = dt("dbg_dgi", [128, 20], I32, kind="ExternalOutput")
            self.dbg_ieo = dt("dbg_ieo", [128, NT], I32, kind="ExternalOutput")
            self.dbg_isc = dt("dbg_isc", [128, NT], I32, kind="ExternalOutput")
            self.dbg_gk = dt("dbg_gk", [128, NT], F32, kind="ExternalOutput")
            self.dbg_cnt = dt("dbg_cnt", [16, 8], F32, kind="ExternalOutput")
            self.dbg_seg = dt("dbg_seg", [16, 8], F32, kind="ExternalOutput")
            self.dbg_cmp = dt("dbg_cmp", [T, DM], BF16, kind="ExternalOutput")
            self.dbg_cmpo = dt("dbg_cmpo", [NCORE * T, DM], BF16, kind="ExternalOutput")
            self.dbg_eoin = dt("dbg_eoin", [EO_STRIDE, DM], BF16, kind="ExternalOutput")
            self.dbg_disp = dt("dbg_disp", [128, NDC * CAP], BF16, kind="ExternalOutput")
            self.dbg_h = dt("dbg_h", [128, NFC * CAP], BF16, kind="ExternalOutput")
        self.kv_ar_in = dt("kv_ar_in", [128, 2080], F32)
        self.kv_ar_out = dt("kv_ar_out", [128, 2080], F32)
        self.cnt_in = dt("cnt_in", [2, 16], F32)
        self.cnt_out = dt("cnt_out", [16, 16], F32)
        self.cmp_in = dt("cmp_in", [T, DM], BF16)
        self.cmp_out = dt("cmp_out", [NCORE * T, DM], BF16)
        self.eo_in = dt("eo_in", [EO_STRIDE, DM], BF16)
        self.eo_out = dt("eo_out", [NCORE * EO_STRIDE, DM], BF16)
        self.halo_in = dt("halo_in", [256, DM], F32)
        self.halo_out = dt("halo_out", [NCORE * 256, DM], F32)
        it = nc.inline_tensor
        self.c_idf = it(np.eye(128, dtype=np.float32), name="c_idf")
        self.c_idb = it(np.eye(128, dtype=np.float32).astype(ml_dtypes.bfloat16), name="c_idb")
        self.c_1b = it(np.ones((128, 1), dtype=ml_dtypes.bfloat16), name="c_1b")
        self.c_1f = it(np.ones((128, 1), dtype=np.float32), name="c_1f")
        self.c_1r = it(np.ones((1, 128), dtype=np.float32), name="c_1r")
        base_e = np.array([(e // 2) * EO_STRIDE + (e % 2) * CAP for e in range(NE)], np.float32)
        trash_e = np.array([(e // 2) * EO_STRIDE + 2 * CAP for e in range(NE)], np.float32)
        self.c_base = it(np.broadcast_to(base_e, (128, NE)).copy(), name="c_base")
        self.c_trash = it(np.broadcast_to(trash_e, (128, NE)).copy(), name="c_trash")
        self.c_io16 = it(np.broadcast_to(np.arange(16, dtype=np.float32), (128, 16)).copy(), name="c_io16")
        self.c_io8 = it(np.broadcast_to(np.arange(8, dtype=np.float32), (128, 8)).copy(), name="c_io8")
        b2 = np.zeros((2, 128), np.float32)
        b2[0, :64] = 1.0
        b2[1, 64:] = 1.0
        self.c_b2 = it(b2.astype(ml_dtypes.bfloat16), name="c_b2")

        en = ctx.enter_context
        sbt = lambda name, shape, d=F32: en(nc.sbuf_tensor(name, shape, d))
        self.xb = sbt("xb", [128, NT * DM])
        self.AR = sbt("AR", [128, 70656], BF16)
        self.csm = sbt("csm", [128, 64])
        self.idt_f = sbt("idt_f", [128, 128])
        self.idt_b = sbt("idt_b", [128, 128], BF16)
        self.ones_b = sbt("ones_b", [128, 1], BF16)
        self.ones_f = sbt("ones_f", [128, 1])
        self.ones1r = sbt("ones1r", [1, 128])
        self.b2h = sbt("b2h", [2, 128], BF16)
        self.cst = sbt("cst", [128, 64])
        self.cmask8 = sbt("cmask8", [16, 8])
        self.myexp_s = sbt("myexp_s", [16, 2])
        self.cntT = sbt("cntT", [16, 8])
        self.segT = sbt("segT", [16, 8])
        self.cumT = sbt("cumT", [16, 8])
        self.zrow16 = sbt("zrow16", [16, 8])
        self.gwT = sbt("gwT", [128, NDC * NE], BF16)
        self.masks = sbt("masks", [128, NT * NE])
        self.gcol = sbt("gcol", [128, NT])
        self.idx_sc = sbt("idx_sc", [128, NT], I32)
        self.idx_eo = sbt("idx_eo", [128, NT], I32)
        self.gk = sbt("gk", [128, NT])
        self.hl_idx = sbt("hl_idx", [128, 1], I32)
        self.hr_idx = sbt("hr_idx", [128, 1], I32)
        self.bq_a = sbt("bq_a", [128, 8])
        self.bo_a = sbt("bo_a", [128, 8])
        self.bp_b = sbt("bp_b", [128, 8])
        self.dgi = sbt("dgi", [128, 20], I32)
        self.psum = [en(nc.psum_tensor(f"pb{i}", [128, 512], F32)) for i in range(8)]
        self.allsems = []
        for nm in ("sL", "sG", "sT", "sV", "sS", "sC", "sP"):
            s = Sem(en(nc.semaphore(nm)))
            setattr(self, nm, s)
            self.allsems.append(s)
        self.C_EPSLN, self.C_P0, self.C_HPI = 0, 2, 3
        self.C_BASEE, self.C_TRASHE, self.C_IOTA16, self.C_IOTA8 = 8, 24, 40, 56
        self.LNT0 = 0
        self.LNF0 = 16384
        self.WB0 = 34816
        self.SC0 = 51200
        self.HS0 = 0
        self.W2G0 = 40960
        self.EOS0 = 49152

    def bank(self, i, extra=()):
        w = list(self.bfree[i]) + list(extra)
        self.bfree[i] = []
        return self.psum[i], w

    def bdone(self, i, waits):
        self.bfree[i] = list(waits)

    def p0(self):
        s = self
        sL, sG, sV = s.sL, s.sG, s.sV

        def d(dst, src):
            s.op("sync", lambda sy, dst=dst, src=src: sy.dma_start(out=dst, in_=src), inc=sL, inc_n=16)
        d(s.idt_f[:], s.c_idf[:, :])
        d(s.idt_b[:], s.c_idb[:, :])
        d(s.ones_b[:], s.c_1b[:, :])
        d(s.ones_f[:], s.c_1f[:, :])
        d(s.ones1r[:], s.c_1r[:, :])
        d(s.b2h[:], s.c_b2[:, :])
        d(s.cst[:, s.C_P0:s.C_P0 + 1], s.p0col_d[:, :])
        d(s.cst[:, s.C_BASEE:s.C_BASEE + 16], s.c_base[:, :])
        d(s.cst[:, s.C_TRASHE:s.C_TRASHE + 16], s.c_trash[:, :])
        d(s.cst[:, s.C_IOTA16:s.C_IOTA16 + 16], s.c_io16[:, :])
        d(s.cst[:, s.C_IOTA8:s.C_IOTA8 + 8], s.c_io8[:, :])
        d(s.cmask8[:], s.coremask8_d[:, :])
        d(s.myexp_s[:], s.myexp_d[:, :])
        d(s.hl_idx[:], s.hleft_d[:, :])
        d(s.hr_idx[:], s.hright_d[:, :])
        d(s.bq_a[:], s.bq_a_in[:, :])
        d(s.bo_a[:], s.bout_a[:, :])
        d(s.bp_b[:], s.bproj_b[:, :])
        for t in range(NT):
            d(s.xb[:, t * DM:(t + 1) * DM], s.x_in[t * 128:(t + 1) * 128, :])
        for dc in range(NDC):
            s.op("gpsimd", lambda g, dc=dc: g.dma_start(out=s.gwT[:, dc * NE:(dc + 1) * NE],
                                                        in_=s.gate_wT_d[dc * 128:(dc + 1) * 128, :]),
                 inc=sG, inc_n=16)
        s.op("vector", lambda v: v.memset(s.cst[:, s.C_EPSLN:s.C_EPSLN + 1], LN_EPS), inc=sV)
        s.op("vector", lambda v: v.memset(s.cst[:, s.C_HPI:s.C_HPI + 1], float(np.pi / 2)), inc=sV)
        s.op("vector", lambda v: v.memset(s.zrow16[:], 0.0), inc=sV)
        self.fence()

    def lnt(self, halo=False, tile_waits=None):
        s = self
        sV, sS, sT = s.sV, s.sS, s.sT
        HXL = s.SC0 + 4096
        HXR = s.SC0 + 8192
        self.lnf_done = {}
        tiles = [("own", t) for t in range(NT)]
        if halo:
            tiles += [("hl", 0), ("hr", 0)]
        for kind, t in tiles:
            cc = [s.csm[:, i:i + 1] for i in range(7)]
            c_sum, c_mean, c_m2, c_var, c_sd, c_rstd, c_sumsq = cc
            if kind == "own":
                xt = s.xb[:, t * DM:(t + 1) * DM]
                tmdst = s.arb(s.LNT0 + t * DM, DM)
                fmcol = 128 + t * 128
            else:
                xt = s.arf(HXL if kind == "hl" else HXR, DM)
                tmdst = s.arb(s.SC0 + 6912 + (0 if kind == "hl" else DM), DM)
                fmcol = 0 if kind == "hl" else (T + 128)
            trash = s.arf(s.SC0, 1024)
            if tile_waits is not None and kind == "own":
                sn = list(tile_waits[t])
            else:
                sn = [sV.at(), sS.at(), s.sL.at(), s.sG.at()]
            s.op("vector", lambda v, xt=xt, c=c_sum: v.reduce_sum(c, xt, axis=AX), waits=sn, inc=sV)
            s.op("scalar", lambda sc, xt=xt, o=trash, a=c_sumsq: sc.activation(o, xt, AF.Square, accum_out=a),
                 waits=sn, inc=sS)
            s.op("vector", lambda v, i=c_sum, o=c_mean: v.tensor_scalar(o, i, 1.0 / DM, None, ALU.mult),
                 waits=[sV.at()], inc=sV)
            s.op("vector", lambda v, i=c_mean, o=c_m2: v.tensor_mul(o, i, i), waits=[sV.at()], inc=sV)
            s.op("vector", lambda v, i=c_sumsq, o=c_var: v.tensor_scalar(o, i, 1.0 / DM, None, ALU.mult),
                 waits=[sS.at()], inc=sV)
            s.op("vector", lambda v, o=c_var, a=c_m2: v.tensor_sub(o, o, a), waits=[sV.at()], inc=sV)
            s.op("scalar", lambda sc, i=c_var, o=c_sd:
                 sc.activation(o, i, AF.Sqrt, bias=s.cst[:, s.C_EPSLN:s.C_EPSLN + 1]),
                 waits=[sV.at()], inc=sS)
            s.op("vector", lambda v, i=c_sd, o=c_rstd: v.reciprocal(o, i), waits=[sS.at()], inc=sV)
            s.op("vector", lambda v, xt=xt, m=c_mean, r=c_rstd, o=tmdst:
                 v.tensor_scalar(o, xt, m, r, ALU.subtract, ALU.mult), waits=[sV.at()], inc=sV)
            lnt_done = sV.at()
            for dc in range(NDC):
                bi = 6 + (dc % 2)
                pb, bw = s.bank(bi, extra=[lnt_done])
                s.op("tensor", lambda te, pb=pb, src=tmdst[:, dc * 128:(dc + 1) * 128]:
                     te.transpose(pb[:, 0:64].bitcast(BF16), src, s.idt_b[:]), waits=bw, inc=sT)
                s.op("scalar", lambda sc, pb=pb, dst=s.arb(s.LNF0 + dc * FM + fmcol, 128):
                     sc.copy(dst, pb[:, 0:64].bitcast(BF16)), waits=[sT.at()], inc=sS)
                s.bdone(bi, [sS.at()])
            self.lnf_done[(kind, t)] = [sS.at(), sV.at()]
        self.lnf_done_all = [sS.at(), sV.at(), sT.at()]

    def attn_a(self):
        s = self
        sL, sG, sT, sV, sS, sC = s.sL, s.sG, s.sT, s.sV, s.sS, s.sC
        SC = s.SC0
        COS0, SIN0 = SC, SC + 2048
        KC0, KS0, VT0 = SC + 4096, SC + 5120, SC + 6144
        KT_F = SC + 7168
        KVS_F = SC + 9216
        POSR_F = SC + 13376
        self.lnt(tile_waits=getattr(self, "resid_done", None))

        SCALE = float(np.pi / 2 / SEQ)
        posr = s.arf(POSR_F, 2048)
        s.op("gpsimd", lambda g: g.iota(posr, pattern=[[1, 2048]], base=0, channel_multiplier=0,
                                        allow_small_or_imprecise_dtypes=True), inc=s.sP)
        s.op("vector", lambda v: v.tensor_scalar(posr, posr, s.cst[:, s.C_P0:s.C_P0 + 1], None, ALU.add),
             waits=[s.sP.at()], inc=sV)
        s.op("scalar", lambda sc: sc.activation(s.arb(SIN0, 2048), posr, AF.Sin, scale=SCALE),
             waits=[sV.at()], inc=sS)
        s.op("scalar", lambda sc: sc.activation(s.arb(COS0, 2048), posr, AF.Sin, scale=-SCALE,
                                                bias=s.cst[:, s.C_HPI:s.C_HPI + 1]),
             waits=[sV.at()], inc=sS)
        pc = s.csm[:, 8:24]
        cc = s.csm[:, 24:40]
        scc = s.csm[:, 40:56]
        s.op("gpsimd", lambda g: g.iota(pc, pattern=[[128, 16]], base=0, channel_multiplier=1,
                                        allow_small_or_imprecise_dtypes=True), inc=s.sP)
        s.op("vector", lambda v: v.tensor_scalar(pc, pc, s.cst[:, s.C_P0:s.C_P0 + 1], None, ALU.add),
             waits=[s.sP.at()], inc=sV)
        s.op("scalar", lambda sc: sc.activation(scc, pc, AF.Sin, scale=SCALE), waits=[sV.at()], inc=sS)
        s.op("scalar", lambda sc: sc.activation(cc, pc, AF.Sin, scale=-SCALE,
                                                bias=s.cst[:, s.C_HPI:s.C_HPI + 1]), waits=[sV.at()], inc=sS)
        trig_done = [sS.at()]
        bkv = s.arf(POSR_F, 2048)
        s.op("sync", lambda sy: sy.dma_start(out=bkv[:, 0:1024],
                                             in_=s.bqkv_a.ap()[0:1, DM:2 * DM].partition_broadcast(128)),
             waits=trig_done, inc=sL, inc_n=16)
        s.op("sync", lambda sy: sy.dma_start(out=bkv[:, 1024:2048],
                                             in_=s.bqkv_a.ap()[0:1, 2 * DM:3 * DM].partition_broadcast(128)),
             waits=trig_done, inc=sL, inc_n=16)
        bkv_ld = [sL.at()]

        for dc in range(NDC):
            s.op("gpsimd", lambda g, dc=dc: g.dma_start(out=s.arb(s.WB0 + dc * 2048, 1024),
                                                        in_=s.wqkv_a[dc * 128:(dc + 1) * 128, DM:2 * DM]),
                 inc=sG, inc_n=16)
            s.op("gpsimd", lambda g, dc=dc: g.dma_start(out=s.arb(s.WB0 + dc * 2048 + 1024, 1024),
                                                        in_=s.wqkv_a[dc * 128:(dc + 1) * 128, 2 * DM:3 * DM]),
                 inc=sG, inc_n=16)
        wkv_ld = [sG.at()]

        kvs = s.arf(KVS_F, 2080)
        s.op("vector", lambda v: v.memset(kvs, 0.0), inc=sV)
        for t in range(NT):
            mm_w = wkv_ld + trig_done + s.lnf_done[("own", t)]
            for j, wcol in enumerate((0, 512, 1024, 1536)):
                pb, bw = s.bank(j, extra=mm_w)
                for dc in range(NDC):
                    ap = s.arb(s.LNF0 + dc * FM + 128 + t * 128, 128)
                    s.op("tensor", lambda te, pb=pb, ap=ap, dc=dc, wcol=wcol:
                         te.matmul(pb[:, 0:512], ap, s.arb(s.WB0 + dc * 2048 + wcol, 512),
                                   start=(dc == 0), stop=(dc == NDC - 1)),
                         waits=bw if dc == 0 else (), inc=sT if dc == NDC - 1 else None)
            mm_done = [sT.at()]
            kt = s.arf(KT_F, 1024)
            kc = s.arb(KC0, 1024)
            ks = s.arb(KS0, 1024)
            vt = s.arb(VT0, 1024)
            s.op("vector", lambda v, kt=kt: v.tensor_add(kt[:, 0:512], s.psum[0][:, 0:512], bkv[:, 0:512]),
                 waits=mm_done + bkv_ld, inc=sV)
            s.op("vector", lambda v, kt=kt: v.tensor_add(kt[:, 512:1024], s.psum[1][:, 0:512],
                                                         bkv[:, 512:1024]), inc=sV)
            s.op("scalar", lambda sc, kt=kt: sc.activation(kt, kt, AF.Relu), waits=[sV.at()], inc=sS)
            s.op("vector", lambda v, kt=kt, kc=kc, t=t: v.tensor_scalar(kc, kt, cc[:, t:t + 1], None, ALU.mult),
                 waits=[sS.at()], inc=sV)
            s.op("vector", lambda v, kt=kt, ks=ks, t=t: v.tensor_scalar(ks, kt, scc[:, t:t + 1], None, ALU.mult),
                 inc=sV)
            s.op("vector", lambda v, vt=vt: v.tensor_add(vt[:, 0:512], s.psum[2][:, 0:512],
                                                         bkv[:, 1024:1536]), waits=mm_done, inc=sV)
            s.op("vector", lambda v, vt=vt: v.tensor_add(vt[:, 512:1024], s.psum[3][:, 0:512],
                                                         bkv[:, 1536:2048]), inc=sV)
            kcv = [sV.at()]
            for j in range(4):
                s.bdone(j, kcv)
            for hh in range(NDC):
                bi = 4 + (hh % 2)
                pb, bw = s.bank(bi, extra=kcv)
                kcs_ = kc[:, hh * 128:(hh + 1) * 128]
                kss_ = ks[:, hh * 128:(hh + 1) * 128]
                vs_ = vt[:, hh * 128:(hh + 1) * 128]
                s.op("tensor", lambda te, pb=pb, a=kcs_, b=vs_:
                     te.matmul(pb[:, 0:128], a, b, start=True, stop=False), waits=bw)
                s.op("tensor", lambda te, pb=pb, a=kcs_:
                     te.matmul(pb[:, 128:129], a, s.ones_b[:], start=False, stop=False))
                s.op("tensor", lambda te, pb=pb, a=kss_, b=vs_:
                     te.matmul(pb[:, 132:260], a, b, start=False, stop=False))
                s.op("tensor", lambda te, pb=pb, a=kss_:
                     te.matmul(pb[:, 260:261], a, s.ones_b[:], start=False, stop=True), inc=sT)
                kvmm = [sT.at()]
                dstc = kvs[:, hh * 130:hh * 130 + 129]
                dsts = kvs[:, 1040 + hh * 130:1040 + hh * 130 + 129]
                if t == 0:
                    s.op("vector", lambda v, pb=pb, d=dstc: v.tensor_copy(d, pb[:, 0:129]), waits=kvmm, inc=sV)
                    s.op("vector", lambda v, pb=pb, d=dsts: v.tensor_copy(d, pb[:, 132:261]), inc=sV)
                else:
                    s.op("vector", lambda v, pb=pb, d=dstc: v.tensor_add(d, d, pb[:, 0:129]), waits=kvmm, inc=sV)
                    s.op("vector", lambda v, pb=pb, d=dsts: v.tensor_add(d, d, pb[:, 132:261]), inc=sV)
                s.bdone(bi, [sV.at()])

        s.op("sync", lambda sy: sy.dma_start(out=s.kv_ar_in[:, :], in_=kvs), waits=[sV.at()], inc=sL, inc_n=16)
        s.op("gpsimd", lambda g: g.collective_compute(
            "AllReduce", ALU.add, replica_groups=GROUPS_BATCH,
            ins=[s.kv_ar_in.ap().opt()], outs=[s.kv_ar_out.ap().opt()]), waits=[sL.at()], inc=sC)
        s.op("sync", lambda sy: sy.dma_start(out=kvs, in_=s.kv_ar_out[:, :]), waits=[sC.at()], inc=sL, inc_n=16)
        self.fence()

        for dc in range(NDC):
            s.op("gpsimd", lambda g, dc=dc: g.dma_start(out=s.arb(s.WB0 + dc * 1024, 1024),
                                                        in_=s.wqkv_a[dc * 128:(dc + 1) * 128, 0:DM]),
                 inc=sG, inc_n=16)
            s.op("gpsimd", lambda g, dc=dc: g.dma_start(out=s.arb(s.WB0 + 8192 + dc * 1024, 1024),
                                                        in_=s.wout_a[dc * 128:(dc + 1) * 128, :]),
                 inc=sG, inc_n=16)
        w_ld = [sG.at()]

        SBD0 = SC + 4096
        NBD0 = SC + 6144
        CTXN0 = SC + 6400
        QR_F = SC + 10496
        QC0, QS0 = SC + 11520, SC + 12032
        RNB0, BCB0 = SC + 12544, SC + 13056
        OSB_F = SC + 13568
        NSB_F = SC + 14592
        for hh in range(NDC):
            for which, srcoff in ((0, 0), (1, 1040)):
                d_ = s.arb(SBD0 + hh * 256 + which * 128, 128)
                s.op("vector", lambda v, d=d_: v.memset(d, 0.0), inc=sV)
                s.op("vector", lambda v, d=d_, so=srcoff, hh=hh:
                     v.tensor_copy(d[0:64, 0:64], kvs[0:64, so + hh * 130:so + hh * 130 + 64]),
                     waits=[sV.at()], inc=sV)
                s.op("vector", lambda v, d=d_, so=srcoff, hh=hh:
                     v.tensor_copy(d[64:128, 64:128], kvs[64:128, so + hh * 130 + 64:so + hh * 130 + 128]),
                     waits=[sV.at()], inc=sV)
                n_ = s.arb(NBD0 + hh * 4 + which * 2, 2)
                s.op("vector", lambda v, d=n_: v.memset(d, 0.0), inc=sV)
                s.op("vector", lambda v, d=n_, so=srcoff, hh=hh:
                     v.tensor_copy(d[0:64, 0:1], kvs[0:64, so + hh * 130 + 128:so + hh * 130 + 129]),
                     waits=[sV.at()], inc=sV)
                s.op("vector", lambda v, d=n_, so=srcoff, hh=hh:
                     v.tensor_copy(d[64:128, 1:2], kvs[64:128, so + hh * 130 + 128:so + hh * 130 + 129]),
                     waits=[sV.at()], inc=sV)
        sbd_done = [sV.at()]

        for tg in range(4):
            for hh in range(NDC):
                pbq, bwq = s.bank(0, extra=w_ld + sbd_done + s.lnf_done_all)
                for dc in range(NDC):
                    s.op("tensor", lambda te, pbq=pbq, dc=dc, hh=hh, tg=tg:
                         te.matmul(pbq[:, 0:512], s.arb(s.WB0 + dc * 1024 + hh * 128, 128),
                                   s.arb(s.LNF0 + dc * FM + 128 + tg * 512, 512),
                                   start=(dc == 0), stop=(dc == NDC - 1)),
                         waits=bwq if dc == 0 else (), inc=sT if dc == NDC - 1 else None)
                qr = s.arf(QR_F, 512)
                s.op("scalar", lambda sc, pbq=pbq, qr=qr, hh=hh:
                     sc.activation(qr, pbq[:, 0:512], AF.Relu, bias=s.bq_a[:, hh:hh + 1]),
                     waits=[sT.at()], inc=sS)
                s.bdone(0, [sS.at()])
                qc = s.arb(QC0, 512)
                qs = s.arb(QS0, 512)
                s.op("vector", lambda v, qr=qr, qc=qc, tg=tg: v.tensor_mul(qc, qr, s.arb(COS0 + tg * 512, 512)),
                     waits=[sS.at()], inc=sV)
                s.op("vector", lambda v, qr=qr, qs=qs, tg=tg: v.tensor_mul(qs, qr, s.arb(SIN0 + tg * 512, 512)),
                     inc=sV)
                qcs = [sV.at()]
                pbc, bwc = s.bank(1, extra=qcs)
                s.op("tensor", lambda te, pbc=pbc, hh=hh, qc=qc:
                     te.matmul(pbc[:, 0:512], s.arb(SBD0 + hh * 256, 128), qc, start=True, stop=False),
                     waits=bwc)
                s.op("tensor", lambda te, pbc=pbc, hh=hh, qs=qs:
                     te.matmul(pbc[:, 0:512], s.arb(SBD0 + hh * 256 + 128, 128), qs, start=False, stop=True),
                     inc=sT)
                pbn, bwn = s.bank(2, extra=qcs)
                s.op("tensor", lambda te, pbn=pbn, hh=hh, qc=qc:
                     te.matmul(pbn[0:2, 0:512], s.arb(NBD0 + hh * 4, 2), qc, start=True, stop=False),
                     waits=bwn)
                s.op("tensor", lambda te, pbn=pbn, hh=hh, qs=qs:
                     te.matmul(pbn[0:2, 0:512], s.arb(NBD0 + hh * 4 + 2, 2), qs, start=False, stop=True),
                     inc=sT)
                ctxmm = [sT.at()]
                nsb = s.arf(NSB_F, 512)
                s.op("vector", lambda v, pbn=pbn, nsb=nsb:
                     v.tensor_scalar(nsb[0:2, :], pbn[0:2, 0:512], ATT_EPS, None, ALU.add),
                     waits=ctxmm, inc=sV)
                s.bdone(2, [sV.at()])
                s.op("vector", lambda v, nsb=nsb: v.reciprocal(nsb[0:2, :], nsb[0:2, :]), waits=[sV.at()], inc=sV)
                rnb = s.arb(RNB0, 512)
                s.op("vector", lambda v, nsb=nsb, rnb=rnb: v.tensor_copy(rnb[0:2, :], nsb[0:2, :]),
                     waits=[sV.at()], inc=sV)
                pbb, bwb = s.bank(3, extra=[sV.at()])
                s.op("tensor", lambda te, pbb=pbb, rnb=rnb:
                     te.matmul(pbb[:, 0:512], s.b2h[:, :], rnb[0:2, :], start=True, stop=True),
                     waits=bwb, inc=sT)
                bcb = s.arb(BCB0, 512)
                s.op("scalar", lambda sc, pbb=pbb, bcb=bcb: sc.copy(bcb, pbb[:, 0:512]), waits=[sT.at()], inc=sS)
                s.bdone(3, [sS.at()])
                s.op("vector", lambda v, pbc=pbc, hh=hh, bcb=bcb:
                     v.tensor_mul(s.arb(CTXN0 + hh * 512, 512), pbc[:, 0:512], bcb),
                     waits=[sS.at()], inc=sV)
                s.bdone(1, [sV.at()])
            ctxn_done = [sV.at()]
            banks = []
            for fj in range(NDC):
                pb, bw = s.bank(fj, extra=ctxn_done)
                banks.append(pb)
                for hh in range(NDC):
                    s.op("tensor", lambda te, pb=pb, fj=fj, hh=hh:
                         te.matmul(pb[:, 0:512], s.arb(s.WB0 + 8192 + hh * 1024 + fj * 128, 128),
                                   s.arb(CTXN0 + hh * 512, 512), start=(hh == 0), stop=(hh == NDC - 1)),
                         waits=bw if hh == 0 else (), inc=sT if hh == NDC - 1 else None)
            wout_mm = [sT.at()]
            for fj in range(NDC):
                osb = s.arf(OSB_F, 512)
                s.op("scalar", lambda sc, fj=fj, osb=osb:
                     sc.activation(osb, banks[fj][:, 0:512], AF.Identity, bias=s.bo_a[:, fj:fj + 1]),
                     waits=wout_mm if fj == 0 else [sV.at()], inc=sS)
                ev = [sS.at()]
                for i in range(4):
                    t2 = tg * 4 + i
                    pb2, bw2 = s.bank(fj, extra=ev)
                    s.op("tensor", lambda te, pb2=pb2, osb=osb, i=i:
                         te.transpose(pb2[:, 0:128], osb[:, i * 128:(i + 1) * 128], s.idt_f[:]),
                         waits=bw2, inc=sT)
                    dst = s.xb[:, t2 * DM + fj * 128:t2 * DM + (fj + 1) * 128]
                    s.op("vector", lambda v, pb2=pb2, dst=dst: v.tensor_add(dst, dst, pb2[:, 0:128]),
                         waits=[sT.at()], inc=sV)
                    s.bdone(fj, [sV.at()])
                    if fj == NDC - 1:
                        self.resid_done[t2] = [sV.at()]



    def moe(self):
        s = self
        sL, sG, sT, sV, sS, sC = s.sL, s.sG, s.sT, s.sV, s.sS, s.sC
        SC = s.SC0
        self.lnt(tile_waits=self.resid_done)
        masket = s.arf(SC, 2048)
        cumet = s.arf(SC + 4096, 2048)
        zros = s.arf(SC + 8192, 2048)
        GT0 = SC + 12288
        W1B = SC + 4096
        ECST = SC + 16384
        Z = SC + 17408
        tmp16 = s.arf(Z, 16)
        rcol = s.arf(Z + 32, 16)
        crow = s.arf(Z + 64, 16)
        segi = s.arf(Z + 96, 16)
        segr = s.arf(Z + 128, 16)
        expt = s.arf(Z + 160, 16)
        sume = s.arf(Z + 192, 2)
        mxc = s.arf(Z + 196, 2)
        lrt = s.arf(Z + 200, 16)
        prefb = s.arf(Z + 232, 16)
        c1 = s.arf(Z + 264, 1)
        c2 = s.arf(Z + 266, 1)
        c3 = s.arf(Z + 268, 1)
        c4 = s.arf(Z + 270, 1)
        c5 = s.arf(Z + 272, 1)
        mcum = s.arf(Z + 274, 16)
        mseg = s.arf(Z + 306, 16)
        cumb = s.arf(Z + 338, 8)
        segb = s.arf(Z + 354, 8)
        tmp8 = s.arf(Z + 370, 8)
        pcol16 = s.arf(Z + 386, 1)
        prow = s.arf(Z + 388, 16)
        DSP0 = 40960
        HS0 = 0
        W2G0 = 40960
        EOSF = 57344

        s.op("vector", lambda v: v.memset(zros, 0.0), inc=sV)
        for t in range(NT):
            pb, bw = s.bank(0, extra=s.lnf_done[("own", t)])
            for dc in range(NDC):
                s.op("tensor", lambda te, pb=pb, dc=dc, t=t:
                     te.matmul(pb[:, 0:16], s.arb(s.LNF0 + dc * FM + 128 + t * 128, 128),
                               s.gwT[:, dc * NE:(dc + 1) * NE], start=(dc == 0), stop=(dc == NDC - 1)),
                     waits=bw if dc == 0 else (), inc=sT if dc == NDC - 1 else None)
            lg = [sT.at()]
            mx = mxc[:, 0:1]
            nmx = mxc[:, 1:2]
            s.op("vector", lambda v, pb=pb, mx=mx: v.reduce_max(mx, pb[:, 0:16], axis=AX), waits=lg, inc=sV)
            s.op("vector", lambda v, mx=mx, nmx=nmx: v.tensor_scalar(nmx, mx, -1.0, None, ALU.mult),
                 waits=[sV.at()], inc=sV)
            s.op("scalar", lambda sc, pb=pb, nmx=nmx: sc.activation(expt, pb[:, 0:16], AF.Exp, bias=nmx,
                                                                   accum_out=sume[:, 0:1]),
                 waits=[sV.at()], inc=sS)
            s.op("vector", lambda v, t=t: v.reciprocal(s.gcol[:, t:t + 1], sume[:, 0:1]), waits=[sS.at()], inc=sV)
            s.op("vector", lambda v, pb=pb, mx=mx, t=t:
                 v.tensor_scalar(s.masks[:, t * NE:(t + 1) * NE], pb[:, 0:16], mx, None, ALU.is_equal),
                 waits=[sV.at()], inc=sV)
            s.bdone(0, [sV.at(), sS.at()])
        pb1, bw1 = s.bank(1)
        for t in range(NT):
            s.op("tensor", lambda te, pb1=pb1, t=t:
                 te.matmul(pb1[0:1, 0:16], s.ones_f[:], s.masks[:, t * NE:(t + 1) * NE],
                           start=(t == 0), stop=(t == NT - 1)),
                 waits=(bw1 + [sV.at()]) if t == 0 else (), inc=sT if t == NT - 1 else None)
        s.op("vector", lambda v: v.tensor_copy(crow[0:1, :], pb1[0:1, 0:16]), waits=[sT.at()], inc=sV)
        s.bdone(1, [sV.at()])
        s.op("vector", lambda v: v.tensor_tensor_scan(segi[0:1, :], crow[0:1, :], zros[0:1, 0:16], 0.0,
                                                      ALU.add, ALU.add), waits=[sV.at()], inc=sV)
        s.op("vector", lambda v: v.tensor_sub(segr[0:1, :], segi[0:1, :], crow[0:1, :]), waits=[sV.at()], inc=sV)
        s.op("sync", lambda sy: sy.dma_start(out=s.cnt_in[0:1, :], in_=crow[0:1, :]), waits=[sV.at()],
             inc=sL, inc_n=16)
        s.op("sync", lambda sy: sy.dma_start(out=s.cnt_in[1:2, :], in_=segr[0:1, :]), waits=[sV.at()],
             inc=sL, inc_n=16)
        s.op("gpsimd", lambda g: g.collective_compute(
            "AllGather", ALU.bypass, replica_groups=GROUPS_ALL,
            ins=[s.cnt_in.ap().opt()], outs=[s.cnt_out.ap().opt()]), waits=[sL.at()], inc=sC)
        cnt_ag = [sC.at()]

        for t in range(NT):
            bi = 6 + t % 2
            pb, bw = s.bank(bi, extra=[sV.at()])
            s.op("tensor", lambda te, pb=pb, t=t:
                 te.transpose(pb[0:16, 0:128], s.masks[:, t * NE:(t + 1) * NE], s.idt_f[:]), waits=bw, inc=sT)
            s.op("vector", lambda v, pb=pb, t=t: v.tensor_copy(masket[0:16, t * 128:(t + 1) * 128],
                                                               pb[0:16, 0:128]), waits=[sT.at()], inc=sV)
            s.bdone(bi, [sV.at()])
        s.op("vector", lambda v: v.tensor_tensor_scan(cumet[0:16, :], masket[0:16, :], zros[0:16, :], 0.0,
                                                      ALU.add, ALU.add), waits=[sV.at()], inc=sV)
        cum_done = [sV.at()]
        pb2, bw2 = s.bank(2, extra=[sV.at()])
        s.op("tensor", lambda te, pb2=pb2: te.matmul(pb2[:, 0:16], s.ones1r[:], segr[0:1, :],
                                                     start=True, stop=True), waits=bw2, inc=sT)
        segb_mm = [sT.at()]
        for t in range(NT):
            bi = 6 + t % 2
            pb, bw = s.bank(bi, extra=cum_done)
            s.op("tensor", lambda te, pb=pb, t=t:
                 te.transpose(pb[:, 0:16], cumet[0:16, t * 128:(t + 1) * 128], s.idt_f[0:16, 0:16]),
                 waits=bw, inc=sT)
            trm = [sT.at()]
            s.op("vector", lambda v, pb=pb, t=t: v.tensor_mul(tmp16, s.masks[:, t * NE:(t + 1) * NE],
                                                              pb[:, 0:16]), waits=trm, inc=sV)
            s.bdone(bi, [sV.at()])
            s.op("vector", lambda v, t=t: v.reduce_sum(lrt[:, t:t + 1], tmp16, axis=AX), waits=[sV.at()], inc=sV)
            s.op("vector", lambda v, t=t: v.tensor_scalar(lrt[:, t:t + 1], lrt[:, t:t + 1], -1.0, None, ALU.add),
                 waits=[sV.at()], inc=sV)
            s.op("vector", lambda v, t=t: v.tensor_mul(tmp16, s.masks[:, t * NE:(t + 1) * NE], pb2[:, 0:16]),
                 waits=segb_mm + [sV.at()], inc=sV)
            s.op("vector", lambda v, t=t: v.reduce_sum(c1, tmp16, axis=AX), waits=[sV.at()], inc=sV)
            s.op("vector", lambda v, t=t: v.tensor_add(c2, c1, lrt[:, t:t + 1]), waits=[sV.at()], inc=sV)
            s.op("vector", lambda v, t=t: v.tensor_copy(s.idx_sc[:, t:t + 1], c2), waits=[sV.at()], inc=sV)
            s.op("gpsimd", lambda g, t=t: g.indirect_dma_start(
                out=s.cmp_in[:, :], out_offset=bass.IndirectOffsetOnAxis(ap=s.idx_sc[:, t:t + 1], axis=0),
                in_=s.arb(s.LNT0 + t * DM, DM), in_offset=None),
                waits=[sV.at()], inc=sG, inc_n=16)
        s.bdone(2, [sV.at()])
        s.op("gpsimd", lambda g: g.collective_compute(
            "AllGather", ALU.bypass, replica_groups=GROUPS_ALL,
            ins=[s.cmp_in.ap().opt()], outs=[s.cmp_out.ap().opt()]), waits=[sG.at()], inc=sC)
        tok_ag = [sC.at()]

        s.op("sync", lambda sy: sy.dma_start(out=s.cntT[:, :],
                                             in_=s.cnt_out.ap()[0:16:2, :].rearrange("r e -> e r"),
                                             allow_slow_non_contiguous=True),
             waits=cnt_ag, inc=sL, inc_n=16)
        s.op("sync", lambda sy: sy.dma_start(out=s.segT[:, :],
                                             in_=s.cnt_out.ap()[1:16:2, :].rearrange("r e -> e r"),
                                             allow_slow_non_contiguous=True),
             waits=cnt_ag, inc=sL, inc_n=16)
        ld2 = [sL.at()]
        s.op("vector", lambda v: v.tensor_tensor_scan(s.cumT[:, :], s.cntT[:, :], s.zrow16[:, :], 0.0,
                                                      ALU.add, ALU.add), waits=ld2, inc=sV)
        s.op("vector", lambda v: v.tensor_mul(tmp8[0:16, :], s.cntT[:, :], s.cmask8[:, :]), waits=ld2, inc=sV)
        s.op("vector", lambda v: v.reduce_sum(pcol16[0:16, :], tmp8[0:16, 0:8], axis=AX), waits=[sV.at()], inc=sV)
        pcol_done = [sV.at()]
        bi = 6
        pb, bw = s.bank(bi, extra=pcol_done)
        s.op("tensor", lambda te, pb=pb: te.transpose(pb[0:1, 0:16], pcol16[0:16, 0:1], s.idt_f[0:16, 0:16]),
             waits=bw, inc=sT)
        s.op("vector", lambda v, pb=pb: v.tensor_copy(prow[0:1, :], pb[0:1, 0:16]), waits=[sT.at()], inc=sV)
        s.bdone(bi, [sV.at()])
        pb3, bw3 = s.bank(3, extra=[sV.at()])
        s.op("tensor", lambda te, pb3=pb3: te.matmul(pb3[:, 0:16], s.ones1r[:], prow[0:1, :],
                                                     start=True, stop=True), waits=bw3, inc=sT)
        s.op("vector", lambda v, pb3=pb3: v.tensor_copy(prefb, pb3[:, 0:16]), waits=[sT.at()], inc=sV)
        s.bdone(3, [sV.at()])
        for t in range(NT):
            s.op("vector", lambda v, t=t: v.tensor_mul(tmp16, s.masks[:, t * NE:(t + 1) * NE], prefb),
                 waits=[sV.at()], inc=sV)
            s.op("vector", lambda v: v.reduce_sum(c1, tmp16, axis=AX), waits=[sV.at()], inc=sV)
            s.op("vector", lambda v, t=t: v.tensor_add(c2, c1, lrt[:, t:t + 1]), waits=[sV.at()], inc=sV)
            s.op("vector", lambda v: v.tensor_scalar(c3, c2, float(CAP), None, ALU.is_lt), waits=[sV.at()],
                 inc=sV)
            s.op("vector", lambda v, t=t: v.tensor_mul(s.gk[:, t:t + 1], s.gcol[:, t:t + 1], c3),
                 waits=[sV.at()], inc=sV)
            s.op("vector", lambda v, t=t: v.tensor_mul(tmp16, s.masks[:, t * NE:(t + 1) * NE],
                                                       s.cst[:, s.C_BASEE:s.C_BASEE + 16]), inc=sV)
            s.op("vector", lambda v: v.reduce_sum(c4, tmp16, axis=AX), waits=[sV.at()], inc=sV)
            s.op("vector", lambda v, t=t: v.tensor_mul(tmp16, s.masks[:, t * NE:(t + 1) * NE],
                                                       s.cst[:, s.C_TRASHE:s.C_TRASHE + 16]), waits=[sV.at()],
                 inc=sV)
            s.op("vector", lambda v: v.reduce_sum(c5, tmp16, axis=AX), waits=[sV.at()], inc=sV)
            s.op("vector", lambda v: v.tensor_add(c4, c4, c2), waits=[sV.at()], inc=sV)
            s.op("vector", lambda v: v.tensor_sub(c4, c4, c5), waits=[sV.at()], inc=sV)
            s.op("vector", lambda v: v.scalar_tensor_tensor(c4, c4, c3, c5, ALU.mult, ALU.add),
                 waits=[sV.at()], inc=sV)
            s.op("vector", lambda v, t=t: v.tensor_copy(s.idx_eo[:, t:t + 1], c4), waits=[sV.at()], inc=sV)

        s.op("gpsimd", lambda g: g.iota(rcol[:, 0:10], pattern=[[128, 10]], base=0, channel_multiplier=1,
                                        allow_small_or_imprecise_dtypes=True), inc=s.sP)
        rdy = [s.sP.at(), sV.at()]
        pb4, bw4 = s.bank(4, extra=rdy)
        for el in range(2):
            s.op("tensor", lambda te, pb4=pb4, el=el:
                 te.matmul(pb4[0:1, el * 8:el * 8 + 8], s.myexp_s[:, el:el + 1], s.cumT[:, :],
                           start=(el == 0), stop=False), waits=bw4 if el == 0 else (), inc=None)
            s.op("tensor", lambda te, pb4=pb4, el=el:
                 te.matmul(pb4[0:1, 16 + el * 8:16 + el * 8 + 8], s.myexp_s[:, el:el + 1], s.segT[:, :],
                           start=False, stop=(el == 1)), inc=sT)
        s.op("vector", lambda v, pb4=pb4: v.tensor_copy(mcum[0:1, 0:8], pb4[0:1, 0:8]), waits=[sT.at()], inc=sV)
        s.op("vector", lambda v, pb4=pb4: v.tensor_copy(mcum[0:1, 8:16], pb4[0:1, 8:16]), inc=sV)
        s.op("vector", lambda v, pb4=pb4: v.tensor_copy(mseg[0:1, 0:8], pb4[0:1, 16:24]), inc=sV)
        s.op("vector", lambda v, pb4=pb4: v.tensor_copy(mseg[0:1, 8:16], pb4[0:1, 24:32]), inc=sV)
        s.bdone(4, [sV.at()])
        for el in range(2):
            pb5, bw5 = s.bank(5, extra=[sV.at()])
            s.op("tensor", lambda te, pb5=pb5, el=el:
                 te.matmul(pb5[:, 0:8], s.ones1r[:], mcum[0:1, el * 8:el * 8 + 8],
                           start=True, stop=False), waits=bw5, inc=sT)
            s.op("tensor", lambda te, pb5=pb5, el=el:
                 te.matmul(pb5[:, 8:16], s.ones1r[:], mseg[0:1, el * 8:el * 8 + 8],
                           start=False, stop=True), inc=sT)
            s.op("vector", lambda v, pb5=pb5: v.tensor_copy(cumb, pb5[:, 0:8]), waits=[sT.at()], inc=sV)
            s.op("vector", lambda v, pb5=pb5: v.tensor_copy(segb, pb5[:, 8:16]), waits=[sT.at()], inc=sV)
            s.bdone(5, [sV.at()])
            for st in range(10):
                r_ = rcol[:, st:st + 1]
                s.op("vector", lambda v, r_=r_: v.tensor_scalar(tmp8, cumb, r_, None, ALU.is_le),
                     waits=[sV.at()], inc=sV)
                s.op("vector", lambda v: v.reduce_sum(c1, tmp8, axis=AX), waits=[sV.at()], inc=sV)
                s.op("vector", lambda v: v.tensor_mul(tmp8, tmp8, cumb), waits=[sV.at()], inc=sV)
                s.op("vector", lambda v: v.reduce_max(c2, tmp8, axis=AX), waits=[sV.at()], inc=sV)
                s.op("vector", lambda v: v.tensor_scalar(tmp8, s.cst[:, s.C_IOTA8:s.C_IOTA8 + 8], c1, None,
                                                         ALU.is_equal), waits=[sV.at()], inc=sV)
                s.op("vector", lambda v: v.tensor_mul(tmp8, tmp8, segb), waits=[sV.at()], inc=sV)
                s.op("vector", lambda v: v.reduce_sum(c3, tmp8, axis=AX), waits=[sV.at()], inc=sV)
                s.op("vector", lambda v: v.tensor_scalar(c4, c1, float(T), None, ALU.mult), waits=[sV.at()], inc=sV)
                s.op("vector", lambda v: v.tensor_add(c4, c4, c3), waits=[sV.at()], inc=sV)
                s.op("vector", lambda v, r_=r_: v.tensor_add(c4, c4, r_), waits=[sV.at()], inc=sV)
                s.op("vector", lambda v: v.tensor_sub(c4, c4, c2), waits=[sV.at()], inc=sV)
                s.op("vector", lambda v: v.tensor_scalar_min(c4, c4, float(NCORE * T - 1)), waits=[sV.at()], inc=sV)
                s.op("vector", lambda v, el=el, st=st: v.tensor_copy(s.dgi[:, el * 10 + st:el * 10 + st + 1], c4),
                     waits=[sV.at()], inc=sV)
        dgi_done = [sV.at()]
        self.fence()

        for el in range(2):
            for st in range(10):
                gt = s.arb(GT0 + (st % 2) * 1024, 1024)
                s.op("gpsimd", lambda g, el=el, st=st, gt=gt: g.indirect_dma_start(
                    out=gt, out_offset=None, in_=s.cmp_out[:, :],
                    in_offset=bass.IndirectOffsetOnAxis(ap=s.dgi[:, el * 10 + st:el * 10 + st + 1], axis=0)),
                    waits=[sS.at()], inc=sG, inc_n=16)
                gw_ = [sG.at()]
                for dc in range(NDC):
                    bi = 6 + dc % 2
                    pb, bw = s.bank(bi, extra=gw_)
                    s.op("tensor", lambda te, pb=pb, gt=gt, dc=dc:
                         te.transpose(pb[:, 0:64].bitcast(BF16), gt[:, dc * 128:(dc + 1) * 128], s.idt_b[:]),
                         waits=bw, inc=sT)
                    s.op("scalar", lambda sc, pb=pb, dc=dc, st=st:
                         sc.copy(s.arb(DSP0 + dc * CAP + st * 128, 128), pb[:, 0:64].bitcast(BF16)),
                         waits=[sT.at()], inc=sS)
                    s.bdone(bi, [sS.at()])
            disp_done = [sS.at()]
            if self.dbg and self.moe_calls == 0 and el == 0:
                s.op("sync", lambda sy: sy.dma_start(out=s.dbg_disp[:, :], in_=s.arb(DSP0, NDC * CAP)),
                     waits=disp_done, inc=sL, inc_n=16)
                s.op("gpsimd", lambda g: None, waits=[sL.at()])
            burst_end = {}
            mmdone = {}

            def w1_burst(b):
                for fc in range(4 * b, 4 * b + 4):
                    prev = [mmdone[b - 2]] if (b >= 2 and fc == 4 * b) else []
                    s.op("gpsimd", lambda g, el=el, fc=fc:
                         g.dma_start(out=s.arb(W1B + (fc % 8) * 1024, 1024), in_=s.w1_m[el, fc, :, :]),
                         waits=(disp_done if b == 0 and fc == 0 else []) + prev, inc=sG, inc_n=16)
                burst_end[b] = sG.at()
                s.op("gpsimd", lambda g: None, waits=[burst_end[b]])
            w1_burst(0)
            w1_burst(1)
            for b in range(8):
                for fc in range(4 * b, 4 * b + 4):
                    wb = s.arb(W1B + (fc % 8) * 1024, 1024)
                    for sl, (c0, w) in enumerate(((0, 512), (512, 512), (1024, 256))):
                        bi = sl % 2
                        pb, bw = s.bank(bi, extra=[burst_end[b]] + (disp_done if fc == 0 else []))
                        for dc in range(NDC):
                            s.op("tensor", lambda te, pb=pb, wb=wb, dc=dc, c0=c0, w=w:
                                 te.matmul(pb[:, 0:w], wb[:, dc * 128:(dc + 1) * 128],
                                           s.arb(DSP0 + dc * CAP + c0, w), start=(dc == 0), stop=(dc == NDC - 1)),
                                 waits=bw if dc == 0 else (), inc=sT if dc == NDC - 1 else None)
                        s.op("scalar", lambda sc, pb=pb, fc=fc, c0=c0, w=w:
                             sc.activation(s.arb(HS0 + fc * CAP + c0, w), pb[:, 0:w], AF.Gelu_apprx_tanh),
                             waits=[sT.at()], inc=sS)
                        s.bdone(bi, [sS.at()])
                mmdone[b] = sT.at()
                if b + 2 <= 7:
                    w1_burst(b + 2)
            h_done = [sS.at()]
            if self.dbg and self.moe_calls == 0 and el == 0:
                s.op("sync", lambda sy: sy.dma_start(out=s.dbg_h[:, :], in_=s.arb(HS0, NFC * CAP)),
                     waits=h_done, inc=sL, inc_n=16)
                s.op("tensor", lambda te: None, waits=[sL.at()])
                s.op("gpsimd", lambda g: None, waits=[sL.at()])
            eosb = s.arf(EOSF, 5120)
            w2done = {}
            w2l = {}

            def w2_load(k):
                buf = W2G0 + (k % 2) * 8192
                fg = k % 4
                for j in range(8):
                    prevw = [w2done[k - 2]] if (k >= 2 and j == 0) else []
                    s.op("gpsimd", lambda g, el=el, fg=fg, j=j, buf=buf:
                         g.dma_start(out=s.arb(buf + j * 1024, 1024),
                                     in_=s.w2_m[el, (fg * 8 + j) * 128:(fg * 8 + j + 1) * 128, :]),
                         waits=(h_done if k == 0 and j == 0 else []) + prevw, inc=sG, inc_n=16)
                w2l[k] = sG.at()
                s.op("gpsimd", lambda g: None, waits=[w2l[k]])
            w2_load(0)
            w2_load(1)
            for half in range(2):
                for fg in range(4):
                    k = half * 4 + fg
                    buf = W2G0 + (k % 2) * 8192
                    for si in range(5):
                        ss = half * 5 + si
                        for dh in range(2):
                            bi = (si * 2 + dh) % 2
                            pb, bw = s.bank(bi, extra=[w2l[k]] + (h_done if k == 0 else []))
                            for j in range(8):
                                s.op("tensor", lambda te, pb=pb, fg=fg, j=j, ss=ss, dh=dh, buf=buf:
                                     te.matmul(pb[:, 0:512], s.arb(HS0 + (fg * 8 + j) * CAP + ss * 128, 128),
                                               s.arb(buf + j * 1024 + dh * 512, 512),
                                               start=(j == 0), stop=(j == 7)),
                                     waits=bw if j == 0 else (), inc=sT if j == 7 else None)
                            dst = eosb[:, si * 1024 + dh * 512:si * 1024 + dh * 512 + 512]
                            if fg == 0:
                                s.op("vector", lambda v, pb=pb, dst=dst: v.tensor_copy(dst, pb[:, 0:512]),
                                     waits=[sT.at()], inc=sV)
                            else:
                                s.op("vector", lambda v, pb=pb, dst=dst: v.tensor_add(dst, dst, pb[:, 0:512]),
                                     waits=[sT.at()], inc=sV)
                            s.bdone(bi, [sV.at()])
                    w2done[k] = sT.at()
                    if k + 2 <= 7:
                        w2_load(k + 2)
                for si in range(5):
                    ss = half * 5 + si
                    ec = s.arb(ECST + (si % 2) * 1024, 1024)
                    s.op("vector", lambda v, si=si, ec=ec: v.tensor_copy(ec, eosb[:, si * 1024:(si + 1) * 1024]),
                         waits=[sV.at(), sL.at()], inc=sV)
                    s.op("sync", lambda sy, el=el, ss=ss, ec=ec:
                         sy.dma_start(out=s.eo_in[el * CAP + ss * 128:el * CAP + (ss + 1) * 128, :], in_=ec),
                         waits=[sV.at()], inc=sL, inc_n=16)
            self.fence()
        zb = s.arb(GT0, 1024)
        s.op("vector", lambda v, zb=zb: v.memset(zb, 0.0), inc=sV)
        s.op("sync", lambda sy, zb=zb: sy.dma_start(out=s.eo_in[2 * CAP:EO_STRIDE, :], in_=zb[0:64, :]),
             waits=[sV.at()], inc=sL, inc_n=16)
        s.op("gpsimd", lambda g: g.collective_compute(
            "AllGather", ALU.bypass, replica_groups=GROUPS_ALL,
            ins=[s.eo_in.ap().opt()], outs=[s.eo_out.ap().opt()]), waits=[sL.at()], inc=sC)
        eo_ag = [sC.at()]
        prev_pair_resid = []
        for pt in range(NT // 2):
            pair = (2 * pt, 2 * pt + 1)
            extra = [s.sL.at()] if pt == 0 else []
            for t in pair:
                s.op("gpsimd", lambda g, t=t: g.indirect_dma_start(
                    out=s.arb(GT0 + (t % 2) * 1024, 1024), out_offset=None, in_=s.eo_out[:, :],
                    in_offset=bass.IndirectOffsetOnAxis(ap=s.idx_eo[:, t:t + 1], axis=0)),
                    waits=(eo_ag if pt == 0 and t == 2 * pt else []) + extra + prev_pair_resid,
                    inc=sG, inc_n=16)
                extra = []
            pair_end = s.sG.at()
            s.op("gpsimd", lambda g: None, waits=[pair_end])
            prev_pair_resid = []
            for t in pair:
                rd2 = self.resid_done[t] if self.resid_done[t] else []
                xt = s.xb[:, t * DM:(t + 1) * DM]
                s.op("vector", lambda v, xt=xt, t=t:
                     v.scalar_tensor_tensor(xt, s.arb(GT0 + (t % 2) * 1024, 1024), s.gk[:, t:t + 1], xt,
                                            ALU.mult, ALU.add),
                     waits=[pair_end] + rd2 + s.lnf_done[("own", t)], inc=sV)
                self.resid_done[t] = [sV.at()]
                prev_pair_resid = [sV.at()]
        self.moe_calls += 1
        if self.dbg and self.moe_calls == 1:
            def dmp(dst, srcap):
                s.op("sync", lambda sy, dst=dst, srcap=srcap: sy.dma_start(out=dst, in_=srcap),
                     inc=sL, inc_n=16)
            dmp(s.dbg_dgi[:, :], s.dgi[:, :])
            dmp(s.dbg_ieo[:, :], s.idx_eo[:, :])
            dmp(s.dbg_isc[:, :], s.idx_sc[:, :])
            dmp(s.dbg_gk[:, :], s.gk[:, :])
            dmp(s.dbg_cnt[:, :], s.cntT[:, :])
            dmp(s.dbg_seg[:, :], s.segT[:, :])
            dmp(s.dbg_cmp[:, :], s.cmp_in.ap())
            dmp(s.dbg_cmpo[:, :], s.cmp_out.ap())
            dmp(s.dbg_eoin[:, :], s.eo_in.ap())
            self.fence()


    def attn_b(self):
        s = self
        sL, sG, sT, sV, sS, sC = s.sL, s.sG, s.sT, s.sV, s.sS, s.sC
        SC = s.SC0
        QF, KF, VF, VTM = SC, SC + 2304, SC + 4608, SC + 6912
        MB0 = SC + 9216
        SCB_F = SC + 15360
        EXPT_F = SC + 16128
        ATT0 = SC + 16896
        ATTT0 = SC + 17280
        IOT_F = SC + 17664
        KP_F = SC + 18432
        HXL = SC + 4096
        HXR = SC + 8192

        s.op("sync", lambda sy: sy.dma_start(out=s.halo_in[0:128, :], in_=s.xb[:, 0:DM]),
             waits=self.resid_done[0], inc=sL, inc_n=16)
        s.op("sync", lambda sy: sy.dma_start(out=s.halo_in[128:256, :], in_=s.xb[:, (NT - 1) * DM:NT * DM]),
             waits=self.resid_done[NT - 1], inc=sL, inc_n=16)
        s.op("gpsimd", lambda g: g.collective_compute(
            "AllGather", ALU.bypass, replica_groups=GROUPS_ALL,
            ins=[s.halo_in.ap().opt()], outs=[s.halo_out.ap().opt()]), waits=[sL.at()], inc=sC)
        halo_ag = [sC.at()]
        s.op("gpsimd", lambda g: g.indirect_dma_start(
            out=s.arf(HXL, DM), out_offset=None, in_=s.halo_out[:, :],
            in_offset=bass.IndirectOffsetOnAxis(ap=s.hl_idx[:, 0:1], axis=0)),
            waits=halo_ag + [s.sT.at()], inc=sG, inc_n=16)
        s.op("gpsimd", lambda g: g.indirect_dma_start(
            out=s.arf(HXR, DM), out_offset=None, in_=s.halo_out[:, :],
            in_offset=bass.IndirectOffsetOnAxis(ap=s.hr_idx[:, 0:1], axis=0)),
            waits=[s.sG.at()], inc=sG, inc_n=16)
        s.op("gpsimd", lambda g: None, waits=[s.sG.at()])
        self.lnt(halo=True, tile_waits=self.resid_done)

        iot = s.arf(IOT_F, 384)
        kp = s.arf(KP_F, 384)
        ma = s.arf(SCB_F, 384)
        mb_ = s.arf(EXPT_F, 384)
        s.op("gpsimd", lambda g: g.iota(iot, pattern=[[1, 384]], base=-128, channel_multiplier=-1,
                                        allow_small_or_imprecise_dtypes=True),
             waits=[s.sV.at()], inc=s.sP)
        iota_done = [s.sP.at()]
        for t in range(NT):
            s.op("gpsimd", lambda g, t=t: g.iota(kp, pattern=[[1, 384]], base=t * 128 - 128, channel_multiplier=0,
                                                 allow_small_or_imprecise_dtypes=True),
                 waits=[sV.at()], inc=s.sP)
            s.op("vector", lambda v: v.tensor_scalar(kp, kp, s.cst[:, s.C_P0:s.C_P0 + 1], None, ALU.add),
                 waits=[s.sP.at()], inc=sV)
            s.op("vector", lambda v: v.tensor_scalar(ma, iot, -128.0, None, ALU.is_ge),
                 waits=iota_done + [sV.at()], inc=sV)
            s.op("vector", lambda v: v.tensor_scalar(mb_, iot, 128.0, None, ALU.is_le), inc=sV)
            s.op("vector", lambda v: v.tensor_mul(ma, ma, mb_), waits=[sV.at()], inc=sV)
            s.op("vector", lambda v: v.tensor_scalar(mb_, kp, 0.0, None, ALU.is_ge), waits=[sV.at()], inc=sV)
            s.op("vector", lambda v: v.tensor_mul(ma, ma, mb_), waits=[sV.at()], inc=sV)
            s.op("vector", lambda v: v.tensor_scalar(mb_, kp, float(SEQ), None, ALU.is_lt), waits=[sV.at()], inc=sV)
            s.op("vector", lambda v: v.tensor_mul(ma, ma, mb_), waits=[sV.at()], inc=sV)
            s.op("vector", lambda v, t=t: v.tensor_scalar(s.arb(MB0 + t * 384, 384), ma, 1.0e30, -1.0e30,
                                                          ALU.mult, ALU.add), waits=[sV.at()], inc=sV)
        mb_done = [sV.at()]

        for hh in range(NDC):
            s.op("gpsimd", lambda g, hh=hh: g.dma_start(out=s.arb(s.WB0 + hh * 1024, 1024),
                                                        in_=s.wproj_b[hh * 128:(hh + 1) * 128, :]),
                 waits=[s.sG.at()] if hh == 0 else (), inc=sG, inc_n=16)
        wp_ld = [sG.at()]

        TT5 = ((0, 512), (512, 512), (1024, 512), (1536, 512), (2048, 256))
        for hh in range(NDC):
            s.op("gpsimd", lambda g, hh=hh: g.dma_start(out=s.arb(SC + 19200, 0) if False else
                                                        s.arb(s.WB0 + 8192, 3072), in_=s.wqkv_b[hh, :, :]),
                 waits=[sT.at()], inc=sG, inc_n=16)
            whh_ld = [sG.at()]
            for part, dst0 in ((0, QF), (1, KF), (2, VF)):
                for it, (c0, w) in enumerate(TT5):
                    bi = it % 2
                    pb, bw = s.bank(bi, extra=whh_ld + (s.lnf_done_all if part == 0 and it == 0 else []))
                    for dc in range(NDC):
                        s.op("tensor", lambda te, pb=pb, part=part, dc=dc, c0=c0, w=w:
                             te.matmul(pb[:, 0:w], s.arb(s.WB0 + 8192 + (part * 8 + dc) * 128, 128),
                                       s.arb(s.LNF0 + dc * FM + c0, w), start=(dc == 0), stop=(dc == NDC - 1)),
                             waits=bw if dc == 0 else (), inc=sT if dc == NDC - 1 else None)
                    s.op("scalar", lambda sc, pb=pb, dst0=dst0, c0=c0, w=w:
                         sc.copy(s.arb(dst0 + c0, w), pb[:, 0:w]), waits=[sT.at()], inc=sS)
                    s.bdone(bi, [sS.at()])
            qkv_done = [sS.at()]
            for j in range(18):
                bi = 6 + j % 2
                pb, bw = s.bank(bi, extra=qkv_done)
                s.op("tensor", lambda te, pb=pb, j=j:
                     te.transpose(pb[:, 0:64].bitcast(BF16), s.arb(VF + j * 128, 128), s.idt_b[:]),
                     waits=bw, inc=sT)
                s.op("scalar", lambda sc, pb=pb, j=j: sc.copy(s.arb(VTM + j * 128, 128),
                                                              pb[:, 0:64].bitcast(BF16)),
                     waits=[sT.at()], inc=sS)
                s.bdone(bi, [sS.at()])
            vtm_done = [sS.at()]
            for t in range(NT):
                for head in range(2):
                    po = head * 64
                    qcol = 128 + t * 128
                    pb3, bw3 = s.bank(3, extra=vtm_done + mb_done)
                    s.op("tensor", lambda te, pb3=pb3, po=po, qcol=qcol, t=t:
                         te.matmul(pb3[:, 0:384], s.arb(QF + qcol, 128)[po:po + 64, :],
                                   s.arb(KF + t * 128, 384)[po:po + 64, :], start=True, stop=True),
                         waits=bw3, inc=sT)
                    scmm = [sT.at()]
                    scb = s.arf(SCB_F, 384)
                    s.op("vector", lambda v, pb3=pb3, t=t: v.tensor_add(scb, pb3[:, 0:384],
                                                                        s.arb(MB0 + t * 384, 384)),
                         waits=scmm, inc=sV)
                    s.bdone(3, [sV.at()])
                    mx = s.csm[:, 60:61]
                    nmx = s.csm[:, 61:62]
                    sume = s.csm[:, 62:63]
                    s.op("vector", lambda v, mx=mx: v.reduce_max(mx, scb, axis=AX), waits=[sV.at()], inc=sV)
                    s.op("vector", lambda v, mx=mx, nmx=nmx: v.tensor_scalar(nmx, mx, -0.125, None, ALU.mult),
                         waits=[sV.at()], inc=sV)
                    expt = s.arf(EXPT_F, 384)
                    s.op("scalar", lambda sc, nmx=nmx, expt=expt:
                         sc.activation(expt, scb, AF.Exp, bias=nmx, scale=0.125, accum_out=sume),
                         waits=[sV.at()], inc=sS)
                    s.op("vector", lambda v, sume=sume: v.reciprocal(s.csm[:, 63:64], sume), waits=[sS.at()],
                         inc=sV)
                    att = s.arb(ATT0, 384)
                    s.op("vector", lambda v, att=att, expt=expt:
                         v.tensor_scalar(att, expt, s.csm[:, 63:64], None, ALU.mult), waits=[sV.at()], inc=sV)
                    att_done = [sV.at()]
                    for j3 in range(3):
                        bi = 6 + j3 % 2
                        pb, bw = s.bank(bi, extra=att_done)
                        s.op("tensor", lambda te, pb=pb, att=att, j3=j3:
                             te.transpose(pb[:, 0:64].bitcast(BF16), att[:, j3 * 128:(j3 + 1) * 128], s.idt_b[:]),
                             waits=bw, inc=sT)
                        s.op("scalar", lambda sc, pb=pb, j3=j3: sc.copy(s.arb(ATTT0 + j3 * 128, 128),
                                                                        pb[:, 0:64].bitcast(BF16)),
                             waits=[sT.at()], inc=sS)
                        s.bdone(bi, [sS.at()])
                    attT_done = [sS.at()]
                    pb4, bw4 = s.bank(4, extra=attT_done)
                    for j3 in range(3):
                        s.op("tensor", lambda te, pb4=pb4, j3=j3, t=t, po=po, head=head:
                             te.matmul(pb4[po:po + 64, 0:128], s.arb(VTM + (t + j3) * 128 + po, 64),
                                       s.arb(ATTT0 + j3 * 128, 128),
                                       start=(j3 == 0), stop=(j3 == 2),
                                       tile_position=(0, po)),
                             waits=bw4 if j3 == 0 else (), inc=sT if j3 == 2 else None)
                    avmm = [sT.at()]
                    if head == 1:
                        s.op("scalar", lambda sc, pb4=pb4, hh=hh, t=t:
                             sc.copy(s.arb(s.LNT0 + hh * 2048 + t * 128, 128), pb4[:, 0:128]),
                             waits=avmm, inc=sS)
                        s.bdone(4, [sS.at()])
            self.fence()
        for tg in range(4):
            banks = []
            bw_all = [sV.at(), sS.at()]
            for fj in range(NDC):
                pb, bw = s.bank(fj, extra=bw_all)
                banks.append(pb)
                for hh in range(NDC):
                    s.op("tensor", lambda te, pb=pb, fj=fj, hh=hh, tg=tg:
                         te.matmul(pb[:, 0:512], s.arb(s.WB0 + hh * 1024 + fj * 128, 128),
                                   s.arb(s.LNT0 + hh * 2048 + tg * 512, 512), start=(hh == 0), stop=(hh == NDC - 1)),
                         waits=(bw + [s.sG.at()]) if hh == 0 else (), inc=sT if hh == NDC - 1 else None)
            pr_mm = [sT.at()]
            for fj in range(NDC):
                osb = s.arf(SCB_F, 512)
                s.op("scalar", lambda sc, fj=fj, osb=osb:
                     sc.activation(osb, banks[fj][:, 0:512], AF.Identity, bias=s.bp_b[:, fj:fj + 1]),
                     waits=pr_mm if fj == 0 else [sV.at()], inc=sS)
                ev = [sS.at()]
                for i in range(4):
                    t2 = tg * 4 + i
                    pb2, bw2 = s.bank(fj, extra=ev)
                    s.op("tensor", lambda te, pb2=pb2, osb=osb, i=i:
                         te.transpose(pb2[:, 0:128], osb[:, i * 128:(i + 1) * 128], s.idt_f[:]),
                         waits=bw2, inc=sT)
                    dst = s.xb[:, t2 * DM + fj * 128:t2 * DM + (fj + 1) * 128]
                    s.op("vector", lambda v, pb2=pb2, dst=dst: v.tensor_add(dst, dst, pb2[:, 0:128]),
                         waits=[sT.at()], inc=sV)
                    s.bdone(fj, [sV.at()])
                    if fj == NDC - 1:
                        self.resid_done[t2] = [sV.at()]

    def store(self):
        s = self
        for t in range(NT):
            s.op("sync", lambda sy, t=t: sy.dma_start(out=s.y_out[t * 128:(t + 1) * 128, :],
                                                      in_=s.xb[:, t * DM:(t + 1) * DM]),
                 waits=self.resid_done[t], inc=s.sL, inc_n=16)
        s.op("sync", lambda sy: None, waits=[s.sL.at()])

    def emit(self, block):
        @block.sync
        def _(sy):
            for f in self.ops["sync"]:
                f(sy)

        @block.tensor
        def _(te):
            for f in self.ops["tensor"]:
                f(te)

        @block.vector
        def _(v):
            for f in self.ops["vector"]:
                f(v)

        @block.scalar
        def _(sc):
            for f in self.ops["scalar"]:
                f(sc)

        @block.gpsimd
        def _(g):
            for f in self.ops["gpsimd"]:
                f(g)


def build(n_sub=4):
    nc = bass.Bass("TRN2", target_bir_lowering=False)
    mk = MK(nc, n_sub)
    with ExitStack() as ctx:
        mk.declare(ctx)
        block = ctx.enter_context(nc.Block())
        mk.p0()
        mk.resid_done = [[] for _ in range(NT)]
        if n_sub >= 1:
            mk.attn_a()
        if n_sub >= 2:
            mk.moe()
        if n_sub >= 3:
            mk.attn_b()
        if n_sub >= 4:
            mk.moe()
        mk.store()
        mk.emit(block)
    return nc


def kernel(**inputs):
    maps = shard_inputs(inputs)
    nc = build(4)
    res = run_bass_kernel_spmd(nc, maps, core_ids=list(range(NCORE)))
    y = np.concatenate([r["y_out"] for r in res.results], axis=0)
    return y.reshape(2, SEQ, DM)


# version 06 (final)
# speedup vs baseline: 1.1972x; 1.0159x over previous
import numpy as np
from contextlib import ExitStack
import concourse.bass as bass
import concourse.mybir as mybir
from concourse.bass_utils import run_bass_kernel_spmd
import ml_dtypes

F32 = mybir.dt.float32
BF16 = mybir.dt.bfloat16
I32 = mybir.dt.int32
AF = mybir.ActivationFunctionType
ALU = mybir.AluOpType
AX = mybir.AxisListType.X

NCORE = 8
T = 2048
NT = 16
DM = 1024
NDC = 8
SEQ = 8192
CAP = 1280
NE = 16
DFF = 4096
NFC = 32
EO_STRIDE = 2 * CAP + 64
FM = T + 256
LN_EPS = 1e-5
ATT_EPS = 1e-4
GROUPS_ALL = [list(range(NCORE))]
GROUPS_BATCH = [[0, 1, 2, 3], [4, 5, 6, 7]]


def shard_inputs(inp):
    x = np.ascontiguousarray(np.asarray(inp["L0_attn_a_x"], dtype=np.float32).reshape(2 * SEQ, DM))
    w1 = np.asarray(inp["L0_ffn_w1"], dtype=np.float32)
    w2 = np.asarray(inp["L0_ffn_w2"], dtype=np.float32)
    w1r = np.ascontiguousarray(w1.reshape(NE, NDC, 128, NFC, 128).transpose(0, 3, 2, 1, 4)
                               .reshape(NE, NFC, 128, NDC * 128))
    gate_wT = np.ascontiguousarray(np.asarray(inp["L0_ffn_gate_w"], dtype=np.float32).T)
    wqkv_b_hm = np.ascontiguousarray(
        np.asarray(inp["L0_attn_b_w_qkv"], dtype=np.float32)
        .reshape(NDC, 128, 3, NDC, 128).transpose(3, 1, 2, 0, 4).reshape(NDC, 128, 3 * DM))
    maps = []
    for c in range(NCORE):
        coremask8 = np.zeros((16, 8), np.float32)
        coremask8[:, :c] = 1.0
        myexp = np.zeros((16, 2), np.float32)
        myexp[2 * c, 0] = 1.0
        myexp[2 * c + 1, 1] = 1.0
        li = np.arange(128, dtype=np.int32)
        hleft = ((c - 1) * 256 + 128 + li) if c % 4 > 0 else (c * 256 + li)
        hright = ((c + 1) * 256 + li) if c % 4 < 3 else (c * 256 + 128 + li)
        maps.append(dict(
            x_in=np.ascontiguousarray(x[c * T:(c + 1) * T]),
            wqkv_a=np.asarray(inp["L0_attn_a_W_qkv"], dtype=np.float32),
            bqkv_a=np.ascontiguousarray(np.asarray(inp["L0_attn_a_b_qkv"], dtype=np.float32).reshape(1, 3 * DM)),
            bq_a_in=np.ascontiguousarray(np.asarray(inp["L0_attn_a_b_qkv"], dtype=np.float32)[:DM].reshape(8, 128).T),
            wout_a=np.asarray(inp["L0_attn_a_W_out"], dtype=np.float32),
            bout_a=np.ascontiguousarray(np.asarray(inp["L0_attn_a_b_out"], dtype=np.float32).reshape(8, 128).T),
            wqkv_b=wqkv_b_hm,
            wproj_b=np.asarray(inp["L0_attn_b_w_proj"], dtype=np.float32),
            bproj_b=np.ascontiguousarray(np.asarray(inp["L0_attn_b_b_proj"], dtype=np.float32).reshape(8, 128).T),
            gate_wT=gate_wT,
            w1_m=np.ascontiguousarray(w1r[2 * c:2 * c + 2]),
            w2_m=np.ascontiguousarray(w2[2 * c:2 * c + 2]),
            coremask8=coremask8,
            myexp=myexp,
            p0col=np.full((128, 1), float((c % 4) * T), np.float32),
            hleft_idx=hleft.reshape(128, 1).astype(np.int32),
            hright_idx=hright.reshape(128, 1).astype(np.int32),
        ))
    return maps


class Sem:
    def __init__(self, h):
        self.h = h
        self.v = 0

    def at(self):
        return (self, self.v)


class MK:

    def __init__(self, nc, n_sub=4):
        self.nc = nc
        self.n_sub = n_sub
        self.ops = {k: [] for k in ("sync", "tensor", "vector", "scalar", "gpsimd")}
        self.bfree = [[] for _ in range(8)]

    def op(self, engine, fn, waits=(), inc=None, inc_n=1):
        if inc is not None:
            inc.v += inc_n
        ws = tuple((s.h, v) for (s, v) in waits if v > 0)

        def run(eng, fn=fn, ws=ws, inc=inc, inc_n=inc_n):
            for h, v in ws:
                eng.wait_ge(h, v)
            ins = fn(eng)
            if inc is not None:
                ins.then_inc(inc.h, inc_n)
        self.ops[engine].append(run)

    def snap(self):
        return [s.at() for s in self.allsems]

    def fence(self):
        sn = self.snap()
        for e in ("sync", "tensor", "vector", "scalar", "gpsimd"):
            self.op(e, lambda eng: None, waits=sn)
        for i in range(8):
            self.bfree[i] = []

    def arb(self, c0, n):
        assert c0 >= 0 and c0 + n <= 70656, (c0, n)
        return self.AR[:, c0:c0 + n]

    def arf(self, c0, n):
        assert c0 % 2 == 0
        return self.arb(c0, 2 * n).bitcast(F32)

    def declare(self, ctx):
        nc = self.nc
        dt = nc.dram_tensor
        self.x_in = dt("x_in", [T, DM], F32, kind="ExternalInput")
        self.wqkv_a = dt("wqkv_a", [DM, 3 * DM], F32, kind="ExternalInput")
        self.bqkv_a = dt("bqkv_a", [1, 3 * DM], F32, kind="ExternalInput")
        self.bq_a_in = dt("bq_a_in", [128, 8], F32, kind="ExternalInput")
        self.wout_a = dt("wout_a", [DM, DM], F32, kind="ExternalInput")
        self.bout_a = dt("bout_a", [128, 8], F32, kind="ExternalInput")
        self.wqkv_b = dt("wqkv_b", [NDC, 128, 3 * DM], F32, kind="ExternalInput")
        self.wproj_b = dt("wproj_b", [DM, DM], F32, kind="ExternalInput")
        self.bproj_b = dt("bproj_b", [128, 8], F32, kind="ExternalInput")
        self.gate_wT_d = dt("gate_wT", [DM, NE], F32, kind="ExternalInput")
        self.w1_m = dt("w1_m", [2, NFC, 128, NDC * 128], F32, kind="ExternalInput")
        self.w2_m = dt("w2_m", [2, DFF, DM], F32, kind="ExternalInput")
        self.coremask8_d = dt("coremask8", [16, 8], F32, kind="ExternalInput")
        self.myexp_d = dt("myexp", [16, 2], F32, kind="ExternalInput")
        self.p0col_d = dt("p0col", [128, 1], F32, kind="ExternalInput")
        self.hleft_d = dt("hleft_idx", [128, 1], I32, kind="ExternalInput")
        self.hright_d = dt("hright_idx", [128, 1], I32, kind="ExternalInput")
        self.y_out = dt("y_out", [T, DM], F32, kind="ExternalOutput")
        import os
        self.dbg = bool(os.environ.get("MKDEBUG"))
        self.moe_calls = 0
        if self.dbg:
            self.dbg_dgi = dt("dbg_dgi", [128, 20], I32, kind="ExternalOutput")
            self.dbg_ieo = dt("dbg_ieo", [128, NT], I32, kind="ExternalOutput")
            self.dbg_isc = dt("dbg_isc", [128, NT], I32, kind="ExternalOutput")
            self.dbg_gk = dt("dbg_gk", [128, NT], F32, kind="ExternalOutput")
            self.dbg_cnt = dt("dbg_cnt", [16, 8], F32, kind="ExternalOutput")
            self.dbg_seg = dt("dbg_seg", [16, 8], F32, kind="ExternalOutput")
            self.dbg_cmp = dt("dbg_cmp", [T, DM], BF16, kind="ExternalOutput")
            self.dbg_cmpo = dt("dbg_cmpo", [NCORE * T, DM], BF16, kind="ExternalOutput")
            self.dbg_eoin = dt("dbg_eoin", [EO_STRIDE, DM], BF16, kind="ExternalOutput")
            self.dbg_disp = dt("dbg_disp", [128, NDC * CAP], BF16, kind="ExternalOutput")
            self.dbg_h = dt("dbg_h", [128, NFC * CAP], BF16, kind="ExternalOutput")
        self.kv_ar_in = dt("kv_ar_in", [128, 2080], F32)
        self.kv_ar_out = dt("kv_ar_out", [128, 2080], F32)
        self.cnt_in = dt("cnt_in", [2, 16], F32)
        self.cnt_out = dt("cnt_out", [16, 16], F32)
        self.cmp_in = dt("cmp_in", [T, DM], BF16)
        self.cmp_out = dt("cmp_out", [NCORE * T, DM], BF16)
        self.eo_in = dt("eo_in", [EO_STRIDE, DM], BF16)
        self.eo_out = dt("eo_out", [NCORE * EO_STRIDE, DM], BF16)
        self.halo_in = dt("halo_in", [256, DM], F32)
        self.halo_out = dt("halo_out", [NCORE * 256, DM], F32)
        it = nc.inline_tensor
        self.c_idf = it(np.eye(128, dtype=np.float32), name="c_idf")
        self.c_idb = it(np.eye(128, dtype=np.float32).astype(ml_dtypes.bfloat16), name="c_idb")
        self.c_1b = it(np.ones((128, 1), dtype=ml_dtypes.bfloat16), name="c_1b")
        self.c_1f = it(np.ones((128, 1), dtype=np.float32), name="c_1f")
        self.c_1r = it(np.ones((1, 128), dtype=np.float32), name="c_1r")
        base_e = np.array([(e // 2) * EO_STRIDE + (e % 2) * CAP for e in range(NE)], np.float32)
        trash_e = np.array([(e // 2) * EO_STRIDE + 2 * CAP for e in range(NE)], np.float32)
        self.c_base = it(np.broadcast_to(base_e, (128, NE)).copy(), name="c_base")
        self.c_trash = it(np.broadcast_to(trash_e, (128, NE)).copy(), name="c_trash")
        self.c_io16 = it(np.broadcast_to(np.arange(16, dtype=np.float32), (128, 16)).copy(), name="c_io16")
        self.c_io8 = it(np.broadcast_to(np.arange(8, dtype=np.float32), (128, 8)).copy(), name="c_io8")
        b2 = np.zeros((2, 128), np.float32)
        b2[0, :64] = 1.0
        b2[1, 64:] = 1.0
        self.c_b2 = it(b2.astype(ml_dtypes.bfloat16), name="c_b2")

        en = ctx.enter_context
        sbt = lambda name, shape, d=F32: en(nc.sbuf_tensor(name, shape, d))
        self.xb = sbt("xb", [128, NT * DM])
        self.AR = sbt("AR", [128, 70656], BF16)
        self.csm = sbt("csm", [128, 64])
        self.idt_f = sbt("idt_f", [128, 128])
        self.idt_b = sbt("idt_b", [128, 128], BF16)
        self.ones_b = sbt("ones_b", [128, 1], BF16)
        self.ones_f = sbt("ones_f", [128, 1])
        self.ones1r = sbt("ones1r", [1, 128])
        self.b2h = sbt("b2h", [2, 128], BF16)
        self.cst = sbt("cst", [128, 64])
        self.cmask8 = sbt("cmask8", [16, 8])
        self.myexp_s = sbt("myexp_s", [16, 2])
        self.cntT = sbt("cntT", [16, 8])
        self.segT = sbt("segT", [16, 8])
        self.cumT = sbt("cumT", [16, 8])
        self.zrow16 = sbt("zrow16", [16, 8])
        self.gwT = sbt("gwT", [128, NDC * NE], BF16)
        self.masks = sbt("masks", [128, NT * NE])
        self.gcol = sbt("gcol", [128, NT])
        self.idx_sc = sbt("idx_sc", [128, NT], I32)
        self.idx_eo = sbt("idx_eo", [128, NT], I32)
        self.gk = sbt("gk", [128, NT])
        self.hl_idx = sbt("hl_idx", [128, 1], I32)
        self.hr_idx = sbt("hr_idx", [128, 1], I32)
        self.bq_a = sbt("bq_a", [128, 8])
        self.bo_a = sbt("bo_a", [128, 8])
        self.bp_b = sbt("bp_b", [128, 8])
        self.dgi = sbt("dgi", [128, 20], I32)
        self.psum = [en(nc.psum_tensor(f"pb{i}", [128, 512], F32)) for i in range(8)]
        self.allsems = []
        for nm in ("sL", "sG", "sT", "sV", "sS", "sC", "sP"):
            s = Sem(en(nc.semaphore(nm)))
            setattr(self, nm, s)
            self.allsems.append(s)
        self.C_EPSLN, self.C_P0, self.C_HPI = 0, 2, 3
        self.C_BASEE, self.C_TRASHE, self.C_IOTA16, self.C_IOTA8 = 8, 24, 40, 56
        self.LNT0 = 0
        self.LNF0 = 16384
        self.WB0 = 34816
        self.SC0 = 51200
        self.HS0 = 0
        self.W2G0 = 40960
        self.EOS0 = 49152

    def bank(self, i, extra=()):
        w = list(self.bfree[i]) + list(extra)
        self.bfree[i] = []
        return self.psum[i], w

    def bdone(self, i, waits):
        self.bfree[i] = list(waits)

    def p0(self):
        s = self
        sL, sG, sV = s.sL, s.sG, s.sV

        def d(dst, src):
            s.op("sync", lambda sy, dst=dst, src=src: sy.dma_start(out=dst, in_=src), inc=sL, inc_n=16)
        d(s.idt_f[:], s.c_idf[:, :])
        d(s.idt_b[:], s.c_idb[:, :])
        d(s.ones_b[:], s.c_1b[:, :])
        d(s.ones_f[:], s.c_1f[:, :])
        d(s.ones1r[:], s.c_1r[:, :])
        d(s.b2h[:], s.c_b2[:, :])
        d(s.cst[:, s.C_P0:s.C_P0 + 1], s.p0col_d[:, :])
        d(s.cst[:, s.C_BASEE:s.C_BASEE + 16], s.c_base[:, :])
        d(s.cst[:, s.C_TRASHE:s.C_TRASHE + 16], s.c_trash[:, :])
        d(s.cst[:, s.C_IOTA16:s.C_IOTA16 + 16], s.c_io16[:, :])
        d(s.cst[:, s.C_IOTA8:s.C_IOTA8 + 8], s.c_io8[:, :])
        d(s.cmask8[:], s.coremask8_d[:, :])
        d(s.myexp_s[:], s.myexp_d[:, :])
        d(s.hl_idx[:], s.hleft_d[:, :])
        d(s.hr_idx[:], s.hright_d[:, :])
        d(s.bq_a[:], s.bq_a_in[:, :])
        d(s.bo_a[:], s.bout_a[:, :])
        d(s.bp_b[:], s.bproj_b[:, :])
        for t in range(NT):
            d(s.xb[:, t * DM:(t + 1) * DM], s.x_in[t * 128:(t + 1) * 128, :])
        for dc in range(NDC):
            s.op("gpsimd", lambda g, dc=dc: g.dma_start(out=s.gwT[:, dc * NE:(dc + 1) * NE],
                                                        in_=s.gate_wT_d[dc * 128:(dc + 1) * 128, :]),
                 inc=sG, inc_n=16)
        s.op("vector", lambda v: v.memset(s.cst[:, s.C_EPSLN:s.C_EPSLN + 1], LN_EPS), inc=sV)
        s.op("vector", lambda v: v.memset(s.cst[:, s.C_HPI:s.C_HPI + 1], float(np.pi / 2)), inc=sV)
        s.op("vector", lambda v: v.memset(s.zrow16[:], 0.0), inc=sV)
        self.fence()

    def lnt(self, halo=False, tile_waits=None):
        s = self
        sV, sS, sT = s.sV, s.sS, s.sT
        HXL = s.SC0 + 4096
        HXR = s.SC0 + 8192
        self.lnf_done = {}
        tiles = [("own", t) for t in range(NT)]
        if halo:
            tiles += [("hl", 0), ("hr", 0)]
        for kind, t in tiles:
            cc = [s.csm[:, i:i + 1] for i in range(7)]
            c_sum, c_mean, c_m2, c_var, c_sd, c_rstd, c_sumsq = cc
            if kind == "own":
                xt = s.xb[:, t * DM:(t + 1) * DM]
                tmdst = s.arb(s.LNT0 + t * DM, DM)
                fmcol = 128 + t * 128
            else:
                xt = s.arf(HXL if kind == "hl" else HXR, DM)
                tmdst = s.arb(s.SC0 + 6912 + (0 if kind == "hl" else DM), DM)
                fmcol = 0 if kind == "hl" else (T + 128)
            trash = s.arf(s.SC0, 1024)
            if tile_waits is not None and kind == "own":
                sn = list(tile_waits[t])
            else:
                sn = [sV.at(), sS.at(), s.sL.at(), s.sG.at()]
            s.op("vector", lambda v, xt=xt, c=c_sum: v.reduce_sum(c, xt, axis=AX), waits=sn, inc=sV)
            s.op("scalar", lambda sc, xt=xt, o=trash, a=c_sumsq: sc.activation(o, xt, AF.Square, accum_out=a),
                 waits=sn, inc=sS)
            s.op("vector", lambda v, i=c_sum, o=c_mean: v.tensor_scalar(o, i, 1.0 / DM, None, ALU.mult),
                 waits=[sV.at()], inc=sV)
            s.op("vector", lambda v, i=c_mean, o=c_m2: v.tensor_mul(o, i, i), waits=[sV.at()], inc=sV)
            s.op("vector", lambda v, i=c_sumsq, o=c_var: v.tensor_scalar(o, i, 1.0 / DM, None, ALU.mult),
                 waits=[sS.at()], inc=sV)
            s.op("vector", lambda v, o=c_var, a=c_m2: v.tensor_sub(o, o, a), waits=[sV.at()], inc=sV)
            s.op("scalar", lambda sc, i=c_var, o=c_sd:
                 sc.activation(o, i, AF.Sqrt, bias=s.cst[:, s.C_EPSLN:s.C_EPSLN + 1]),
                 waits=[sV.at()], inc=sS)
            s.op("vector", lambda v, i=c_sd, o=c_rstd: v.reciprocal(o, i), waits=[sS.at()], inc=sV)
            c_nmr = s.csm[:, 7:8]
            s.op("vector", lambda v, m=c_mean, r=c_rstd, o=c_nmr:
                 v.scalar_tensor_tensor(o, m, -1.0, r, ALU.mult, ALU.mult), waits=[sV.at()], inc=sV)
            s.op("scalar", lambda sc, xt=xt, r=c_rstd, b=c_nmr, o=tmdst:
                 sc.activation(o, xt, AF.Identity, bias=b, scale=r), waits=[sV.at()], inc=sS)
            lnt_done = sS.at()
            for dc in range(NDC):
                bi = 6 + (dc % 2)
                pb, bw = s.bank(bi, extra=[lnt_done])
                s.op("tensor", lambda te, pb=pb, src=tmdst[:, dc * 128:(dc + 1) * 128]:
                     te.transpose(pb[:, 0:64].bitcast(BF16), src, s.idt_b[:]), waits=bw, inc=sT)
                s.op("scalar", lambda sc, pb=pb, dst=s.arb(s.LNF0 + dc * FM + fmcol, 128):
                     sc.copy(dst, pb[:, 0:64].bitcast(BF16)), waits=[sT.at()], inc=sS)
                s.bdone(bi, [sS.at()])
            self.lnf_done[(kind, t)] = [sS.at(), sV.at()]
        self.lnf_done_all = [sS.at(), sV.at(), sT.at()]

    def attn_a(self):
        s = self
        sL, sG, sT, sV, sS, sC = s.sL, s.sG, s.sT, s.sV, s.sS, s.sC
        SC = s.SC0
        COS0, SIN0 = SC, SC + 2048
        KC0, KS0, VT0 = SC + 4096, SC + 5120, SC + 6144
        KT_F = SC + 7168
        KVS_F = SC + 9216
        POSR_F = SC + 13376
        self.lnt(tile_waits=getattr(self, "resid_done", None))

        SCALE = float(np.pi / 2 / SEQ)
        posr = s.arf(POSR_F, 2048)
        s.op("gpsimd", lambda g: g.iota(posr, pattern=[[1, 2048]], base=0, channel_multiplier=0,
                                        allow_small_or_imprecise_dtypes=True), inc=s.sP)
        s.op("vector", lambda v: v.tensor_scalar(posr, posr, s.cst[:, s.C_P0:s.C_P0 + 1], None, ALU.add),
             waits=[s.sP.at()], inc=sV)
        s.op("scalar", lambda sc: sc.activation(s.arb(SIN0, 2048), posr, AF.Sin, scale=SCALE),
             waits=[sV.at()], inc=sS)
        s.op("scalar", lambda sc: sc.activation(s.arb(COS0, 2048), posr, AF.Sin, scale=-SCALE,
                                                bias=s.cst[:, s.C_HPI:s.C_HPI + 1]),
             waits=[sV.at()], inc=sS)
        pc = s.csm[:, 8:24]
        cc = s.csm[:, 24:40]
        scc = s.csm[:, 40:56]
        s.op("gpsimd", lambda g: g.iota(pc, pattern=[[128, 16]], base=0, channel_multiplier=1,
                                        allow_small_or_imprecise_dtypes=True), inc=s.sP)
        s.op("vector", lambda v: v.tensor_scalar(pc, pc, s.cst[:, s.C_P0:s.C_P0 + 1], None, ALU.add),
             waits=[s.sP.at()], inc=sV)
        s.op("scalar", lambda sc: sc.activation(scc, pc, AF.Sin, scale=SCALE), waits=[sV.at()], inc=sS)
        s.op("scalar", lambda sc: sc.activation(cc, pc, AF.Sin, scale=-SCALE,
                                                bias=s.cst[:, s.C_HPI:s.C_HPI + 1]), waits=[sV.at()], inc=sS)
        trig_done = [sS.at()]
        bkv = s.arf(POSR_F, 2048)
        s.op("sync", lambda sy: sy.dma_start(out=bkv[:, 0:1024],
                                             in_=s.bqkv_a.ap()[0:1, DM:2 * DM].partition_broadcast(128)),
             waits=trig_done, inc=sL, inc_n=16)
        s.op("sync", lambda sy: sy.dma_start(out=bkv[:, 1024:2048],
                                             in_=s.bqkv_a.ap()[0:1, 2 * DM:3 * DM].partition_broadcast(128)),
             waits=trig_done, inc=sL, inc_n=16)
        bkv_ld = [sL.at()]

        for dc in range(NDC):
            s.op("gpsimd", lambda g, dc=dc: g.dma_start(out=s.arb(s.WB0 + dc * 2048, 1024),
                                                        in_=s.wqkv_a[dc * 128:(dc + 1) * 128, DM:2 * DM]),
                 inc=sG, inc_n=16)
            s.op("gpsimd", lambda g, dc=dc: g.dma_start(out=s.arb(s.WB0 + dc * 2048 + 1024, 1024),
                                                        in_=s.wqkv_a[dc * 128:(dc + 1) * 128, 2 * DM:3 * DM]),
                 inc=sG, inc_n=16)
        wkv_ld = [sG.at()]

        kvs = s.arf(KVS_F, 2080)
        s.op("vector", lambda v: v.memset(kvs, 0.0), inc=sV)
        for t in range(NT):
            mm_w = wkv_ld + trig_done + s.lnf_done[("own", t)]
            for j, wcol in enumerate((0, 512, 1024, 1536)):
                pb, bw = s.bank(j, extra=mm_w)
                for dc in range(NDC):
                    ap = s.arb(s.LNF0 + dc * FM + 128 + t * 128, 128)
                    s.op("tensor", lambda te, pb=pb, ap=ap, dc=dc, wcol=wcol:
                         te.matmul(pb[:, 0:512], ap, s.arb(s.WB0 + dc * 2048 + wcol, 512),
                                   start=(dc == 0), stop=(dc == NDC - 1)),
                         waits=bw if dc == 0 else (), inc=sT if dc == NDC - 1 else None)
            mm_done = [sT.at()]
            kt = s.arf(KT_F, 1024)
            kc = s.arb(KC0, 1024)
            ks = s.arb(KS0, 1024)
            vt = s.arb(VT0, 1024)
            s.op("vector", lambda v, kt=kt: v.tensor_add(kt[:, 0:512], s.psum[0][:, 0:512], bkv[:, 0:512]),
                 waits=mm_done + bkv_ld, inc=sV)
            s.op("vector", lambda v, kt=kt: v.tensor_add(kt[:, 512:1024], s.psum[1][:, 0:512],
                                                         bkv[:, 512:1024]), inc=sV)
            s.op("scalar", lambda sc, kt=kt: sc.activation(kt, kt, AF.Relu), waits=[sV.at()], inc=sS)
            s.op("vector", lambda v, kt=kt, kc=kc, t=t: v.tensor_scalar(kc, kt, cc[:, t:t + 1], None, ALU.mult),
                 waits=[sS.at()], inc=sV)
            s.op("vector", lambda v, kt=kt, ks=ks, t=t: v.tensor_scalar(ks, kt, scc[:, t:t + 1], None, ALU.mult),
                 inc=sV)
            s.op("vector", lambda v, vt=vt: v.tensor_add(vt[:, 0:512], s.psum[2][:, 0:512],
                                                         bkv[:, 1024:1536]), waits=mm_done, inc=sV)
            s.op("vector", lambda v, vt=vt: v.tensor_add(vt[:, 512:1024], s.psum[3][:, 0:512],
                                                         bkv[:, 1536:2048]), inc=sV)
            kcv = [sV.at()]
            for j in range(4):
                s.bdone(j, kcv)
            for hh in range(NDC):
                bi = 4 + (hh % 2)
                pb, bw = s.bank(bi, extra=kcv)
                kcs_ = kc[:, hh * 128:(hh + 1) * 128]
                kss_ = ks[:, hh * 128:(hh + 1) * 128]
                vs_ = vt[:, hh * 128:(hh + 1) * 128]
                s.op("tensor", lambda te, pb=pb, a=kcs_, b=vs_:
                     te.matmul(pb[:, 0:128], a, b, start=True, stop=False), waits=bw)
                s.op("tensor", lambda te, pb=pb, a=kcs_:
                     te.matmul(pb[:, 128:129], a, s.ones_b[:], start=False, stop=False))
                s.op("tensor", lambda te, pb=pb, a=kss_, b=vs_:
                     te.matmul(pb[:, 132:260], a, b, start=False, stop=False))
                s.op("tensor", lambda te, pb=pb, a=kss_:
                     te.matmul(pb[:, 260:261], a, s.ones_b[:], start=False, stop=True), inc=sT)
                kvmm = [sT.at()]
                dstc = kvs[:, hh * 130:hh * 130 + 129]
                dsts = kvs[:, 1040 + hh * 130:1040 + hh * 130 + 129]
                if t == 0:
                    s.op("vector", lambda v, pb=pb, d=dstc: v.tensor_copy(d, pb[:, 0:129]), waits=kvmm, inc=sV)
                    s.op("vector", lambda v, pb=pb, d=dsts: v.tensor_copy(d, pb[:, 132:261]), inc=sV)
                else:
                    s.op("vector", lambda v, pb=pb, d=dstc: v.tensor_add(d, d, pb[:, 0:129]), waits=kvmm, inc=sV)
                    s.op("vector", lambda v, pb=pb, d=dsts: v.tensor_add(d, d, pb[:, 132:261]), inc=sV)
                s.bdone(bi, [sV.at()])

        s.op("sync", lambda sy: sy.dma_start(out=s.kv_ar_in[:, :], in_=kvs), waits=[sV.at()], inc=sL, inc_n=16)
        s.op("gpsimd", lambda g: g.collective_compute(
            "AllReduce", ALU.add, replica_groups=GROUPS_BATCH,
            ins=[s.kv_ar_in.ap().opt()], outs=[s.kv_ar_out.ap().opt()]), waits=[sL.at()], inc=sC)
        s.op("sync", lambda sy: sy.dma_start(out=kvs, in_=s.kv_ar_out[:, :]), waits=[sC.at()], inc=sL, inc_n=16)
        self.fence()

        for dc in range(NDC):
            s.op("gpsimd", lambda g, dc=dc: g.dma_start(out=s.arb(s.WB0 + dc * 1024, 1024),
                                                        in_=s.wqkv_a[dc * 128:(dc + 1) * 128, 0:DM]),
                 inc=sG, inc_n=16)
            s.op("gpsimd", lambda g, dc=dc: g.dma_start(out=s.arb(s.WB0 + 8192 + dc * 1024, 1024),
                                                        in_=s.wout_a[dc * 128:(dc + 1) * 128, :]),
                 inc=sG, inc_n=16)
        w_ld = [sG.at()]

        SBD0 = SC + 4096
        NBD0 = SC + 6144
        CTXN0 = SC + 6400
        QR_F = SC + 10496
        QC0, QS0 = SC + 11520, SC + 12032
        RNB0, BCB0 = SC + 12544, SC + 13056
        OSB_F = SC + 13568
        NSB_F = SC + 14592
        for hh in range(NDC):
            for which, srcoff in ((0, 0), (1, 1040)):
                d_ = s.arb(SBD0 + hh * 256 + which * 128, 128)
                s.op("vector", lambda v, d=d_: v.memset(d, 0.0), inc=sV)
                s.op("vector", lambda v, d=d_, so=srcoff, hh=hh:
                     v.tensor_copy(d[0:64, 0:64], kvs[0:64, so + hh * 130:so + hh * 130 + 64]),
                     waits=[sV.at()], inc=sV)
                s.op("vector", lambda v, d=d_, so=srcoff, hh=hh:
                     v.tensor_copy(d[64:128, 64:128], kvs[64:128, so + hh * 130 + 64:so + hh * 130 + 128]),
                     waits=[sV.at()], inc=sV)
                n_ = s.arb(NBD0 + hh * 4 + which * 2, 2)
                s.op("vector", lambda v, d=n_: v.memset(d, 0.0), inc=sV)
                s.op("vector", lambda v, d=n_, so=srcoff, hh=hh:
                     v.tensor_copy(d[0:64, 0:1], kvs[0:64, so + hh * 130 + 128:so + hh * 130 + 129]),
                     waits=[sV.at()], inc=sV)
                s.op("vector", lambda v, d=n_, so=srcoff, hh=hh:
                     v.tensor_copy(d[64:128, 1:2], kvs[64:128, so + hh * 130 + 128:so + hh * 130 + 129]),
                     waits=[sV.at()], inc=sV)
        sbd_done = [sV.at()]

        for tg in range(4):
            for hh in range(NDC):
                pbq, bwq = s.bank(0, extra=w_ld + sbd_done + s.lnf_done_all)
                for dc in range(NDC):
                    s.op("tensor", lambda te, pbq=pbq, dc=dc, hh=hh, tg=tg:
                         te.matmul(pbq[:, 0:512], s.arb(s.WB0 + dc * 1024 + hh * 128, 128),
                                   s.arb(s.LNF0 + dc * FM + 128 + tg * 512, 512),
                                   start=(dc == 0), stop=(dc == NDC - 1)),
                         waits=bwq if dc == 0 else (), inc=sT if dc == NDC - 1 else None)
                qr = s.arf(QR_F, 512)
                s.op("scalar", lambda sc, pbq=pbq, qr=qr, hh=hh:
                     sc.activation(qr, pbq[:, 0:512], AF.Relu, bias=s.bq_a[:, hh:hh + 1]),
                     waits=[sT.at()], inc=sS)
                s.bdone(0, [sS.at()])
                qc = s.arb(QC0, 512)
                qs = s.arb(QS0, 512)
                s.op("vector", lambda v, qr=qr, qc=qc, tg=tg: v.tensor_mul(qc, qr, s.arb(COS0 + tg * 512, 512)),
                     waits=[sS.at()], inc=sV)
                s.op("vector", lambda v, qr=qr, qs=qs, tg=tg: v.tensor_mul(qs, qr, s.arb(SIN0 + tg * 512, 512)),
                     inc=sV)
                qcs = [sV.at()]
                pbc, bwc = s.bank(1, extra=qcs)
                s.op("tensor", lambda te, pbc=pbc, hh=hh, qc=qc:
                     te.matmul(pbc[:, 0:512], s.arb(SBD0 + hh * 256, 128), qc, start=True, stop=False),
                     waits=bwc)
                s.op("tensor", lambda te, pbc=pbc, hh=hh, qs=qs:
                     te.matmul(pbc[:, 0:512], s.arb(SBD0 + hh * 256 + 128, 128), qs, start=False, stop=True),
                     inc=sT)
                pbn, bwn = s.bank(2, extra=qcs)
                s.op("tensor", lambda te, pbn=pbn, hh=hh, qc=qc:
                     te.matmul(pbn[0:2, 0:512], s.arb(NBD0 + hh * 4, 2), qc, start=True, stop=False),
                     waits=bwn)
                s.op("tensor", lambda te, pbn=pbn, hh=hh, qs=qs:
                     te.matmul(pbn[0:2, 0:512], s.arb(NBD0 + hh * 4 + 2, 2), qs, start=False, stop=True),
                     inc=sT)
                ctxmm = [sT.at()]
                nsb = s.arf(NSB_F, 512)
                s.op("vector", lambda v, pbn=pbn, nsb=nsb:
                     v.tensor_scalar(nsb[0:2, :], pbn[0:2, 0:512], ATT_EPS, None, ALU.add),
                     waits=ctxmm, inc=sV)
                s.bdone(2, [sV.at()])
                s.op("vector", lambda v, nsb=nsb: v.reciprocal(nsb[0:2, :], nsb[0:2, :]), waits=[sV.at()], inc=sV)
                rnb = s.arb(RNB0, 512)
                s.op("vector", lambda v, nsb=nsb, rnb=rnb: v.tensor_copy(rnb[0:2, :], nsb[0:2, :]),
                     waits=[sV.at()], inc=sV)
                pbb, bwb = s.bank(3, extra=[sV.at()])
                s.op("tensor", lambda te, pbb=pbb, rnb=rnb:
                     te.matmul(pbb[:, 0:512], s.b2h[:, :], rnb[0:2, :], start=True, stop=True),
                     waits=bwb, inc=sT)
                bcb = s.arb(BCB0, 512)
                s.op("scalar", lambda sc, pbb=pbb, bcb=bcb: sc.copy(bcb, pbb[:, 0:512]), waits=[sT.at()], inc=sS)
                s.bdone(3, [sS.at()])
                s.op("vector", lambda v, pbc=pbc, hh=hh, bcb=bcb:
                     v.tensor_mul(s.arb(CTXN0 + hh * 512, 512), pbc[:, 0:512], bcb),
                     waits=[sS.at()], inc=sV)
                s.bdone(1, [sV.at()])
            ctxn_done = [sV.at()]
            banks = []
            for fj in range(NDC):
                pb, bw = s.bank(fj, extra=ctxn_done)
                banks.append(pb)
                for hh in range(NDC):
                    s.op("tensor", lambda te, pb=pb, fj=fj, hh=hh:
                         te.matmul(pb[:, 0:512], s.arb(s.WB0 + 8192 + hh * 1024 + fj * 128, 128),
                                   s.arb(CTXN0 + hh * 512, 512), start=(hh == 0), stop=(hh == NDC - 1)),
                         waits=bw if hh == 0 else (), inc=sT if hh == NDC - 1 else None)
            wout_mm = [sT.at()]
            for fj in range(NDC):
                osb = s.arf(OSB_F, 512)
                s.op("scalar", lambda sc, fj=fj, osb=osb:
                     sc.activation(osb, banks[fj][:, 0:512], AF.Identity, bias=s.bo_a[:, fj:fj + 1]),
                     waits=wout_mm if fj == 0 else [sV.at()], inc=sS)
                ev = [sS.at()]
                for i in range(4):
                    t2 = tg * 4 + i
                    pb2, bw2 = s.bank(fj, extra=ev)
                    s.op("tensor", lambda te, pb2=pb2, osb=osb, i=i:
                         te.transpose(pb2[:, 0:128], osb[:, i * 128:(i + 1) * 128], s.idt_f[:]),
                         waits=bw2, inc=sT)
                    dst = s.xb[:, t2 * DM + fj * 128:t2 * DM + (fj + 1) * 128]
                    s.op("vector", lambda v, pb2=pb2, dst=dst: v.tensor_add(dst, dst, pb2[:, 0:128]),
                         waits=[sT.at()], inc=sV)
                    s.bdone(fj, [sV.at()])
                    if fj == NDC - 1:
                        self.resid_done[t2] = [sV.at()]



    def moe(self):
        s = self
        sL, sG, sT, sV, sS, sC = s.sL, s.sG, s.sT, s.sV, s.sS, s.sC
        SC = s.SC0
        self.lnt(tile_waits=self.resid_done)
        masket = s.arf(SC, 2048)
        cumet = s.arf(SC + 4096, 2048)
        zros = s.arf(SC + 8192, 2048)
        GT0 = SC + 12288
        W1B = SC + 4096
        ECST = SC + 16384
        Z = SC + 17408
        tmp16 = s.arf(Z, 16)
        rcol = s.arf(Z + 32, 16)
        crow = s.arf(Z + 64, 16)
        segi = s.arf(Z + 96, 16)
        segr = s.arf(Z + 128, 16)
        expt = s.arf(Z + 160, 16)
        sume = s.arf(Z + 192, 2)
        mxc = s.arf(Z + 196, 2)
        lrt = s.arf(Z + 200, 16)
        prefb = s.arf(Z + 232, 16)
        c1 = s.arf(Z + 264, 1)
        c2 = s.arf(Z + 266, 1)
        c3 = s.arf(Z + 268, 1)
        c4 = s.arf(Z + 270, 1)
        c5 = s.arf(Z + 272, 1)
        mcum = s.arf(Z + 274, 16)
        mseg = s.arf(Z + 306, 16)
        cumb = s.arf(Z + 338, 8)
        segb = s.arf(Z + 354, 8)
        tmp8 = s.arf(Z + 370, 8)
        pcol16 = s.arf(Z + 386, 1)
        prow = s.arf(Z + 388, 16)
        DSP0 = 40960
        HS0 = 0
        W2G0 = 40960
        EOSF = 57344

        s.op("vector", lambda v: v.memset(zros, 0.0), inc=sV)
        for t in range(NT):
            pb, bw = s.bank(0, extra=s.lnf_done[("own", t)])
            for dc in range(NDC):
                s.op("tensor", lambda te, pb=pb, dc=dc, t=t:
                     te.matmul(pb[:, 0:16], s.arb(s.LNF0 + dc * FM + 128 + t * 128, 128),
                               s.gwT[:, dc * NE:(dc + 1) * NE], start=(dc == 0), stop=(dc == NDC - 1)),
                     waits=bw if dc == 0 else (), inc=sT if dc == NDC - 1 else None)
            lg = [sT.at()]
            mx = mxc[:, 0:1]
            nmx = mxc[:, 1:2]
            s.op("vector", lambda v, pb=pb, mx=mx: v.reduce_max(mx, pb[:, 0:16], axis=AX), waits=lg, inc=sV)
            s.op("vector", lambda v, mx=mx, nmx=nmx: v.tensor_scalar(nmx, mx, -1.0, None, ALU.mult),
                 waits=[sV.at()], inc=sV)
            s.op("scalar", lambda sc, pb=pb, nmx=nmx: sc.activation(expt, pb[:, 0:16], AF.Exp, bias=nmx,
                                                                   accum_out=sume[:, 0:1]),
                 waits=[sV.at()], inc=sS)
            s.op("vector", lambda v, t=t: v.reciprocal(s.gcol[:, t:t + 1], sume[:, 0:1]), waits=[sS.at()], inc=sV)
            s.op("vector", lambda v, pb=pb, mx=mx, t=t:
                 v.tensor_scalar(s.masks[:, t * NE:(t + 1) * NE], pb[:, 0:16], mx, None, ALU.is_equal),
                 waits=[sV.at()], inc=sV)
            s.bdone(0, [sV.at(), sS.at()])
        pb1, bw1 = s.bank(1)
        for t in range(NT):
            s.op("tensor", lambda te, pb1=pb1, t=t:
                 te.matmul(pb1[0:1, 0:16], s.ones_f[:], s.masks[:, t * NE:(t + 1) * NE],
                           start=(t == 0), stop=(t == NT - 1)),
                 waits=(bw1 + [sV.at()]) if t == 0 else (), inc=sT if t == NT - 1 else None)
        s.op("vector", lambda v: v.tensor_copy(crow[0:1, :], pb1[0:1, 0:16]), waits=[sT.at()], inc=sV)
        s.bdone(1, [sV.at()])
        s.op("vector", lambda v: v.tensor_tensor_scan(segi[0:1, :], crow[0:1, :], zros[0:1, 0:16], 0.0,
                                                      ALU.add, ALU.add), waits=[sV.at()], inc=sV)
        s.op("vector", lambda v: v.tensor_sub(segr[0:1, :], segi[0:1, :], crow[0:1, :]), waits=[sV.at()], inc=sV)
        s.op("sync", lambda sy: sy.dma_start(out=s.cnt_in[0:1, :], in_=crow[0:1, :]), waits=[sV.at()],
             inc=sL, inc_n=16)
        s.op("sync", lambda sy: sy.dma_start(out=s.cnt_in[1:2, :], in_=segr[0:1, :]), waits=[sV.at()],
             inc=sL, inc_n=16)
        s.op("gpsimd", lambda g: g.collective_compute(
            "AllGather", ALU.bypass, replica_groups=GROUPS_ALL,
            ins=[s.cnt_in.ap().opt()], outs=[s.cnt_out.ap().opt()]), waits=[sL.at()], inc=sC)
        cnt_ag = [sC.at()]

        for t in range(NT):
            bi = 6 + t % 2
            pb, bw = s.bank(bi, extra=[sV.at()])
            s.op("tensor", lambda te, pb=pb, t=t:
                 te.transpose(pb[0:16, 0:128], s.masks[:, t * NE:(t + 1) * NE], s.idt_f[:]), waits=bw, inc=sT)
            s.op("vector", lambda v, pb=pb, t=t: v.tensor_copy(masket[0:16, t * 128:(t + 1) * 128],
                                                               pb[0:16, 0:128]), waits=[sT.at()], inc=sV)
            s.bdone(bi, [sV.at()])
        s.op("vector", lambda v: v.tensor_tensor_scan(cumet[0:16, :], masket[0:16, :], zros[0:16, :], 0.0,
                                                      ALU.add, ALU.add), waits=[sV.at()], inc=sV)
        cum_done = [sV.at()]
        pb2, bw2 = s.bank(2, extra=[sV.at()])
        s.op("tensor", lambda te, pb2=pb2: te.matmul(pb2[:, 0:16], s.ones1r[:], segr[0:1, :],
                                                     start=True, stop=True), waits=bw2, inc=sT)
        segb_mm = [sT.at()]
        for t in range(NT):
            bi = 6 + t % 2
            pb, bw = s.bank(bi, extra=cum_done)
            s.op("tensor", lambda te, pb=pb, t=t:
                 te.transpose(pb[:, 0:16], cumet[0:16, t * 128:(t + 1) * 128], s.idt_f[0:16, 0:16]),
                 waits=bw, inc=sT)
            trm = [sT.at()]
            s.op("vector", lambda v, pb=pb, t=t: v.tensor_mul(tmp16, s.masks[:, t * NE:(t + 1) * NE],
                                                              pb[:, 0:16]), waits=trm, inc=sV)
            s.bdone(bi, [sV.at()])
            s.op("vector", lambda v, t=t: v.reduce_sum(lrt[:, t:t + 1], tmp16, axis=AX), waits=[sV.at()], inc=sV)
            s.op("vector", lambda v, t=t: v.tensor_scalar(lrt[:, t:t + 1], lrt[:, t:t + 1], -1.0, None, ALU.add),
                 waits=[sV.at()], inc=sV)
            s.op("vector", lambda v, t=t: v.tensor_mul(tmp16, s.masks[:, t * NE:(t + 1) * NE], pb2[:, 0:16]),
                 waits=segb_mm + [sV.at()], inc=sV)
            s.op("vector", lambda v, t=t: v.reduce_sum(c1, tmp16, axis=AX), waits=[sV.at()], inc=sV)
            s.op("vector", lambda v, t=t: v.tensor_add(c2, c1, lrt[:, t:t + 1]), waits=[sV.at()], inc=sV)
            s.op("vector", lambda v, t=t: v.tensor_copy(s.idx_sc[:, t:t + 1], c2), waits=[sV.at()], inc=sV)
            s.op("gpsimd", lambda g, t=t: g.indirect_dma_start(
                out=s.cmp_in[:, :], out_offset=bass.IndirectOffsetOnAxis(ap=s.idx_sc[:, t:t + 1], axis=0),
                in_=s.arb(s.LNT0 + t * DM, DM), in_offset=None),
                waits=[sV.at()], inc=sG, inc_n=16)
        s.bdone(2, [sV.at()])
        s.op("gpsimd", lambda g: g.collective_compute(
            "AllGather", ALU.bypass, replica_groups=GROUPS_ALL,
            ins=[s.cmp_in.ap().opt()], outs=[s.cmp_out.ap().opt()]), waits=[sG.at()], inc=sC)
        tok_ag = [sC.at()]

        s.op("sync", lambda sy: sy.dma_start(out=s.cntT[:, :],
                                             in_=s.cnt_out.ap()[0:16:2, :].rearrange("r e -> e r"),
                                             allow_slow_non_contiguous=True),
             waits=cnt_ag, inc=sL, inc_n=16)
        s.op("sync", lambda sy: sy.dma_start(out=s.segT[:, :],
                                             in_=s.cnt_out.ap()[1:16:2, :].rearrange("r e -> e r"),
                                             allow_slow_non_contiguous=True),
             waits=cnt_ag, inc=sL, inc_n=16)
        ld2 = [sL.at()]
        s.op("vector", lambda v: v.tensor_tensor_scan(s.cumT[:, :], s.cntT[:, :], s.zrow16[:, :], 0.0,
                                                      ALU.add, ALU.add), waits=ld2, inc=sV)
        s.op("vector", lambda v: v.tensor_mul(tmp8[0:16, :], s.cntT[:, :], s.cmask8[:, :]), waits=ld2, inc=sV)
        s.op("vector", lambda v: v.reduce_sum(pcol16[0:16, :], tmp8[0:16, 0:8], axis=AX), waits=[sV.at()], inc=sV)
        pcol_done = [sV.at()]
        bi = 6
        pb, bw = s.bank(bi, extra=pcol_done)
        s.op("tensor", lambda te, pb=pb: te.transpose(pb[0:1, 0:16], pcol16[0:16, 0:1], s.idt_f[0:16, 0:16]),
             waits=bw, inc=sT)
        s.op("vector", lambda v, pb=pb: v.tensor_copy(prow[0:1, :], pb[0:1, 0:16]), waits=[sT.at()], inc=sV)
        s.bdone(bi, [sV.at()])
        pb3, bw3 = s.bank(3, extra=[sV.at()])
        s.op("tensor", lambda te, pb3=pb3: te.matmul(pb3[:, 0:16], s.ones1r[:], prow[0:1, :],
                                                     start=True, stop=True), waits=bw3, inc=sT)
        s.op("vector", lambda v, pb3=pb3: v.tensor_copy(prefb, pb3[:, 0:16]), waits=[sT.at()], inc=sV)
        s.bdone(3, [sV.at()])
        for t in range(NT):
            s.op("vector", lambda v, t=t: v.tensor_mul(tmp16, s.masks[:, t * NE:(t + 1) * NE], prefb),
                 waits=[sV.at()], inc=sV)
            s.op("vector", lambda v: v.reduce_sum(c1, tmp16, axis=AX), waits=[sV.at()], inc=sV)
            s.op("vector", lambda v, t=t: v.tensor_add(c2, c1, lrt[:, t:t + 1]), waits=[sV.at()], inc=sV)
            s.op("vector", lambda v: v.tensor_scalar(c3, c2, float(CAP), None, ALU.is_lt), waits=[sV.at()],
                 inc=sV)
            s.op("vector", lambda v, t=t: v.tensor_mul(s.gk[:, t:t + 1], s.gcol[:, t:t + 1], c3),
                 waits=[sV.at()], inc=sV)
            s.op("vector", lambda v, t=t: v.tensor_mul(tmp16, s.masks[:, t * NE:(t + 1) * NE],
                                                       s.cst[:, s.C_BASEE:s.C_BASEE + 16]), inc=sV)
            s.op("vector", lambda v: v.reduce_sum(c4, tmp16, axis=AX), waits=[sV.at()], inc=sV)
            s.op("vector", lambda v, t=t: v.tensor_mul(tmp16, s.masks[:, t * NE:(t + 1) * NE],
                                                       s.cst[:, s.C_TRASHE:s.C_TRASHE + 16]), waits=[sV.at()],
                 inc=sV)
            s.op("vector", lambda v: v.reduce_sum(c5, tmp16, axis=AX), waits=[sV.at()], inc=sV)
            s.op("vector", lambda v: v.tensor_add(c4, c4, c2), waits=[sV.at()], inc=sV)
            s.op("vector", lambda v: v.tensor_sub(c4, c4, c5), waits=[sV.at()], inc=sV)
            s.op("vector", lambda v: v.scalar_tensor_tensor(c4, c4, c3, c5, ALU.mult, ALU.add),
                 waits=[sV.at()], inc=sV)
            s.op("vector", lambda v, t=t: v.tensor_copy(s.idx_eo[:, t:t + 1], c4), waits=[sV.at()], inc=sV)

        s.op("gpsimd", lambda g: g.iota(rcol[:, 0:10], pattern=[[128, 10]], base=0, channel_multiplier=1,
                                        allow_small_or_imprecise_dtypes=True), inc=s.sP)
        rdy = [s.sP.at(), sV.at()]
        pb4, bw4 = s.bank(4, extra=rdy)
        for el in range(2):
            s.op("tensor", lambda te, pb4=pb4, el=el:
                 te.matmul(pb4[0:1, el * 8:el * 8 + 8], s.myexp_s[:, el:el + 1], s.cumT[:, :],
                           start=(el == 0), stop=False), waits=bw4 if el == 0 else (), inc=None)
            s.op("tensor", lambda te, pb4=pb4, el=el:
                 te.matmul(pb4[0:1, 16 + el * 8:16 + el * 8 + 8], s.myexp_s[:, el:el + 1], s.segT[:, :],
                           start=False, stop=(el == 1)), inc=sT)
        s.op("vector", lambda v, pb4=pb4: v.tensor_copy(mcum[0:1, 0:8], pb4[0:1, 0:8]), waits=[sT.at()], inc=sV)
        s.op("vector", lambda v, pb4=pb4: v.tensor_copy(mcum[0:1, 8:16], pb4[0:1, 8:16]), inc=sV)
        s.op("vector", lambda v, pb4=pb4: v.tensor_copy(mseg[0:1, 0:8], pb4[0:1, 16:24]), inc=sV)
        s.op("vector", lambda v, pb4=pb4: v.tensor_copy(mseg[0:1, 8:16], pb4[0:1, 24:32]), inc=sV)
        s.bdone(4, [sV.at()])
        for el in range(2):
            pb5, bw5 = s.bank(5, extra=[sV.at()])
            s.op("tensor", lambda te, pb5=pb5, el=el:
                 te.matmul(pb5[:, 0:8], s.ones1r[:], mcum[0:1, el * 8:el * 8 + 8],
                           start=True, stop=False), waits=bw5, inc=sT)
            s.op("tensor", lambda te, pb5=pb5, el=el:
                 te.matmul(pb5[:, 8:16], s.ones1r[:], mseg[0:1, el * 8:el * 8 + 8],
                           start=False, stop=True), inc=sT)
            s.op("vector", lambda v, pb5=pb5: v.tensor_copy(cumb, pb5[:, 0:8]), waits=[sT.at()], inc=sV)
            s.op("vector", lambda v, pb5=pb5: v.tensor_copy(segb, pb5[:, 8:16]), waits=[sT.at()], inc=sV)
            s.bdone(5, [sV.at()])
            for st in range(10):
                r_ = rcol[:, st:st + 1]
                s.op("vector", lambda v, r_=r_: v.tensor_scalar(tmp8, cumb, r_, None, ALU.is_le),
                     waits=[sV.at()], inc=sV)
                s.op("vector", lambda v: v.reduce_sum(c1, tmp8, axis=AX), waits=[sV.at()], inc=sV)
                s.op("vector", lambda v: v.tensor_mul(tmp8, tmp8, cumb), waits=[sV.at()], inc=sV)
                s.op("vector", lambda v: v.reduce_max(c2, tmp8, axis=AX), waits=[sV.at()], inc=sV)
                s.op("vector", lambda v: v.tensor_scalar(tmp8, s.cst[:, s.C_IOTA8:s.C_IOTA8 + 8], c1, None,
                                                         ALU.is_equal), waits=[sV.at()], inc=sV)
                s.op("vector", lambda v: v.tensor_mul(tmp8, tmp8, segb), waits=[sV.at()], inc=sV)
                s.op("vector", lambda v: v.reduce_sum(c3, tmp8, axis=AX), waits=[sV.at()], inc=sV)
                s.op("vector", lambda v: v.tensor_scalar(c4, c1, float(T), None, ALU.mult), waits=[sV.at()], inc=sV)
                s.op("vector", lambda v: v.tensor_add(c4, c4, c3), waits=[sV.at()], inc=sV)
                s.op("vector", lambda v, r_=r_: v.tensor_add(c4, c4, r_), waits=[sV.at()], inc=sV)
                s.op("vector", lambda v: v.tensor_sub(c4, c4, c2), waits=[sV.at()], inc=sV)
                s.op("vector", lambda v: v.tensor_scalar_min(c4, c4, float(NCORE * T - 1)), waits=[sV.at()], inc=sV)
                s.op("vector", lambda v, el=el, st=st: v.tensor_copy(s.dgi[:, el * 10 + st:el * 10 + st + 1], c4),
                     waits=[sV.at()], inc=sV)
        dgi_done = [sV.at()]
        self.fence()

        for el in range(2):
            for st in range(10):
                gt = s.arb(GT0 + (st % 2) * 1024, 1024)
                s.op("gpsimd", lambda g, el=el, st=st, gt=gt: g.indirect_dma_start(
                    out=gt, out_offset=None, in_=s.cmp_out[:, :],
                    in_offset=bass.IndirectOffsetOnAxis(ap=s.dgi[:, el * 10 + st:el * 10 + st + 1], axis=0)),
                    waits=[sS.at()], inc=sG, inc_n=16)
                gw_ = [sG.at()]
                for dc in range(NDC):
                    bi = 6 + dc % 2
                    pb, bw = s.bank(bi, extra=gw_)
                    s.op("tensor", lambda te, pb=pb, gt=gt, dc=dc:
                         te.transpose(pb[:, 0:64].bitcast(BF16), gt[:, dc * 128:(dc + 1) * 128], s.idt_b[:]),
                         waits=bw, inc=sT)
                    s.op("scalar", lambda sc, pb=pb, dc=dc, st=st:
                         sc.copy(s.arb(DSP0 + dc * CAP + st * 128, 128), pb[:, 0:64].bitcast(BF16)),
                         waits=[sT.at()], inc=sS)
                    s.bdone(bi, [sS.at()])
            disp_done = [sS.at()]
            if self.dbg and self.moe_calls == 0 and el == 0:
                s.op("sync", lambda sy: sy.dma_start(out=s.dbg_disp[:, :], in_=s.arb(DSP0, NDC * CAP)),
                     waits=disp_done, inc=sL, inc_n=16)
                s.op("gpsimd", lambda g: None, waits=[sL.at()])
            burst_end = {}
            mmdone = {}

            def w1_burst(b):
                for fc in range(4 * b, 4 * b + 4):
                    prev = [mmdone[b - 2]] if (b >= 2 and fc == 4 * b) else []
                    s.op("gpsimd", lambda g, el=el, fc=fc:
                         g.dma_start(out=s.arb(W1B + (fc % 8) * 1024, 1024), in_=s.w1_m[el, fc, :, :]),
                         waits=(disp_done if b == 0 and fc == 0 else []) + prev, inc=sG, inc_n=16)
                burst_end[b] = sG.at()
                s.op("gpsimd", lambda g: None, waits=[burst_end[b]])
            w1_burst(0)
            w1_burst(1)
            for b in range(8):
                for fc in range(4 * b, 4 * b + 4):
                    wb = s.arb(W1B + (fc % 8) * 1024, 1024)
                    for sl, (c0, w) in enumerate(((0, 512), (512, 512), (1024, 256))):
                        bi = (fc * 3 + sl) % 4 + 4
                        pb, bw = s.bank(bi, extra=[burst_end[b]] + (disp_done if fc == 0 else []))
                        for dc in range(NDC):
                            s.op("tensor", lambda te, pb=pb, wb=wb, dc=dc, c0=c0, w=w:
                                 te.matmul(pb[:, 0:w], wb[:, dc * 128:(dc + 1) * 128],
                                           s.arb(DSP0 + dc * CAP + c0, w), start=(dc == 0), stop=(dc == NDC - 1)),
                                 waits=bw if dc == 0 else (), inc=sT if dc == NDC - 1 else None)
                        s.op("scalar", lambda sc, pb=pb, fc=fc, c0=c0, w=w:
                             sc.activation(s.arb(HS0 + fc * CAP + c0, w), pb[:, 0:w], AF.Gelu_apprx_tanh),
                             waits=[sT.at()], inc=sS)
                        s.bdone(bi, [sS.at()])
                mmdone[b] = sT.at()
                if b + 2 <= 7:
                    w1_burst(b + 2)
            h_done = [sS.at()]
            if self.dbg and self.moe_calls == 0 and el == 0:
                s.op("sync", lambda sy: sy.dma_start(out=s.dbg_h[:, :], in_=s.arb(HS0, NFC * CAP)),
                     waits=h_done, inc=sL, inc_n=16)
                s.op("tensor", lambda te: None, waits=[sL.at()])
                s.op("gpsimd", lambda g: None, waits=[sL.at()])
            eosb = s.arf(EOSF, 5120)
            w2done = {}
            w2l = {}

            def w2_load(k):
                buf = W2G0 + (k % 2) * 8192
                fg = k % 4
                for j in range(8):
                    prevw = [w2done[k - 2]] if (k >= 2 and j == 0) else []
                    s.op("gpsimd", lambda g, el=el, fg=fg, j=j, buf=buf:
                         g.dma_start(out=s.arb(buf + j * 1024, 1024),
                                     in_=s.w2_m[el, (fg * 8 + j) * 128:(fg * 8 + j + 1) * 128, :]),
                         waits=(h_done if k == 0 and j == 0 else []) + prevw, inc=sG, inc_n=16)
                w2l[k] = sG.at()
                s.op("gpsimd", lambda g: None, waits=[w2l[k]])
            w2_load(0)
            w2_load(1)
            for half in range(2):
                for fg in range(4):
                    k = half * 4 + fg
                    buf = W2G0 + (k % 2) * 8192
                    for si in range(5):
                        ss = half * 5 + si
                        for dh in range(2):
                            bi = (si * 2 + dh) % 4
                            pb, bw = s.bank(bi, extra=[w2l[k]] + (h_done if k == 0 else []))
                            for j in range(8):
                                s.op("tensor", lambda te, pb=pb, fg=fg, j=j, ss=ss, dh=dh, buf=buf:
                                     te.matmul(pb[:, 0:512], s.arb(HS0 + (fg * 8 + j) * CAP + ss * 128, 128),
                                               s.arb(buf + j * 1024 + dh * 512, 512),
                                               start=(j == 0), stop=(j == 7)),
                                     waits=bw if j == 0 else (), inc=sT if j == 7 else None)
                            dst = eosb[:, si * 1024 + dh * 512:si * 1024 + dh * 512 + 512]
                            if fg == 0:
                                s.op("vector", lambda v, pb=pb, dst=dst: v.tensor_copy(dst, pb[:, 0:512]),
                                     waits=[sT.at()], inc=sV)
                            else:
                                s.op("vector", lambda v, pb=pb, dst=dst: v.tensor_add(dst, dst, pb[:, 0:512]),
                                     waits=[sT.at()], inc=sV)
                            s.bdone(bi, [sV.at()])
                    w2done[k] = sT.at()
                    if k + 2 <= 7:
                        w2_load(k + 2)
                for si in range(5):
                    ss = half * 5 + si
                    ec = s.arb(ECST + (si % 2) * 1024, 1024)
                    s.op("vector", lambda v, si=si, ec=ec: v.tensor_copy(ec, eosb[:, si * 1024:(si + 1) * 1024]),
                         waits=[sV.at(), sL.at()], inc=sV)
                    s.op("sync", lambda sy, el=el, ss=ss, ec=ec:
                         sy.dma_start(out=s.eo_in[el * CAP + ss * 128:el * CAP + (ss + 1) * 128, :], in_=ec),
                         waits=[sV.at()], inc=sL, inc_n=16)
            self.fence()
        zb = s.arb(GT0, 1024)
        s.op("vector", lambda v, zb=zb: v.memset(zb, 0.0), inc=sV)
        s.op("sync", lambda sy, zb=zb: sy.dma_start(out=s.eo_in[2 * CAP:EO_STRIDE, :], in_=zb[0:64, :]),
             waits=[sV.at()], inc=sL, inc_n=16)
        s.op("gpsimd", lambda g: g.collective_compute(
            "AllGather", ALU.bypass, replica_groups=GROUPS_ALL,
            ins=[s.eo_in.ap().opt()], outs=[s.eo_out.ap().opt()]), waits=[sL.at()], inc=sC)
        eo_ag = [sC.at()]
        prev_pair_resid = []
        for pt in range(NT // 2):
            pair = (2 * pt, 2 * pt + 1)
            extra = [s.sL.at()] if pt == 0 else []
            for t in pair:
                s.op("gpsimd", lambda g, t=t: g.indirect_dma_start(
                    out=s.arb(GT0 + (t % 2) * 1024, 1024), out_offset=None, in_=s.eo_out[:, :],
                    in_offset=bass.IndirectOffsetOnAxis(ap=s.idx_eo[:, t:t + 1], axis=0)),
                    waits=(eo_ag if pt == 0 and t == 2 * pt else []) + extra + prev_pair_resid,
                    inc=sG, inc_n=16)
                extra = []
            pair_end = s.sG.at()
            s.op("gpsimd", lambda g: None, waits=[pair_end])
            prev_pair_resid = []
            for t in pair:
                rd2 = self.resid_done[t] if self.resid_done[t] else []
                xt = s.xb[:, t * DM:(t + 1) * DM]
                s.op("vector", lambda v, xt=xt, t=t:
                     v.scalar_tensor_tensor(xt, s.arb(GT0 + (t % 2) * 1024, 1024), s.gk[:, t:t + 1], xt,
                                            ALU.mult, ALU.add),
                     waits=[pair_end] + rd2 + s.lnf_done[("own", t)], inc=sV)
                self.resid_done[t] = [sV.at()]
                prev_pair_resid = [sV.at()]
        self.moe_calls += 1
        if self.dbg and self.moe_calls == 1:
            def dmp(dst, srcap):
                s.op("sync", lambda sy, dst=dst, srcap=srcap: sy.dma_start(out=dst, in_=srcap),
                     inc=sL, inc_n=16)
            dmp(s.dbg_dgi[:, :], s.dgi[:, :])
            dmp(s.dbg_ieo[:, :], s.idx_eo[:, :])
            dmp(s.dbg_isc[:, :], s.idx_sc[:, :])
            dmp(s.dbg_gk[:, :], s.gk[:, :])
            dmp(s.dbg_cnt[:, :], s.cntT[:, :])
            dmp(s.dbg_seg[:, :], s.segT[:, :])
            dmp(s.dbg_cmp[:, :], s.cmp_in.ap())
            dmp(s.dbg_cmpo[:, :], s.cmp_out.ap())
            dmp(s.dbg_eoin[:, :], s.eo_in.ap())
            self.fence()


    def attn_b(self):
        s = self
        sL, sG, sT, sV, sS, sC = s.sL, s.sG, s.sT, s.sV, s.sS, s.sC
        SC = s.SC0
        QF, KF, VF, VTM = SC, SC + 2304, SC + 4608, SC + 6912
        MB0 = SC + 9216
        SCB_F = SC + 15360
        EXPT_F = SC + 16128
        ATT0 = SC + 16896
        ATTT0 = SC + 17280
        IOT_F = SC + 17664
        KP_F = SC + 18432
        HXL = SC + 4096
        HXR = SC + 8192

        s.op("sync", lambda sy: sy.dma_start(out=s.halo_in[0:128, :], in_=s.xb[:, 0:DM]),
             waits=self.resid_done[0], inc=sL, inc_n=16)
        s.op("sync", lambda sy: sy.dma_start(out=s.halo_in[128:256, :], in_=s.xb[:, (NT - 1) * DM:NT * DM]),
             waits=self.resid_done[NT - 1], inc=sL, inc_n=16)
        s.op("gpsimd", lambda g: g.collective_compute(
            "AllGather", ALU.bypass, replica_groups=GROUPS_ALL,
            ins=[s.halo_in.ap().opt()], outs=[s.halo_out.ap().opt()]), waits=[sL.at()], inc=sC)
        halo_ag = [sC.at()]
        s.op("gpsimd", lambda g: g.indirect_dma_start(
            out=s.arf(HXL, DM), out_offset=None, in_=s.halo_out[:, :],
            in_offset=bass.IndirectOffsetOnAxis(ap=s.hl_idx[:, 0:1], axis=0)),
            waits=halo_ag + [s.sT.at()], inc=sG, inc_n=16)
        s.op("gpsimd", lambda g: g.indirect_dma_start(
            out=s.arf(HXR, DM), out_offset=None, in_=s.halo_out[:, :],
            in_offset=bass.IndirectOffsetOnAxis(ap=s.hr_idx[:, 0:1], axis=0)),
            waits=[s.sG.at()], inc=sG, inc_n=16)
        s.op("gpsimd", lambda g: None, waits=[s.sG.at()])
        self.lnt(halo=True, tile_waits=self.resid_done)

        iot = s.arf(IOT_F, 384)
        kp = s.arf(KP_F, 384)
        ma = s.arf(SCB_F, 384)
        mb_ = s.arf(EXPT_F, 384)
        s.op("gpsimd", lambda g: g.iota(iot, pattern=[[1, 384]], base=-128, channel_multiplier=-1,
                                        allow_small_or_imprecise_dtypes=True),
             waits=[s.sV.at()], inc=s.sP)
        iota_done = [s.sP.at()]
        for t in range(NT):
            s.op("gpsimd", lambda g, t=t: g.iota(kp, pattern=[[1, 384]], base=t * 128 - 128, channel_multiplier=0,
                                                 allow_small_or_imprecise_dtypes=True),
                 waits=[sV.at()], inc=s.sP)
            s.op("vector", lambda v: v.tensor_scalar(kp, kp, s.cst[:, s.C_P0:s.C_P0 + 1], None, ALU.add),
                 waits=[s.sP.at()], inc=sV)
            s.op("vector", lambda v: v.tensor_scalar(ma, iot, -128.0, None, ALU.is_ge),
                 waits=iota_done + [sV.at()], inc=sV)
            s.op("vector", lambda v: v.tensor_scalar(mb_, iot, 128.0, None, ALU.is_le), inc=sV)
            s.op("vector", lambda v: v.tensor_mul(ma, ma, mb_), waits=[sV.at()], inc=sV)
            s.op("vector", lambda v: v.tensor_scalar(mb_, kp, 0.0, None, ALU.is_ge), waits=[sV.at()], inc=sV)
            s.op("vector", lambda v: v.tensor_mul(ma, ma, mb_), waits=[sV.at()], inc=sV)
            s.op("vector", lambda v: v.tensor_scalar(mb_, kp, float(SEQ), None, ALU.is_lt), waits=[sV.at()], inc=sV)
            s.op("vector", lambda v: v.tensor_mul(ma, ma, mb_), waits=[sV.at()], inc=sV)
            s.op("vector", lambda v, t=t: v.tensor_scalar(s.arb(MB0 + t * 384, 384), ma, 1.0e30, -1.0e30,
                                                          ALU.mult, ALU.add),
                 waits=[sV.at()] + (s.lnf_done[("hr", 0)] + s.lnf_done[("hl", 0)] if t == 0 else []), inc=sV)
        mb_done = [sV.at()]

        for hh in range(NDC):
            s.op("gpsimd", lambda g, hh=hh: g.dma_start(out=s.arb(s.WB0 + hh * 1024, 1024),
                                                        in_=s.wproj_b[hh * 128:(hh + 1) * 128, :]),
                 waits=[s.sG.at()] if hh == 0 else (), inc=sG, inc_n=16)
        wp_ld = [sG.at()]

        TT5 = ((0, 512), (512, 512), (1024, 512), (1536, 512), (2048, 256))
        for hh in range(NDC):
            s.op("gpsimd", lambda g, hh=hh: g.dma_start(out=s.arb(SC + 19200, 0) if False else
                                                        s.arb(s.WB0 + 8192, 3072), in_=s.wqkv_b[hh, :, :]),
                 waits=[sT.at()], inc=sG, inc_n=16)
            whh_ld = [sG.at()]
            for part, dst0 in ((0, QF), (1, KF), (2, VF)):
                for it, (c0, w) in enumerate(TT5):
                    bi = it % 2
                    pb, bw = s.bank(bi, extra=whh_ld + (s.lnf_done_all if part == 0 and it == 0 else []))
                    for dc in range(NDC):
                        s.op("tensor", lambda te, pb=pb, part=part, dc=dc, c0=c0, w=w:
                             te.matmul(pb[:, 0:w], s.arb(s.WB0 + 8192 + (part * 8 + dc) * 128, 128),
                                       s.arb(s.LNF0 + dc * FM + c0, w), start=(dc == 0), stop=(dc == NDC - 1)),
                             waits=bw if dc == 0 else (), inc=sT if dc == NDC - 1 else None)
                    s.op("scalar", lambda sc, pb=pb, dst0=dst0, c0=c0, w=w:
                         sc.copy(s.arb(dst0 + c0, w), pb[:, 0:w]), waits=[sT.at()], inc=sS)
                    s.bdone(bi, [sS.at()])
            qkv_done = [sS.at()]
            for j in range(18):
                bi = 6 + j % 2
                pb, bw = s.bank(bi, extra=qkv_done)
                s.op("tensor", lambda te, pb=pb, j=j:
                     te.transpose(pb[:, 0:64].bitcast(BF16), s.arb(VF + j * 128, 128), s.idt_b[:]),
                     waits=bw, inc=sT)
                s.op("scalar", lambda sc, pb=pb, j=j: sc.copy(s.arb(VTM + j * 128, 128),
                                                              pb[:, 0:64].bitcast(BF16)),
                     waits=[sT.at()], inc=sS)
                s.bdone(bi, [sS.at()])
            vtm_done = [sS.at()]
            for t in range(NT):
                for head in range(2):
                    po = head * 64
                    qcol = 128 + t * 128
                    pb3, bw3 = s.bank(3, extra=vtm_done + mb_done)
                    s.op("tensor", lambda te, pb3=pb3, po=po, qcol=qcol, t=t:
                         te.matmul(pb3[:, 0:384], s.arb(QF + qcol, 128)[po:po + 64, :],
                                   s.arb(KF + t * 128, 384)[po:po + 64, :], start=True, stop=True),
                         waits=bw3, inc=sT)
                    scmm = [sT.at()]
                    scb = s.arf(SCB_F, 384)
                    s.op("vector", lambda v, pb3=pb3, t=t: v.tensor_add(scb, pb3[:, 0:384],
                                                                        s.arb(MB0 + t * 384, 384)),
                         waits=scmm, inc=sV)
                    s.bdone(3, [sV.at()])
                    mx = s.csm[:, 60:61]
                    nmx = s.csm[:, 61:62]
                    sume = s.csm[:, 62:63]
                    s.op("vector", lambda v, mx=mx: v.reduce_max(mx, scb, axis=AX), waits=[sV.at()], inc=sV)
                    s.op("vector", lambda v, mx=mx, nmx=nmx: v.tensor_scalar(nmx, mx, -0.125, None, ALU.mult),
                         waits=[sV.at()], inc=sV)
                    expt = s.arf(EXPT_F, 384)
                    s.op("scalar", lambda sc, nmx=nmx, expt=expt:
                         sc.activation(expt, scb, AF.Exp, bias=nmx, scale=0.125, accum_out=sume),
                         waits=[sV.at()], inc=sS)
                    s.op("vector", lambda v, sume=sume: v.reciprocal(s.csm[:, 63:64], sume), waits=[sS.at()],
                         inc=sV)
                    att = s.arb(ATT0, 384)
                    s.op("vector", lambda v, att=att, expt=expt:
                         v.tensor_scalar(att, expt, s.csm[:, 63:64], None, ALU.mult), waits=[sV.at()], inc=sV)
                    att_done = [sV.at()]
                    for j3 in range(3):
                        bi = 6 + j3 % 2
                        pb, bw = s.bank(bi, extra=att_done)
                        s.op("tensor", lambda te, pb=pb, att=att, j3=j3:
                             te.transpose(pb[:, 0:64].bitcast(BF16), att[:, j3 * 128:(j3 + 1) * 128], s.idt_b[:]),
                             waits=bw, inc=sT)
                        s.op("scalar", lambda sc, pb=pb, j3=j3: sc.copy(s.arb(ATTT0 + j3 * 128, 128),
                                                                        pb[:, 0:64].bitcast(BF16)),
                             waits=[sT.at()], inc=sS)
                        s.bdone(bi, [sS.at()])
                    attT_done = [sS.at()]
                    pb4, bw4 = s.bank(4, extra=attT_done)
                    for j3 in range(3):
                        s.op("tensor", lambda te, pb4=pb4, j3=j3, t=t, po=po, head=head:
                             te.matmul(pb4[po:po + 64, 0:128], s.arb(VTM + (t + j3) * 128 + po, 64),
                                       s.arb(ATTT0 + j3 * 128, 128),
                                       start=(j3 == 0), stop=(j3 == 2),
                                       tile_position=(0, po)),
                             waits=bw4 if j3 == 0 else (), inc=sT if j3 == 2 else None)
                    avmm = [sT.at()]
                    if head == 1:
                        s.op("scalar", lambda sc, pb4=pb4, hh=hh, t=t:
                             sc.copy(s.arb(s.LNT0 + hh * 2048 + t * 128, 128), pb4[:, 0:128]),
                             waits=avmm, inc=sS)
                        s.bdone(4, [sS.at()])
            self.fence()
        for tg in range(4):
            banks = []
            bw_all = [sV.at(), sS.at()]
            for fj in range(NDC):
                pb, bw = s.bank(fj, extra=bw_all)
                banks.append(pb)
                for hh in range(NDC):
                    s.op("tensor", lambda te, pb=pb, fj=fj, hh=hh, tg=tg:
                         te.matmul(pb[:, 0:512], s.arb(s.WB0 + hh * 1024 + fj * 128, 128),
                                   s.arb(s.LNT0 + hh * 2048 + tg * 512, 512), start=(hh == 0), stop=(hh == NDC - 1)),
                         waits=(bw + [s.sG.at()]) if hh == 0 else (), inc=sT if hh == NDC - 1 else None)
            pr_mm = [sT.at()]
            for fj in range(NDC):
                osb = s.arf(SCB_F, 512)
                s.op("scalar", lambda sc, fj=fj, osb=osb:
                     sc.activation(osb, banks[fj][:, 0:512], AF.Identity, bias=s.bp_b[:, fj:fj + 1]),
                     waits=pr_mm if fj == 0 else [sV.at()], inc=sS)
                ev = [sS.at()]
                for i in range(4):
                    t2 = tg * 4 + i
                    pb2, bw2 = s.bank(fj, extra=ev)
                    s.op("tensor", lambda te, pb2=pb2, osb=osb, i=i:
                         te.transpose(pb2[:, 0:128], osb[:, i * 128:(i + 1) * 128], s.idt_f[:]),
                         waits=bw2, inc=sT)
                    dst = s.xb[:, t2 * DM + fj * 128:t2 * DM + (fj + 1) * 128]
                    s.op("vector", lambda v, pb2=pb2, dst=dst: v.tensor_add(dst, dst, pb2[:, 0:128]),
                         waits=[sT.at()], inc=sV)
                    s.bdone(fj, [sV.at()])
                    if fj == NDC - 1:
                        self.resid_done[t2] = [sV.at()]

    def store(self):
        s = self
        for t in range(NT):
            s.op("sync", lambda sy, t=t: sy.dma_start(out=s.y_out[t * 128:(t + 1) * 128, :],
                                                      in_=s.xb[:, t * DM:(t + 1) * DM]),
                 waits=self.resid_done[t], inc=s.sL, inc_n=16)
        s.op("sync", lambda sy: None, waits=[s.sL.at()])

    def emit(self, block):
        @block.sync
        def _(sy):
            for f in self.ops["sync"]:
                f(sy)

        @block.tensor
        def _(te):
            for f in self.ops["tensor"]:
                f(te)

        @block.vector
        def _(v):
            for f in self.ops["vector"]:
                f(v)

        @block.scalar
        def _(sc):
            for f in self.ops["scalar"]:
                f(sc)

        @block.gpsimd
        def _(g):
            for f in self.ops["gpsimd"]:
                f(g)


def build(n_sub=4):
    nc = bass.Bass("TRN2", target_bir_lowering=False)
    mk = MK(nc, n_sub)
    with ExitStack() as ctx:
        mk.declare(ctx)
        block = ctx.enter_context(nc.Block())
        mk.p0()
        mk.resid_done = [[] for _ in range(NT)]
        if n_sub >= 1:
            mk.attn_a()
        if n_sub >= 2:
            mk.moe()
        if n_sub >= 3:
            mk.attn_b()
        if n_sub >= 4:
            mk.moe()
        mk.store()
        mk.emit(block)
    return nc


def kernel(**inputs):
    maps = shard_inputs(inputs)
    nc = build(4)
    res = run_bass_kernel_spmd(nc, maps, core_ids=list(range(NCORE)))
    y = np.concatenate([r["y_out"] for r in res.results], axis=0)
    return y.reshape(2, SEQ, DM)


# version 07
# speedup vs baseline: 1.1978x; 1.0005x over previous
import numpy as np
from contextlib import ExitStack
import concourse.bass as bass
import concourse.mybir as mybir
from concourse.bass_utils import run_bass_kernel_spmd
import ml_dtypes

F32 = mybir.dt.float32
BF16 = mybir.dt.bfloat16
I32 = mybir.dt.int32
AF = mybir.ActivationFunctionType
ALU = mybir.AluOpType
AX = mybir.AxisListType.X

NCORE = 8
T = 2048
NT = 16
DM = 1024
NDC = 8
SEQ = 8192
CAP = 1280
NE = 16
DFF = 4096
NFC = 32
EO_STRIDE = 2 * CAP + 64
FM = T + 256
LN_EPS = 1e-5
ATT_EPS = 1e-4
GROUPS_ALL = [list(range(NCORE))]
GROUPS_BATCH = [[0, 1, 2, 3], [4, 5, 6, 7]]


def shard_inputs(inp):
    x = np.ascontiguousarray(np.asarray(inp["L0_attn_a_x"], dtype=np.float32).reshape(2 * SEQ, DM))
    w1 = np.asarray(inp["L0_ffn_w1"], dtype=np.float32)
    w2 = np.asarray(inp["L0_ffn_w2"], dtype=np.float32)
    w1r = np.ascontiguousarray(w1.reshape(NE, NDC, 128, NFC, 128).transpose(0, 3, 2, 1, 4)
                               .reshape(NE, NFC, 128, NDC * 128))
    gate_wT = np.ascontiguousarray(np.asarray(inp["L0_ffn_gate_w"], dtype=np.float32).T)
    wqkv_b_hm = np.ascontiguousarray(
        np.asarray(inp["L0_attn_b_w_qkv"], dtype=np.float32)
        .reshape(NDC, 128, 3, NDC, 128).transpose(3, 1, 2, 0, 4).reshape(NDC, 128, 3 * DM))
    maps = []
    for c in range(NCORE):
        coremask8 = np.zeros((16, 8), np.float32)
        coremask8[:, :c] = 1.0
        myexp = np.zeros((16, 2), np.float32)
        myexp[2 * c, 0] = 1.0
        myexp[2 * c + 1, 1] = 1.0
        li = np.arange(128, dtype=np.int32)
        hleft = ((c - 1) * 256 + 128 + li) if c % 4 > 0 else (c * 256 + li)
        hright = ((c + 1) * 256 + li) if c % 4 < 3 else (c * 256 + 128 + li)
        maps.append(dict(
            x_in=np.ascontiguousarray(x[c * T:(c + 1) * T]),
            wqkv_a=np.asarray(inp["L0_attn_a_W_qkv"], dtype=np.float32),
            bqkv_a=np.ascontiguousarray(np.asarray(inp["L0_attn_a_b_qkv"], dtype=np.float32).reshape(1, 3 * DM)),
            bq_a_in=np.ascontiguousarray(np.asarray(inp["L0_attn_a_b_qkv"], dtype=np.float32)[:DM].reshape(8, 128).T),
            wout_a=np.asarray(inp["L0_attn_a_W_out"], dtype=np.float32),
            bout_a=np.ascontiguousarray(np.asarray(inp["L0_attn_a_b_out"], dtype=np.float32).reshape(8, 128).T),
            wqkv_b=wqkv_b_hm,
            wproj_b=np.asarray(inp["L0_attn_b_w_proj"], dtype=np.float32),
            bproj_b=np.ascontiguousarray(np.asarray(inp["L0_attn_b_b_proj"], dtype=np.float32).reshape(8, 128).T),
            gate_wT=gate_wT,
            w1_m=np.ascontiguousarray(w1r[2 * c:2 * c + 2]),
            w2_m=np.ascontiguousarray(w2[2 * c:2 * c + 2]),
            coremask8=coremask8,
            myexp=myexp,
            p0col=np.full((128, 1), float((c % 4) * T), np.float32),
            hleft_idx=hleft.reshape(128, 1).astype(np.int32),
            hright_idx=hright.reshape(128, 1).astype(np.int32),
        ))
    return maps


class Sem:
    def __init__(self, h):
        self.h = h
        self.v = 0

    def at(self):
        return (self, self.v)


class MK:

    def __init__(self, nc, n_sub=4):
        self.nc = nc
        self.n_sub = n_sub
        self.ops = {k: [] for k in ("sync", "tensor", "vector", "scalar", "gpsimd")}
        self.bfree = [[] for _ in range(8)]

    def op(self, engine, fn, waits=(), inc=None, inc_n=1):
        if inc is not None:
            inc.v += inc_n
        ws = tuple((s.h, v) for (s, v) in waits if v > 0)

        def run(eng, fn=fn, ws=ws, inc=inc, inc_n=inc_n):
            for h, v in ws:
                eng.wait_ge(h, v)
            ins = fn(eng)
            if inc is not None:
                ins.then_inc(inc.h, inc_n)
        self.ops[engine].append(run)

    def snap(self):
        return [s.at() for s in self.allsems]

    def fence(self):
        sn = self.snap()
        for e in ("sync", "tensor", "vector", "scalar", "gpsimd"):
            self.op(e, lambda eng: None, waits=sn)
        for i in range(8):
            self.bfree[i] = []

    def arb(self, c0, n):
        assert c0 >= 0 and c0 + n <= 70656, (c0, n)
        return self.AR[:, c0:c0 + n]

    def arf(self, c0, n):
        assert c0 % 2 == 0
        return self.arb(c0, 2 * n).bitcast(F32)

    def declare(self, ctx):
        nc = self.nc
        dt = nc.dram_tensor
        self.x_in = dt("x_in", [T, DM], F32, kind="ExternalInput")
        self.wqkv_a = dt("wqkv_a", [DM, 3 * DM], F32, kind="ExternalInput")
        self.bqkv_a = dt("bqkv_a", [1, 3 * DM], F32, kind="ExternalInput")
        self.bq_a_in = dt("bq_a_in", [128, 8], F32, kind="ExternalInput")
        self.wout_a = dt("wout_a", [DM, DM], F32, kind="ExternalInput")
        self.bout_a = dt("bout_a", [128, 8], F32, kind="ExternalInput")
        self.wqkv_b = dt("wqkv_b", [NDC, 128, 3 * DM], F32, kind="ExternalInput")
        self.wproj_b = dt("wproj_b", [DM, DM], F32, kind="ExternalInput")
        self.bproj_b = dt("bproj_b", [128, 8], F32, kind="ExternalInput")
        self.gate_wT_d = dt("gate_wT", [DM, NE], F32, kind="ExternalInput")
        self.w1_m = dt("w1_m", [2, NFC, 128, NDC * 128], F32, kind="ExternalInput")
        self.w2_m = dt("w2_m", [2, DFF, DM], F32, kind="ExternalInput")
        self.coremask8_d = dt("coremask8", [16, 8], F32, kind="ExternalInput")
        self.myexp_d = dt("myexp", [16, 2], F32, kind="ExternalInput")
        self.p0col_d = dt("p0col", [128, 1], F32, kind="ExternalInput")
        self.hleft_d = dt("hleft_idx", [128, 1], I32, kind="ExternalInput")
        self.hright_d = dt("hright_idx", [128, 1], I32, kind="ExternalInput")
        self.y_out = dt("y_out", [T, DM], F32, kind="ExternalOutput")
        import os
        self.dbg = bool(os.environ.get("MKDEBUG"))
        self.moe_calls = 0
        if self.dbg:
            self.dbg_dgi = dt("dbg_dgi", [128, 20], I32, kind="ExternalOutput")
            self.dbg_ieo = dt("dbg_ieo", [128, NT], I32, kind="ExternalOutput")
            self.dbg_isc = dt("dbg_isc", [128, NT], I32, kind="ExternalOutput")
            self.dbg_gk = dt("dbg_gk", [128, NT], F32, kind="ExternalOutput")
            self.dbg_cnt = dt("dbg_cnt", [16, 8], F32, kind="ExternalOutput")
            self.dbg_seg = dt("dbg_seg", [16, 8], F32, kind="ExternalOutput")
            self.dbg_cmp = dt("dbg_cmp", [T, DM], BF16, kind="ExternalOutput")
            self.dbg_cmpo = dt("dbg_cmpo", [NCORE * T, DM], BF16, kind="ExternalOutput")
            self.dbg_eoin = dt("dbg_eoin", [EO_STRIDE, DM], BF16, kind="ExternalOutput")
            self.dbg_disp = dt("dbg_disp", [128, NDC * CAP], BF16, kind="ExternalOutput")
            self.dbg_h = dt("dbg_h", [128, NFC * CAP], BF16, kind="ExternalOutput")
        self.kv_ar_in = dt("kv_ar_in", [128, 2080], F32)
        self.kv_ar_out = dt("kv_ar_out", [128, 2080], F32)
        self.cnt_in = dt("cnt_in", [2, 16], F32)
        self.cnt_out = dt("cnt_out", [16, 16], F32)
        self.cmp_in = dt("cmp_in", [T, DM], BF16)
        self.cmp_out = dt("cmp_out", [NCORE * T, DM], BF16)
        self.eo_in = dt("eo_in", [EO_STRIDE, DM], BF16)
        self.eo_out = dt("eo_out", [NCORE * EO_STRIDE, DM], BF16)
        self.halo_in = dt("halo_in", [256, DM], F32)
        self.halo_out = dt("halo_out", [NCORE * 256, DM], F32)
        it = nc.inline_tensor
        self.c_idf = it(np.eye(128, dtype=np.float32), name="c_idf")
        self.c_idb = it(np.eye(128, dtype=np.float32).astype(ml_dtypes.bfloat16), name="c_idb")
        self.c_1b = it(np.ones((128, 1), dtype=ml_dtypes.bfloat16), name="c_1b")
        self.c_1f = it(np.ones((128, 1), dtype=np.float32), name="c_1f")
        self.c_1r = it(np.ones((1, 128), dtype=np.float32), name="c_1r")
        base_e = np.array([(e // 2) * EO_STRIDE + (e % 2) * CAP for e in range(NE)], np.float32)
        trash_e = np.array([(e // 2) * EO_STRIDE + 2 * CAP for e in range(NE)], np.float32)
        self.c_base = it(np.broadcast_to(base_e, (128, NE)).copy(), name="c_base")
        self.c_trash = it(np.broadcast_to(trash_e, (128, NE)).copy(), name="c_trash")
        self.c_io16 = it(np.broadcast_to(np.arange(16, dtype=np.float32), (128, 16)).copy(), name="c_io16")
        self.c_io8 = it(np.broadcast_to(np.arange(8, dtype=np.float32), (128, 8)).copy(), name="c_io8")
        b2 = np.zeros((2, 128), np.float32)
        b2[0, :64] = 1.0
        b2[1, 64:] = 1.0
        self.c_b2 = it(b2.astype(ml_dtypes.bfloat16), name="c_b2")

        en = ctx.enter_context
        sbt = lambda name, shape, d=F32: en(nc.sbuf_tensor(name, shape, d))
        self.xb = sbt("xb", [128, NT * DM])
        self.AR = sbt("AR", [128, 70656], BF16)
        self.csm = sbt("csm", [128, 64])
        self.idt_f = sbt("idt_f", [128, 128])
        self.idt_b = sbt("idt_b", [128, 128], BF16)
        self.ones_b = sbt("ones_b", [128, 1], BF16)
        self.ones_f = sbt("ones_f", [128, 1])
        self.ones1r = sbt("ones1r", [1, 128])
        self.b2h = sbt("b2h", [2, 128], BF16)
        self.cst = sbt("cst", [128, 64])
        self.cmask8 = sbt("cmask8", [16, 8])
        self.myexp_s = sbt("myexp_s", [16, 2])
        self.cntT = sbt("cntT", [16, 8])
        self.segT = sbt("segT", [16, 8])
        self.cumT = sbt("cumT", [16, 8])
        self.zrow16 = sbt("zrow16", [16, 8])
        self.gwT = sbt("gwT", [128, NDC * NE], BF16)
        self.masks = sbt("masks", [128, NT * NE])
        self.gcol = sbt("gcol", [128, NT])
        self.idx_sc = sbt("idx_sc", [128, NT], I32)
        self.idx_eo = sbt("idx_eo", [128, NT], I32)
        self.gk = sbt("gk", [128, NT])
        self.hl_idx = sbt("hl_idx", [128, 1], I32)
        self.hr_idx = sbt("hr_idx", [128, 1], I32)
        self.bq_a = sbt("bq_a", [128, 8])
        self.bo_a = sbt("bo_a", [128, 8])
        self.bp_b = sbt("bp_b", [128, 8])
        self.dgi = sbt("dgi", [128, 20], I32)
        self.psum = [en(nc.psum_tensor(f"pb{i}", [128, 512], F32)) for i in range(8)]
        self.allsems = []
        for nm in ("sL", "sG", "sT", "sV", "sS", "sC", "sP"):
            s = Sem(en(nc.semaphore(nm)))
            setattr(self, nm, s)
            self.allsems.append(s)
        self.C_EPSLN, self.C_P0, self.C_HPI = 0, 2, 3
        self.C_BASEE, self.C_TRASHE, self.C_IOTA16, self.C_IOTA8 = 8, 24, 40, 56
        self.LNT0 = 0
        self.LNF0 = 16384
        self.WB0 = 34816
        self.SC0 = 51200
        self.HS0 = 0
        self.W2G0 = 40960
        self.EOS0 = 49152

    def bank(self, i, extra=()):
        w = list(self.bfree[i]) + list(extra)
        self.bfree[i] = []
        return self.psum[i], w

    def bdone(self, i, waits):
        self.bfree[i] = list(waits)

    def p0(self):
        s = self
        sL, sG, sV = s.sL, s.sG, s.sV

        def d(dst, src):
            s.op("sync", lambda sy, dst=dst, src=src: sy.dma_start(out=dst, in_=src), inc=sL, inc_n=16)
        d(s.idt_f[:], s.c_idf[:, :])
        d(s.idt_b[:], s.c_idb[:, :])
        d(s.ones_b[:], s.c_1b[:, :])
        d(s.ones_f[:], s.c_1f[:, :])
        d(s.ones1r[:], s.c_1r[:, :])
        d(s.b2h[:], s.c_b2[:, :])
        d(s.cst[:, s.C_P0:s.C_P0 + 1], s.p0col_d[:, :])
        d(s.cst[:, s.C_BASEE:s.C_BASEE + 16], s.c_base[:, :])
        d(s.cst[:, s.C_TRASHE:s.C_TRASHE + 16], s.c_trash[:, :])
        d(s.cst[:, s.C_IOTA16:s.C_IOTA16 + 16], s.c_io16[:, :])
        d(s.cst[:, s.C_IOTA8:s.C_IOTA8 + 8], s.c_io8[:, :])
        d(s.cmask8[:], s.coremask8_d[:, :])
        d(s.myexp_s[:], s.myexp_d[:, :])
        d(s.hl_idx[:], s.hleft_d[:, :])
        d(s.hr_idx[:], s.hright_d[:, :])
        d(s.bq_a[:], s.bq_a_in[:, :])
        d(s.bo_a[:], s.bout_a[:, :])
        d(s.bp_b[:], s.bproj_b[:, :])
        for t in range(NT):
            d(s.xb[:, t * DM:(t + 1) * DM], s.x_in[t * 128:(t + 1) * 128, :])
        for dc in range(NDC):
            s.op("gpsimd", lambda g, dc=dc: g.dma_start(out=s.gwT[:, dc * NE:(dc + 1) * NE],
                                                        in_=s.gate_wT_d[dc * 128:(dc + 1) * 128, :]),
                 inc=sG, inc_n=16)
        s.op("vector", lambda v: v.memset(s.cst[:, s.C_EPSLN:s.C_EPSLN + 1], LN_EPS), inc=sV)
        s.op("vector", lambda v: v.memset(s.cst[:, s.C_HPI:s.C_HPI + 1], float(np.pi / 2)), inc=sV)
        s.op("vector", lambda v: v.memset(s.zrow16[:], 0.0), inc=sV)
        self.fence()

    def lnt(self, halo=False, tile_waits=None):
        s = self
        sV, sS, sT = s.sV, s.sS, s.sT
        HXL = s.SC0 + 4096
        HXR = s.SC0 + 8192
        self.lnf_done = {}
        tiles = [("own", t) for t in range(NT)]
        if halo:
            tiles += [("hl", 0), ("hr", 0)]
        for kind, t in tiles:
            cc = [s.csm[:, i:i + 1] for i in range(7)]
            c_sum, c_mean, c_m2, c_var, c_sd, c_rstd, c_sumsq = cc
            if kind == "own":
                xt = s.xb[:, t * DM:(t + 1) * DM]
                tmdst = s.arb(s.LNT0 + t * DM, DM)
                fmcol = 128 + t * 128
            else:
                xt = s.arf(HXL if kind == "hl" else HXR, DM)
                tmdst = s.arb(s.SC0 + 6912 + (0 if kind == "hl" else DM), DM)
                fmcol = 0 if kind == "hl" else (T + 128)
            trash = s.arf(s.SC0, 1024)
            if tile_waits is not None and kind == "own":
                sn = list(tile_waits[t])
            else:
                sn = [sV.at(), sS.at(), s.sL.at(), s.sG.at()]
            s.op("vector", lambda v, xt=xt, c=c_sum: v.reduce_sum(c, xt, axis=AX), waits=sn, inc=sV)
            s.op("scalar", lambda sc, xt=xt, o=trash, a=c_sumsq: sc.activation(o, xt, AF.Square, accum_out=a),
                 waits=sn, inc=sS)
            s.op("vector", lambda v, i=c_sum, o=c_mean: v.tensor_scalar(o, i, 1.0 / DM, None, ALU.mult),
                 waits=[sV.at()], inc=sV)
            s.op("vector", lambda v, i=c_mean, o=c_m2: v.tensor_mul(o, i, i), waits=[sV.at()], inc=sV)
            s.op("vector", lambda v, i=c_sumsq, o=c_var: v.tensor_scalar(o, i, 1.0 / DM, None, ALU.mult),
                 waits=[sS.at()], inc=sV)
            s.op("vector", lambda v, o=c_var, a=c_m2: v.tensor_sub(o, o, a), waits=[sV.at()], inc=sV)
            s.op("scalar", lambda sc, i=c_var, o=c_sd:
                 sc.activation(o, i, AF.Sqrt, bias=s.cst[:, s.C_EPSLN:s.C_EPSLN + 1]),
                 waits=[sV.at()], inc=sS)
            s.op("vector", lambda v, i=c_sd, o=c_rstd: v.reciprocal(o, i), waits=[sS.at()], inc=sV)
            c_nmr = s.csm[:, 7:8]
            s.op("vector", lambda v, m=c_mean, r=c_rstd, o=c_nmr:
                 v.scalar_tensor_tensor(o, m, -1.0, r, ALU.mult, ALU.mult), waits=[sV.at()], inc=sV)
            s.op("scalar", lambda sc, xt=xt, r=c_rstd, b=c_nmr, o=tmdst:
                 sc.activation(o, xt, AF.Identity, bias=b, scale=r), waits=[sV.at()], inc=sS)
            lnt_done = sS.at()
            for dc in range(NDC):
                bi = 6 + (dc % 2)
                pb, bw = s.bank(bi, extra=[lnt_done])
                s.op("tensor", lambda te, pb=pb, src=tmdst[:, dc * 128:(dc + 1) * 128]:
                     te.transpose(pb[:, 0:64].bitcast(BF16), src, s.idt_b[:]), waits=bw, inc=sT)
                s.op("scalar", lambda sc, pb=pb, dst=s.arb(s.LNF0 + dc * FM + fmcol, 128):
                     sc.copy(dst, pb[:, 0:64].bitcast(BF16)), waits=[sT.at()], inc=sS)
                s.bdone(bi, [sS.at()])
            self.lnf_done[(kind, t)] = [sS.at(), sV.at()]
        self.lnf_done_all = [sS.at(), sV.at(), sT.at()]

    def attn_a(self):
        s = self
        sL, sG, sT, sV, sS, sC = s.sL, s.sG, s.sT, s.sV, s.sS, s.sC
        SC = s.SC0
        COS0, SIN0 = SC, SC + 2048
        KC0, KS0, VT0 = SC + 4096, SC + 5120, SC + 6144
        KT_F = SC + 7168
        KVS_F = SC + 9216
        POSR_F = SC + 13376
        self.lnt(tile_waits=getattr(self, "resid_done", None))

        SCALE = float(np.pi / 2 / SEQ)
        posr = s.arf(POSR_F, 2048)
        s.op("gpsimd", lambda g: g.iota(posr, pattern=[[1, 2048]], base=0, channel_multiplier=0,
                                        allow_small_or_imprecise_dtypes=True), inc=s.sP)
        s.op("vector", lambda v: v.tensor_scalar(posr, posr, s.cst[:, s.C_P0:s.C_P0 + 1], None, ALU.add),
             waits=[s.sP.at()], inc=sV)
        s.op("scalar", lambda sc: sc.activation(s.arb(SIN0, 2048), posr, AF.Sin, scale=SCALE),
             waits=[sV.at()], inc=sS)
        s.op("scalar", lambda sc: sc.activation(s.arb(COS0, 2048), posr, AF.Sin, scale=-SCALE,
                                                bias=s.cst[:, s.C_HPI:s.C_HPI + 1]),
             waits=[sV.at()], inc=sS)
        pc = s.csm[:, 8:24]
        cc = s.csm[:, 24:40]
        scc = s.csm[:, 40:56]
        s.op("gpsimd", lambda g: g.iota(pc, pattern=[[128, 16]], base=0, channel_multiplier=1,
                                        allow_small_or_imprecise_dtypes=True), inc=s.sP)
        s.op("vector", lambda v: v.tensor_scalar(pc, pc, s.cst[:, s.C_P0:s.C_P0 + 1], None, ALU.add),
             waits=[s.sP.at()], inc=sV)
        s.op("scalar", lambda sc: sc.activation(scc, pc, AF.Sin, scale=SCALE), waits=[sV.at()], inc=sS)
        s.op("scalar", lambda sc: sc.activation(cc, pc, AF.Sin, scale=-SCALE,
                                                bias=s.cst[:, s.C_HPI:s.C_HPI + 1]), waits=[sV.at()], inc=sS)
        trig_done = [sS.at()]
        bkv = s.arf(POSR_F, 2048)
        s.op("sync", lambda sy: sy.dma_start(out=bkv[:, 0:1024],
                                             in_=s.bqkv_a.ap()[0:1, DM:2 * DM].partition_broadcast(128)),
             waits=trig_done, inc=sL, inc_n=16)
        s.op("sync", lambda sy: sy.dma_start(out=bkv[:, 1024:2048],
                                             in_=s.bqkv_a.ap()[0:1, 2 * DM:3 * DM].partition_broadcast(128)),
             waits=trig_done, inc=sL, inc_n=16)
        bkv_ld = [sL.at()]

        for dc in range(NDC):
            s.op("gpsimd", lambda g, dc=dc: g.dma_start(out=s.arb(s.WB0 + dc * 2048, 1024),
                                                        in_=s.wqkv_a[dc * 128:(dc + 1) * 128, DM:2 * DM]),
                 inc=sG, inc_n=16)
            s.op("gpsimd", lambda g, dc=dc: g.dma_start(out=s.arb(s.WB0 + dc * 2048 + 1024, 1024),
                                                        in_=s.wqkv_a[dc * 128:(dc + 1) * 128, 2 * DM:3 * DM]),
                 inc=sG, inc_n=16)
        wkv_ld = [sG.at()]

        kvs = s.arf(KVS_F, 2080)
        s.op("vector", lambda v: v.memset(kvs, 0.0), inc=sV)
        for t in range(NT):
            mm_w = wkv_ld + trig_done + s.lnf_done[("own", t)]
            for j, wcol in enumerate((0, 512, 1024, 1536)):
                pb, bw = s.bank(j, extra=mm_w)
                for dc in range(NDC):
                    ap = s.arb(s.LNF0 + dc * FM + 128 + t * 128, 128)
                    s.op("tensor", lambda te, pb=pb, ap=ap, dc=dc, wcol=wcol:
                         te.matmul(pb[:, 0:512], ap, s.arb(s.WB0 + dc * 2048 + wcol, 512),
                                   start=(dc == 0), stop=(dc == NDC - 1)),
                         waits=bw if dc == 0 else (), inc=sT if dc == NDC - 1 else None)
            mm_done = [sT.at()]
            kt = s.arf(KT_F, 1024)
            kc = s.arb(KC0, 1024)
            ks = s.arb(KS0, 1024)
            vt = s.arb(VT0, 1024)
            s.op("vector", lambda v, kt=kt: v.tensor_add(kt[:, 0:512], s.psum[0][:, 0:512], bkv[:, 0:512]),
                 waits=mm_done + bkv_ld, inc=sV)
            s.op("vector", lambda v, kt=kt: v.tensor_add(kt[:, 512:1024], s.psum[1][:, 0:512],
                                                         bkv[:, 512:1024]), inc=sV)
            s.op("scalar", lambda sc, kt=kt: sc.activation(kt, kt, AF.Relu), waits=[sV.at()], inc=sS)
            s.op("vector", lambda v, kt=kt, kc=kc, t=t: v.tensor_scalar(kc, kt, cc[:, t:t + 1], None, ALU.mult),
                 waits=[sS.at()], inc=sV)
            s.op("vector", lambda v, kt=kt, ks=ks, t=t: v.tensor_scalar(ks, kt, scc[:, t:t + 1], None, ALU.mult),
                 inc=sV)
            s.op("vector", lambda v, vt=vt: v.tensor_add(vt[:, 0:512], s.psum[2][:, 0:512],
                                                         bkv[:, 1024:1536]), waits=mm_done, inc=sV)
            s.op("vector", lambda v, vt=vt: v.tensor_add(vt[:, 512:1024], s.psum[3][:, 0:512],
                                                         bkv[:, 1536:2048]), inc=sV)
            kcv = [sV.at()]
            for j in range(4):
                s.bdone(j, kcv)
            for hh in range(NDC):
                bi = 4 + (hh % 2)
                pb, bw = s.bank(bi, extra=kcv)
                kcs_ = kc[:, hh * 128:(hh + 1) * 128]
                kss_ = ks[:, hh * 128:(hh + 1) * 128]
                vs_ = vt[:, hh * 128:(hh + 1) * 128]
                s.op("tensor", lambda te, pb=pb, a=kcs_, b=vs_:
                     te.matmul(pb[:, 0:128], a, b, start=True, stop=False), waits=bw)
                s.op("tensor", lambda te, pb=pb, a=kcs_:
                     te.matmul(pb[:, 128:129], a, s.ones_b[:], start=False, stop=False))
                s.op("tensor", lambda te, pb=pb, a=kss_, b=vs_:
                     te.matmul(pb[:, 132:260], a, b, start=False, stop=False))
                s.op("tensor", lambda te, pb=pb, a=kss_:
                     te.matmul(pb[:, 260:261], a, s.ones_b[:], start=False, stop=True), inc=sT)
                kvmm = [sT.at()]
                dstc = kvs[:, hh * 130:hh * 130 + 129]
                dsts = kvs[:, 1040 + hh * 130:1040 + hh * 130 + 129]
                if t == 0:
                    s.op("vector", lambda v, pb=pb, d=dstc: v.tensor_copy(d, pb[:, 0:129]), waits=kvmm, inc=sV)
                    s.op("vector", lambda v, pb=pb, d=dsts: v.tensor_copy(d, pb[:, 132:261]), inc=sV)
                else:
                    s.op("vector", lambda v, pb=pb, d=dstc: v.tensor_add(d, d, pb[:, 0:129]), waits=kvmm, inc=sV)
                    s.op("vector", lambda v, pb=pb, d=dsts: v.tensor_add(d, d, pb[:, 132:261]), inc=sV)
                s.bdone(bi, [sV.at()])

        s.op("sync", lambda sy: sy.dma_start(out=s.kv_ar_in[:, :], in_=kvs), waits=[sV.at()], inc=sL, inc_n=16)
        s.op("gpsimd", lambda g: g.collective_compute(
            "AllReduce", ALU.add, replica_groups=GROUPS_BATCH,
            ins=[s.kv_ar_in.ap().opt()], outs=[s.kv_ar_out.ap().opt()]), waits=[sL.at()], inc=sC)
        s.op("sync", lambda sy: sy.dma_start(out=kvs, in_=s.kv_ar_out[:, :]), waits=[sC.at()], inc=sL, inc_n=16)
        self.fence()

        for dc in range(NDC):
            s.op("gpsimd", lambda g, dc=dc: g.dma_start(out=s.arb(s.WB0 + dc * 1024, 1024),
                                                        in_=s.wqkv_a[dc * 128:(dc + 1) * 128, 0:DM]),
                 inc=sG, inc_n=16)
            s.op("gpsimd", lambda g, dc=dc: g.dma_start(out=s.arb(s.WB0 + 8192 + dc * 1024, 1024),
                                                        in_=s.wout_a[dc * 128:(dc + 1) * 128, :]),
                 inc=sG, inc_n=16)
        w_ld = [sG.at()]

        SBD0 = SC + 4096
        NBD0 = SC + 6144
        CTXN0 = SC + 6400
        QR_F = SC + 10496
        QC0, QS0 = SC + 11520, SC + 12032
        RNB0, BCB0 = SC + 12544, SC + 13056
        OSB_F = SC + 13568
        NSB_F = SC + 14592
        for hh in range(NDC):
            for which, srcoff in ((0, 0), (1, 1040)):
                d_ = s.arb(SBD0 + hh * 256 + which * 128, 128)
                s.op("vector", lambda v, d=d_: v.memset(d, 0.0), inc=sV)
                s.op("vector", lambda v, d=d_, so=srcoff, hh=hh:
                     v.tensor_copy(d[0:64, 0:64], kvs[0:64, so + hh * 130:so + hh * 130 + 64]),
                     waits=[sV.at()], inc=sV)
                s.op("vector", lambda v, d=d_, so=srcoff, hh=hh:
                     v.tensor_copy(d[64:128, 64:128], kvs[64:128, so + hh * 130 + 64:so + hh * 130 + 128]),
                     waits=[sV.at()], inc=sV)
                n_ = s.arb(NBD0 + hh * 4 + which * 2, 2)
                s.op("vector", lambda v, d=n_: v.memset(d, 0.0), inc=sV)
                s.op("vector", lambda v, d=n_, so=srcoff, hh=hh:
                     v.tensor_copy(d[0:64, 0:1], kvs[0:64, so + hh * 130 + 128:so + hh * 130 + 129]),
                     waits=[sV.at()], inc=sV)
                s.op("vector", lambda v, d=n_, so=srcoff, hh=hh:
                     v.tensor_copy(d[64:128, 1:2], kvs[64:128, so + hh * 130 + 128:so + hh * 130 + 129]),
                     waits=[sV.at()], inc=sV)
        sbd_done = [sV.at()]

        for tg in range(4):
            for hh in range(NDC):
                pbq, bwq = s.bank(0, extra=w_ld + sbd_done + s.lnf_done_all)
                for dc in range(NDC):
                    s.op("tensor", lambda te, pbq=pbq, dc=dc, hh=hh, tg=tg:
                         te.matmul(pbq[:, 0:512], s.arb(s.WB0 + dc * 1024 + hh * 128, 128),
                                   s.arb(s.LNF0 + dc * FM + 128 + tg * 512, 512),
                                   start=(dc == 0), stop=(dc == NDC - 1)),
                         waits=bwq if dc == 0 else (), inc=sT if dc == NDC - 1 else None)
                qr = s.arf(QR_F, 512)
                s.op("scalar", lambda sc, pbq=pbq, qr=qr, hh=hh:
                     sc.activation(qr, pbq[:, 0:512], AF.Relu, bias=s.bq_a[:, hh:hh + 1]),
                     waits=[sT.at()], inc=sS)
                s.bdone(0, [sS.at()])
                qc = s.arb(QC0, 512)
                qs = s.arb(QS0, 512)
                s.op("vector", lambda v, qr=qr, qc=qc, tg=tg: v.tensor_mul(qc, qr, s.arb(COS0 + tg * 512, 512)),
                     waits=[sS.at()], inc=sV)
                s.op("vector", lambda v, qr=qr, qs=qs, tg=tg: v.tensor_mul(qs, qr, s.arb(SIN0 + tg * 512, 512)),
                     inc=sV)
                qcs = [sV.at()]
                pbc, bwc = s.bank(1, extra=qcs)
                s.op("tensor", lambda te, pbc=pbc, hh=hh, qc=qc:
                     te.matmul(pbc[:, 0:512], s.arb(SBD0 + hh * 256, 128), qc, start=True, stop=False),
                     waits=bwc)
                s.op("tensor", lambda te, pbc=pbc, hh=hh, qs=qs:
                     te.matmul(pbc[:, 0:512], s.arb(SBD0 + hh * 256 + 128, 128), qs, start=False, stop=True),
                     inc=sT)
                pbn, bwn = s.bank(2, extra=qcs)
                s.op("tensor", lambda te, pbn=pbn, hh=hh, qc=qc:
                     te.matmul(pbn[0:2, 0:512], s.arb(NBD0 + hh * 4, 2), qc, start=True, stop=False),
                     waits=bwn)
                s.op("tensor", lambda te, pbn=pbn, hh=hh, qs=qs:
                     te.matmul(pbn[0:2, 0:512], s.arb(NBD0 + hh * 4 + 2, 2), qs, start=False, stop=True),
                     inc=sT)
                ctxmm = [sT.at()]
                nsb = s.arf(NSB_F, 512)
                s.op("vector", lambda v, pbn=pbn, nsb=nsb:
                     v.tensor_scalar(nsb[0:2, :], pbn[0:2, 0:512], ATT_EPS, None, ALU.add),
                     waits=ctxmm, inc=sV)
                s.bdone(2, [sV.at()])
                s.op("vector", lambda v, nsb=nsb: v.reciprocal(nsb[0:2, :], nsb[0:2, :]), waits=[sV.at()], inc=sV)
                rnb = s.arb(RNB0, 512)
                s.op("vector", lambda v, nsb=nsb, rnb=rnb: v.tensor_copy(rnb[0:2, :], nsb[0:2, :]),
                     waits=[sV.at()], inc=sV)
                pbb, bwb = s.bank(3, extra=[sV.at()])
                s.op("tensor", lambda te, pbb=pbb, rnb=rnb:
                     te.matmul(pbb[:, 0:512], s.b2h[:, :], rnb[0:2, :], start=True, stop=True),
                     waits=bwb, inc=sT)
                bcb = s.arb(BCB0, 512)
                s.op("scalar", lambda sc, pbb=pbb, bcb=bcb: sc.copy(bcb, pbb[:, 0:512]), waits=[sT.at()], inc=sS)
                s.bdone(3, [sS.at()])
                s.op("vector", lambda v, pbc=pbc, hh=hh, bcb=bcb:
                     v.tensor_mul(s.arb(CTXN0 + hh * 512, 512), pbc[:, 0:512], bcb),
                     waits=[sS.at()], inc=sV)
                s.bdone(1, [sV.at()])
            ctxn_done = [sV.at()]
            banks = []
            for fj in range(NDC):
                pb, bw = s.bank(fj, extra=ctxn_done)
                banks.append(pb)
                for hh in range(NDC):
                    s.op("tensor", lambda te, pb=pb, fj=fj, hh=hh:
                         te.matmul(pb[:, 0:512], s.arb(s.WB0 + 8192 + hh * 1024 + fj * 128, 128),
                                   s.arb(CTXN0 + hh * 512, 512), start=(hh == 0), stop=(hh == NDC - 1)),
                         waits=bw if hh == 0 else (), inc=sT if hh == NDC - 1 else None)
            wout_mm = [sT.at()]
            for fj in range(NDC):
                osb = s.arf(OSB_F, 512)
                s.op("scalar", lambda sc, fj=fj, osb=osb:
                     sc.activation(osb, banks[fj][:, 0:512], AF.Identity, bias=s.bo_a[:, fj:fj + 1]),
                     waits=wout_mm if fj == 0 else [sV.at()], inc=sS)
                ev = [sS.at()]
                for i in range(4):
                    t2 = tg * 4 + i
                    pb2, bw2 = s.bank(fj, extra=ev)
                    s.op("tensor", lambda te, pb2=pb2, osb=osb, i=i:
                         te.transpose(pb2[:, 0:128], osb[:, i * 128:(i + 1) * 128], s.idt_f[:]),
                         waits=bw2, inc=sT)
                    dst = s.xb[:, t2 * DM + fj * 128:t2 * DM + (fj + 1) * 128]
                    s.op("vector", lambda v, pb2=pb2, dst=dst: v.tensor_add(dst, dst, pb2[:, 0:128]),
                         waits=[sT.at()], inc=sV)
                    s.bdone(fj, [sV.at()])
                    if fj == NDC - 1:
                        self.resid_done[t2] = [sV.at()]



    def moe(self):
        s = self
        sL, sG, sT, sV, sS, sC = s.sL, s.sG, s.sT, s.sV, s.sS, s.sC
        SC = s.SC0
        self.lnt(tile_waits=self.resid_done)
        masket = s.arf(SC, 2048)
        cumet = s.arf(SC + 4096, 2048)
        zros = s.arf(SC + 8192, 2048)
        GT0 = SC + 12288
        W1B = SC + 4096
        ECST = SC + 16384
        Z = SC + 17408
        tmp16 = s.arf(Z, 16)
        rcol = s.arf(Z + 32, 16)
        crow = s.arf(Z + 64, 16)
        segi = s.arf(Z + 96, 16)
        segr = s.arf(Z + 128, 16)
        expt = s.arf(Z + 160, 16)
        sume = s.arf(Z + 192, 2)
        mxc = s.arf(Z + 196, 2)
        lrt = s.arf(Z + 200, 16)
        prefb = s.arf(Z + 232, 16)
        c1 = s.arf(Z + 264, 1)
        c2 = s.arf(Z + 266, 1)
        c3 = s.arf(Z + 268, 1)
        c4 = s.arf(Z + 270, 1)
        c5 = s.arf(Z + 272, 1)
        mcum = s.arf(Z + 274, 16)
        mseg = s.arf(Z + 306, 16)
        cumb = s.arf(Z + 338, 8)
        segb = s.arf(Z + 354, 8)
        tmp8 = s.arf(Z + 370, 8)
        pcol16 = s.arf(Z + 386, 1)
        prow = s.arf(Z + 388, 16)
        DSP0 = 40960
        HS0 = 0
        W2G0 = 40960
        EOSF = 57344

        s.op("vector", lambda v: v.memset(zros, 0.0), inc=sV)
        for t in range(NT):
            pb, bw = s.bank(0, extra=s.lnf_done[("own", t)])
            for dc in range(NDC):
                s.op("tensor", lambda te, pb=pb, dc=dc, t=t:
                     te.matmul(pb[:, 0:16], s.arb(s.LNF0 + dc * FM + 128 + t * 128, 128),
                               s.gwT[:, dc * NE:(dc + 1) * NE], start=(dc == 0), stop=(dc == NDC - 1)),
                     waits=bw if dc == 0 else (), inc=sT if dc == NDC - 1 else None)
            lg = [sT.at()]
            mx = mxc[:, 0:1]
            nmx = mxc[:, 1:2]
            s.op("vector", lambda v, pb=pb, mx=mx: v.reduce_max(mx, pb[:, 0:16], axis=AX), waits=lg, inc=sV)
            s.op("vector", lambda v, mx=mx, nmx=nmx: v.tensor_scalar(nmx, mx, -1.0, None, ALU.mult),
                 waits=[sV.at()], inc=sV)
            s.op("scalar", lambda sc, pb=pb, nmx=nmx: sc.activation(expt, pb[:, 0:16], AF.Exp, bias=nmx,
                                                                   accum_out=sume[:, 0:1]),
                 waits=[sV.at()], inc=sS)
            s.op("vector", lambda v, t=t: v.reciprocal(s.gcol[:, t:t + 1], sume[:, 0:1]), waits=[sS.at()], inc=sV)
            s.op("vector", lambda v, pb=pb, mx=mx, t=t:
                 v.tensor_scalar(s.masks[:, t * NE:(t + 1) * NE], pb[:, 0:16], mx, None, ALU.is_equal),
                 waits=[sV.at()], inc=sV)
            s.bdone(0, [sV.at(), sS.at()])
        pb1, bw1 = s.bank(1)
        for t in range(NT):
            s.op("tensor", lambda te, pb1=pb1, t=t:
                 te.matmul(pb1[0:1, 0:16], s.ones_f[:], s.masks[:, t * NE:(t + 1) * NE],
                           start=(t == 0), stop=(t == NT - 1)),
                 waits=(bw1 + [sV.at()]) if t == 0 else (), inc=sT if t == NT - 1 else None)
        s.op("vector", lambda v: v.tensor_copy(crow[0:1, :], pb1[0:1, 0:16]), waits=[sT.at()], inc=sV)
        s.bdone(1, [sV.at()])
        s.op("vector", lambda v: v.tensor_tensor_scan(segi[0:1, :], crow[0:1, :], zros[0:1, 0:16], 0.0,
                                                      ALU.add, ALU.add), waits=[sV.at()], inc=sV)
        s.op("vector", lambda v: v.tensor_sub(segr[0:1, :], segi[0:1, :], crow[0:1, :]), waits=[sV.at()], inc=sV)
        s.op("sync", lambda sy: sy.dma_start(out=s.cnt_in[0:1, :], in_=crow[0:1, :]), waits=[sV.at()],
             inc=sL, inc_n=16)
        s.op("sync", lambda sy: sy.dma_start(out=s.cnt_in[1:2, :], in_=segr[0:1, :]), waits=[sV.at()],
             inc=sL, inc_n=16)
        s.op("gpsimd", lambda g: g.collective_compute(
            "AllGather", ALU.bypass, replica_groups=GROUPS_ALL,
            ins=[s.cnt_in.ap().opt()], outs=[s.cnt_out.ap().opt()]), waits=[sL.at()], inc=sC)
        cnt_ag = [sC.at()]

        for t in range(NT):
            bi = 6 + t % 2
            pb, bw = s.bank(bi, extra=[sV.at()])
            s.op("tensor", lambda te, pb=pb, t=t:
                 te.transpose(pb[0:16, 0:128], s.masks[:, t * NE:(t + 1) * NE], s.idt_f[:]), waits=bw, inc=sT)
            s.op("vector", lambda v, pb=pb, t=t: v.tensor_copy(masket[0:16, t * 128:(t + 1) * 128],
                                                               pb[0:16, 0:128]), waits=[sT.at()], inc=sV)
            s.bdone(bi, [sV.at()])
        s.op("vector", lambda v: v.tensor_tensor_scan(cumet[0:16, :], masket[0:16, :], zros[0:16, :], 0.0,
                                                      ALU.add, ALU.add), waits=[sV.at()], inc=sV)
        cum_done = [sV.at()]
        pb2, bw2 = s.bank(2, extra=[sV.at()])
        s.op("tensor", lambda te, pb2=pb2: te.matmul(pb2[:, 0:16], s.ones1r[:], segr[0:1, :],
                                                     start=True, stop=True), waits=bw2, inc=sT)
        segb_mm = [sT.at()]
        for t in range(NT):
            bi = 6 + t % 2
            pb, bw = s.bank(bi, extra=cum_done)
            s.op("tensor", lambda te, pb=pb, t=t:
                 te.transpose(pb[:, 0:16], cumet[0:16, t * 128:(t + 1) * 128], s.idt_f[0:16, 0:16]),
                 waits=bw, inc=sT)
            trm = [sT.at()]
            s.op("vector", lambda v, pb=pb, t=t: v.tensor_mul(tmp16, s.masks[:, t * NE:(t + 1) * NE],
                                                              pb[:, 0:16]), waits=trm, inc=sV)
            s.bdone(bi, [sV.at()])
            s.op("vector", lambda v, t=t: v.reduce_sum(lrt[:, t:t + 1], tmp16, axis=AX), waits=[sV.at()], inc=sV)
            s.op("vector", lambda v, t=t: v.tensor_scalar(lrt[:, t:t + 1], lrt[:, t:t + 1], -1.0, None, ALU.add),
                 waits=[sV.at()], inc=sV)
            s.op("vector", lambda v, t=t: v.tensor_mul(tmp16, s.masks[:, t * NE:(t + 1) * NE], pb2[:, 0:16]),
                 waits=segb_mm + [sV.at()], inc=sV)
            s.op("vector", lambda v, t=t: v.reduce_sum(c1, tmp16, axis=AX), waits=[sV.at()], inc=sV)
            s.op("vector", lambda v, t=t: v.tensor_add(c2, c1, lrt[:, t:t + 1]), waits=[sV.at()], inc=sV)
            s.op("vector", lambda v, t=t: v.tensor_copy(s.idx_sc[:, t:t + 1], c2), waits=[sV.at()], inc=sV)
            s.op("gpsimd", lambda g, t=t: g.indirect_dma_start(
                out=s.cmp_in[:, :], out_offset=bass.IndirectOffsetOnAxis(ap=s.idx_sc[:, t:t + 1], axis=0),
                in_=s.arb(s.LNT0 + t * DM, DM), in_offset=None),
                waits=[sV.at()], inc=sG, inc_n=16)
        s.bdone(2, [sV.at()])
        s.op("gpsimd", lambda g: g.collective_compute(
            "AllGather", ALU.bypass, replica_groups=GROUPS_ALL,
            ins=[s.cmp_in.ap().opt()], outs=[s.cmp_out.ap().opt()]), waits=[sG.at()], inc=sC)
        tok_ag = [sC.at()]

        s.op("sync", lambda sy: sy.dma_start(out=s.cntT[:, :],
                                             in_=s.cnt_out.ap()[0:16:2, :].rearrange("r e -> e r"),
                                             allow_slow_non_contiguous=True),
             waits=cnt_ag, inc=sL, inc_n=16)
        s.op("sync", lambda sy: sy.dma_start(out=s.segT[:, :],
                                             in_=s.cnt_out.ap()[1:16:2, :].rearrange("r e -> e r"),
                                             allow_slow_non_contiguous=True),
             waits=cnt_ag, inc=sL, inc_n=16)
        ld2 = [sL.at()]
        s.op("vector", lambda v: v.tensor_tensor_scan(s.cumT[:, :], s.cntT[:, :], s.zrow16[:, :], 0.0,
                                                      ALU.add, ALU.add), waits=ld2, inc=sV)
        s.op("vector", lambda v: v.tensor_mul(tmp8[0:16, :], s.cntT[:, :], s.cmask8[:, :]), waits=ld2, inc=sV)
        s.op("vector", lambda v: v.reduce_sum(pcol16[0:16, :], tmp8[0:16, 0:8], axis=AX), waits=[sV.at()], inc=sV)
        pcol_done = [sV.at()]
        bi = 6
        pb, bw = s.bank(bi, extra=pcol_done)
        s.op("tensor", lambda te, pb=pb: te.transpose(pb[0:1, 0:16], pcol16[0:16, 0:1], s.idt_f[0:16, 0:16]),
             waits=bw, inc=sT)
        s.op("vector", lambda v, pb=pb: v.tensor_copy(prow[0:1, :], pb[0:1, 0:16]), waits=[sT.at()], inc=sV)
        s.bdone(bi, [sV.at()])
        pb3, bw3 = s.bank(3, extra=[sV.at()])
        s.op("tensor", lambda te, pb3=pb3: te.matmul(pb3[:, 0:16], s.ones1r[:], prow[0:1, :],
                                                     start=True, stop=True), waits=bw3, inc=sT)
        s.op("vector", lambda v, pb3=pb3: v.tensor_copy(prefb, pb3[:, 0:16]), waits=[sT.at()], inc=sV)
        s.bdone(3, [sV.at()])
        for t in range(NT):
            s.op("vector", lambda v, t=t: v.tensor_mul(tmp16, s.masks[:, t * NE:(t + 1) * NE], prefb),
                 waits=[sV.at()], inc=sV)
            s.op("vector", lambda v: v.reduce_sum(c1, tmp16, axis=AX), waits=[sV.at()], inc=sV)
            s.op("vector", lambda v, t=t: v.tensor_add(c2, c1, lrt[:, t:t + 1]), waits=[sV.at()], inc=sV)
            s.op("vector", lambda v: v.tensor_scalar(c3, c2, float(CAP), None, ALU.is_lt), waits=[sV.at()],
                 inc=sV)
            s.op("vector", lambda v, t=t: v.tensor_mul(s.gk[:, t:t + 1], s.gcol[:, t:t + 1], c3),
                 waits=[sV.at()], inc=sV)
            s.op("vector", lambda v, t=t: v.tensor_mul(tmp16, s.masks[:, t * NE:(t + 1) * NE],
                                                       s.cst[:, s.C_BASEE:s.C_BASEE + 16]), inc=sV)
            s.op("vector", lambda v: v.reduce_sum(c4, tmp16, axis=AX), waits=[sV.at()], inc=sV)
            s.op("vector", lambda v, t=t: v.tensor_mul(tmp16, s.masks[:, t * NE:(t + 1) * NE],
                                                       s.cst[:, s.C_TRASHE:s.C_TRASHE + 16]), waits=[sV.at()],
                 inc=sV)
            s.op("vector", lambda v: v.reduce_sum(c5, tmp16, axis=AX), waits=[sV.at()], inc=sV)
            s.op("vector", lambda v: v.tensor_add(c4, c4, c2), waits=[sV.at()], inc=sV)
            s.op("vector", lambda v: v.tensor_sub(c4, c4, c5), waits=[sV.at()], inc=sV)
            s.op("vector", lambda v: v.scalar_tensor_tensor(c4, c4, c3, c5, ALU.mult, ALU.add),
                 waits=[sV.at()], inc=sV)
            s.op("vector", lambda v, t=t: v.tensor_copy(s.idx_eo[:, t:t + 1], c4), waits=[sV.at()], inc=sV)

        s.op("gpsimd", lambda g: g.iota(rcol[:, 0:10], pattern=[[128, 10]], base=0, channel_multiplier=1,
                                        allow_small_or_imprecise_dtypes=True), inc=s.sP)
        rdy = [s.sP.at(), sV.at()]
        pb4, bw4 = s.bank(4, extra=rdy)
        for el in range(2):
            s.op("tensor", lambda te, pb4=pb4, el=el:
                 te.matmul(pb4[0:1, el * 8:el * 8 + 8], s.myexp_s[:, el:el + 1], s.cumT[:, :],
                           start=(el == 0), stop=False), waits=bw4 if el == 0 else (), inc=None)
            s.op("tensor", lambda te, pb4=pb4, el=el:
                 te.matmul(pb4[0:1, 16 + el * 8:16 + el * 8 + 8], s.myexp_s[:, el:el + 1], s.segT[:, :],
                           start=False, stop=(el == 1)), inc=sT)
        s.op("vector", lambda v, pb4=pb4: v.tensor_copy(mcum[0:1, 0:8], pb4[0:1, 0:8]), waits=[sT.at()], inc=sV)
        s.op("vector", lambda v, pb4=pb4: v.tensor_copy(mcum[0:1, 8:16], pb4[0:1, 8:16]), inc=sV)
        s.op("vector", lambda v, pb4=pb4: v.tensor_copy(mseg[0:1, 0:8], pb4[0:1, 16:24]), inc=sV)
        s.op("vector", lambda v, pb4=pb4: v.tensor_copy(mseg[0:1, 8:16], pb4[0:1, 24:32]), inc=sV)
        s.bdone(4, [sV.at()])
        for el in range(2):
            pb5, bw5 = s.bank(5, extra=[sV.at()])
            s.op("tensor", lambda te, pb5=pb5, el=el:
                 te.matmul(pb5[:, 0:8], s.ones1r[:], mcum[0:1, el * 8:el * 8 + 8],
                           start=True, stop=False), waits=bw5, inc=sT)
            s.op("tensor", lambda te, pb5=pb5, el=el:
                 te.matmul(pb5[:, 8:16], s.ones1r[:], mseg[0:1, el * 8:el * 8 + 8],
                           start=False, stop=True), inc=sT)
            s.op("vector", lambda v, pb5=pb5: v.tensor_copy(cumb, pb5[:, 0:8]), waits=[sT.at()], inc=sV)
            s.op("vector", lambda v, pb5=pb5: v.tensor_copy(segb, pb5[:, 8:16]), waits=[sT.at()], inc=sV)
            s.bdone(5, [sV.at()])
            for st in range(10):
                r_ = rcol[:, st:st + 1]
                s.op("vector", lambda v, r_=r_: v.tensor_scalar(tmp8, cumb, r_, None, ALU.is_le),
                     waits=[sV.at()], inc=sV)
                s.op("vector", lambda v: v.reduce_sum(c1, tmp8, axis=AX), waits=[sV.at()], inc=sV)
                s.op("vector", lambda v: v.tensor_mul(tmp8, tmp8, cumb), waits=[sV.at()], inc=sV)
                s.op("vector", lambda v: v.reduce_max(c2, tmp8, axis=AX), waits=[sV.at()], inc=sV)
                s.op("vector", lambda v: v.tensor_scalar(tmp8, s.cst[:, s.C_IOTA8:s.C_IOTA8 + 8], c1, None,
                                                         ALU.is_equal), waits=[sV.at()], inc=sV)
                s.op("vector", lambda v: v.tensor_mul(tmp8, tmp8, segb), waits=[sV.at()], inc=sV)
                s.op("vector", lambda v: v.reduce_sum(c3, tmp8, axis=AX), waits=[sV.at()], inc=sV)
                s.op("vector", lambda v: v.tensor_scalar(c4, c1, float(T), None, ALU.mult), waits=[sV.at()], inc=sV)
                s.op("vector", lambda v: v.tensor_add(c4, c4, c3), waits=[sV.at()], inc=sV)
                s.op("vector", lambda v, r_=r_: v.tensor_add(c4, c4, r_), waits=[sV.at()], inc=sV)
                s.op("vector", lambda v: v.tensor_sub(c4, c4, c2), waits=[sV.at()], inc=sV)
                s.op("vector", lambda v: v.tensor_scalar_min(c4, c4, float(NCORE * T - 1)), waits=[sV.at()], inc=sV)
                s.op("vector", lambda v, el=el, st=st: v.tensor_copy(s.dgi[:, el * 10 + st:el * 10 + st + 1], c4),
                     waits=[sV.at()], inc=sV)
        dgi_done = [sV.at()]

        prev_eo_done = None
        for el in range(2):
            for st in range(10):
                gt = s.arb(GT0 + (st % 2) * 1024, 1024)
                first = dgi_done + tok_ag if (el == 0 and st == 0) else []
                if el == 1 and st == 0 and prev_eo_done is not None:
                    first = first + prev_eo_done
                s.op("gpsimd", lambda g, el=el, st=st, gt=gt: g.indirect_dma_start(
                    out=gt, out_offset=None, in_=s.cmp_out[:, :],
                    in_offset=bass.IndirectOffsetOnAxis(ap=s.dgi[:, el * 10 + st:el * 10 + st + 1], axis=0)),
                    waits=[sS.at()] + first, inc=sG, inc_n=16)
                gw_ = [sG.at()]
                for dc in range(NDC):
                    bi = 6 + dc % 2
                    pb, bw = s.bank(bi, extra=gw_)
                    s.op("tensor", lambda te, pb=pb, gt=gt, dc=dc:
                         te.transpose(pb[:, 0:64].bitcast(BF16), gt[:, dc * 128:(dc + 1) * 128], s.idt_b[:]),
                         waits=bw, inc=sT)
                    s.op("scalar", lambda sc, pb=pb, dc=dc, st=st:
                         sc.copy(s.arb(DSP0 + dc * CAP + st * 128, 128), pb[:, 0:64].bitcast(BF16)),
                         waits=[sT.at()], inc=sS)
                    s.bdone(bi, [sS.at()])
            disp_done = [sS.at()]
            if self.dbg and self.moe_calls == 0 and el == 0:
                s.op("sync", lambda sy: sy.dma_start(out=s.dbg_disp[:, :], in_=s.arb(DSP0, NDC * CAP)),
                     waits=disp_done, inc=sL, inc_n=16)
                s.op("gpsimd", lambda g: None, waits=[sL.at()])
            burst_end = {}
            mmdone = {}

            def w1_burst(b):
                for fc in range(4 * b, 4 * b + 4):
                    prev = [mmdone[b - 2]] if (b >= 2 and fc == 4 * b) else []
                    s.op("gpsimd", lambda g, el=el, fc=fc:
                         g.dma_start(out=s.arb(W1B + (fc % 8) * 1024, 1024), in_=s.w1_m[el, fc, :, :]),
                         waits=(disp_done if b == 0 and fc == 0 else []) + prev, inc=sG, inc_n=16)
                burst_end[b] = sG.at()
                s.op("gpsimd", lambda g: None, waits=[burst_end[b]])
            w1_burst(0)
            w1_burst(1)
            for b in range(8):
                for fc in range(4 * b, 4 * b + 4):
                    wb = s.arb(W1B + (fc % 8) * 1024, 1024)
                    for sl, (c0, w) in enumerate(((0, 512), (512, 512), (1024, 256))):
                        bi = (fc * 3 + sl) % 4 + 4
                        pb, bw = s.bank(bi, extra=[burst_end[b]] + (disp_done if fc == 0 else []))
                        for dc in range(NDC):
                            s.op("tensor", lambda te, pb=pb, wb=wb, dc=dc, c0=c0, w=w:
                                 te.matmul(pb[:, 0:w], wb[:, dc * 128:(dc + 1) * 128],
                                           s.arb(DSP0 + dc * CAP + c0, w), start=(dc == 0), stop=(dc == NDC - 1)),
                                 waits=bw if dc == 0 else (), inc=sT if dc == NDC - 1 else None)
                        s.op("scalar", lambda sc, pb=pb, fc=fc, c0=c0, w=w:
                             sc.activation(s.arb(HS0 + fc * CAP + c0, w), pb[:, 0:w], AF.Gelu_apprx_tanh),
                             waits=[sT.at()], inc=sS)
                        s.bdone(bi, [sS.at()])
                mmdone[b] = sT.at()
                if b + 2 <= 7:
                    w1_burst(b + 2)
            h_done = [sS.at()]
            if self.dbg and self.moe_calls == 0 and el == 0:
                s.op("sync", lambda sy: sy.dma_start(out=s.dbg_h[:, :], in_=s.arb(HS0, NFC * CAP)),
                     waits=h_done, inc=sL, inc_n=16)
                s.op("tensor", lambda te: None, waits=[sL.at()])
                s.op("gpsimd", lambda g: None, waits=[sL.at()])
            eosb = s.arf(EOSF, 5120)
            w2done = {}
            w2l = {}

            def w2_load(k):
                buf = W2G0 + (k % 2) * 8192
                fg = k % 4
                for j in range(8):
                    prevw = [w2done[k - 2]] if (k >= 2 and j == 0) else []
                    s.op("gpsimd", lambda g, el=el, fg=fg, j=j, buf=buf:
                         g.dma_start(out=s.arb(buf + j * 1024, 1024),
                                     in_=s.w2_m[el, (fg * 8 + j) * 128:(fg * 8 + j + 1) * 128, :]),
                         waits=(h_done if k == 0 and j == 0 else []) + prevw, inc=sG, inc_n=16)
                w2l[k] = sG.at()
                s.op("gpsimd", lambda g: None, waits=[w2l[k]])
            w2_load(0)
            w2_load(1)
            for half in range(2):
                for fg in range(4):
                    k = half * 4 + fg
                    buf = W2G0 + (k % 2) * 8192
                    for si in range(5):
                        ss = half * 5 + si
                        for dh in range(2):
                            bi = (si * 2 + dh) % 4
                            pb, bw = s.bank(bi, extra=[w2l[k]] + (h_done if k == 0 else []))
                            for j in range(8):
                                s.op("tensor", lambda te, pb=pb, fg=fg, j=j, ss=ss, dh=dh, buf=buf:
                                     te.matmul(pb[:, 0:512], s.arb(HS0 + (fg * 8 + j) * CAP + ss * 128, 128),
                                               s.arb(buf + j * 1024 + dh * 512, 512),
                                               start=(j == 0), stop=(j == 7)),
                                     waits=bw if j == 0 else (), inc=sT if j == 7 else None)
                            dst = eosb[:, si * 1024 + dh * 512:si * 1024 + dh * 512 + 512]
                            if fg == 0:
                                s.op("vector", lambda v, pb=pb, dst=dst: v.tensor_copy(dst, pb[:, 0:512]),
                                     waits=[sT.at()], inc=sV)
                            else:
                                s.op("vector", lambda v, pb=pb, dst=dst: v.tensor_add(dst, dst, pb[:, 0:512]),
                                     waits=[sT.at()], inc=sV)
                            s.bdone(bi, [sV.at()])
                    w2done[k] = sT.at()
                    if k + 2 <= 7:
                        w2_load(k + 2)
                for si in range(5):
                    ss = half * 5 + si
                    ec = s.arb(ECST + (si % 2) * 1024, 1024)
                    s.op("vector", lambda v, si=si, ec=ec: v.tensor_copy(ec, eosb[:, si * 1024:(si + 1) * 1024]),
                         waits=[sV.at(), sL.at()], inc=sV)
                    s.op("sync", lambda sy, el=el, ss=ss, ec=ec:
                         sy.dma_start(out=s.eo_in[el * CAP + ss * 128:el * CAP + (ss + 1) * 128, :], in_=ec),
                         waits=[sV.at()], inc=sL, inc_n=16)
            prev_eo_done = [sT.at(), sV.at()]
        zb = s.arb(GT0, 1024)
        s.op("vector", lambda v, zb=zb: v.memset(zb, 0.0), inc=sV)
        s.op("sync", lambda sy, zb=zb: sy.dma_start(out=s.eo_in[2 * CAP:EO_STRIDE, :], in_=zb[0:64, :]),
             waits=[sV.at()], inc=sL, inc_n=16)
        s.op("gpsimd", lambda g: g.collective_compute(
            "AllGather", ALU.bypass, replica_groups=GROUPS_ALL,
            ins=[s.eo_in.ap().opt()], outs=[s.eo_out.ap().opt()]), waits=[sL.at()], inc=sC)
        eo_ag = [sC.at()]
        prev_pair_resid = []
        for pt in range(NT // 2):
            pair = (2 * pt, 2 * pt + 1)
            extra = [s.sL.at()] if pt == 0 else []
            for t in pair:
                s.op("gpsimd", lambda g, t=t: g.indirect_dma_start(
                    out=s.arb(GT0 + (t % 2) * 1024, 1024), out_offset=None, in_=s.eo_out[:, :],
                    in_offset=bass.IndirectOffsetOnAxis(ap=s.idx_eo[:, t:t + 1], axis=0)),
                    waits=(eo_ag if pt == 0 and t == 2 * pt else []) + extra + prev_pair_resid,
                    inc=sG, inc_n=16)
                extra = []
            pair_end = s.sG.at()
            s.op("gpsimd", lambda g: None, waits=[pair_end])
            prev_pair_resid = []
            for t in pair:
                rd2 = self.resid_done[t] if self.resid_done[t] else []
                xt = s.xb[:, t * DM:(t + 1) * DM]
                s.op("vector", lambda v, xt=xt, t=t:
                     v.scalar_tensor_tensor(xt, s.arb(GT0 + (t % 2) * 1024, 1024), s.gk[:, t:t + 1], xt,
                                            ALU.mult, ALU.add),
                     waits=[pair_end] + rd2 + s.lnf_done[("own", t)], inc=sV)
                self.resid_done[t] = [sV.at()]
                prev_pair_resid = [sV.at()]
        self.moe_calls += 1
        if self.dbg and self.moe_calls == 1:
            def dmp(dst, srcap):
                s.op("sync", lambda sy, dst=dst, srcap=srcap: sy.dma_start(out=dst, in_=srcap),
                     inc=sL, inc_n=16)
            dmp(s.dbg_dgi[:, :], s.dgi[:, :])
            dmp(s.dbg_ieo[:, :], s.idx_eo[:, :])
            dmp(s.dbg_isc[:, :], s.idx_sc[:, :])
            dmp(s.dbg_gk[:, :], s.gk[:, :])
            dmp(s.dbg_cnt[:, :], s.cntT[:, :])
            dmp(s.dbg_seg[:, :], s.segT[:, :])
            dmp(s.dbg_cmp[:, :], s.cmp_in.ap())
            dmp(s.dbg_cmpo[:, :], s.cmp_out.ap())
            dmp(s.dbg_eoin[:, :], s.eo_in.ap())
            self.fence()


    def attn_b(self):
        s = self
        sL, sG, sT, sV, sS, sC = s.sL, s.sG, s.sT, s.sV, s.sS, s.sC
        SC = s.SC0
        QF, KF, VF, VTM = SC, SC + 2304, SC + 4608, SC + 6912
        MB0 = SC + 9216
        SCB_F = SC + 15360
        EXPT_F = SC + 16128
        ATT0 = SC + 16896
        ATTT0 = SC + 17280
        IOT_F = SC + 17664
        KP_F = SC + 18432
        HXL = SC + 4096
        HXR = SC + 8192

        s.op("sync", lambda sy: sy.dma_start(out=s.halo_in[0:128, :], in_=s.xb[:, 0:DM]),
             waits=self.resid_done[0], inc=sL, inc_n=16)
        s.op("sync", lambda sy: sy.dma_start(out=s.halo_in[128:256, :], in_=s.xb[:, (NT - 1) * DM:NT * DM]),
             waits=self.resid_done[NT - 1], inc=sL, inc_n=16)
        s.op("gpsimd", lambda g: g.collective_compute(
            "AllGather", ALU.bypass, replica_groups=GROUPS_ALL,
            ins=[s.halo_in.ap().opt()], outs=[s.halo_out.ap().opt()]), waits=[sL.at()], inc=sC)
        halo_ag = [sC.at()]
        s.op("gpsimd", lambda g: g.indirect_dma_start(
            out=s.arf(HXL, DM), out_offset=None, in_=s.halo_out[:, :],
            in_offset=bass.IndirectOffsetOnAxis(ap=s.hl_idx[:, 0:1], axis=0)),
            waits=halo_ag + [s.sT.at()], inc=sG, inc_n=16)
        s.op("gpsimd", lambda g: g.indirect_dma_start(
            out=s.arf(HXR, DM), out_offset=None, in_=s.halo_out[:, :],
            in_offset=bass.IndirectOffsetOnAxis(ap=s.hr_idx[:, 0:1], axis=0)),
            waits=[s.sG.at()], inc=sG, inc_n=16)
        s.op("gpsimd", lambda g: None, waits=[s.sG.at()])
        self.lnt(halo=True, tile_waits=self.resid_done)

        iot = s.arf(IOT_F, 384)
        kp = s.arf(KP_F, 384)
        ma = s.arf(SCB_F, 384)
        mb_ = s.arf(EXPT_F, 384)
        s.op("gpsimd", lambda g: g.iota(iot, pattern=[[1, 384]], base=-128, channel_multiplier=-1,
                                        allow_small_or_imprecise_dtypes=True),
             waits=[s.sV.at()], inc=s.sP)
        iota_done = [s.sP.at()]
        for t in range(NT):
            s.op("gpsimd", lambda g, t=t: g.iota(kp, pattern=[[1, 384]], base=t * 128 - 128, channel_multiplier=0,
                                                 allow_small_or_imprecise_dtypes=True),
                 waits=[sV.at()], inc=s.sP)
            s.op("vector", lambda v: v.tensor_scalar(kp, kp, s.cst[:, s.C_P0:s.C_P0 + 1], None, ALU.add),
                 waits=[s.sP.at()], inc=sV)
            s.op("vector", lambda v: v.tensor_scalar(ma, iot, -128.0, None, ALU.is_ge),
                 waits=iota_done + [sV.at()], inc=sV)
            s.op("vector", lambda v: v.tensor_scalar(mb_, iot, 128.0, None, ALU.is_le), inc=sV)
            s.op("vector", lambda v: v.tensor_mul(ma, ma, mb_), waits=[sV.at()], inc=sV)
            s.op("vector", lambda v: v.tensor_scalar(mb_, kp, 0.0, None, ALU.is_ge), waits=[sV.at()], inc=sV)
            s.op("vector", lambda v: v.tensor_mul(ma, ma, mb_), waits=[sV.at()], inc=sV)
            s.op("vector", lambda v: v.tensor_scalar(mb_, kp, float(SEQ), None, ALU.is_lt), waits=[sV.at()], inc=sV)
            s.op("vector", lambda v: v.tensor_mul(ma, ma, mb_), waits=[sV.at()], inc=sV)
            s.op("vector", lambda v, t=t: v.tensor_scalar(s.arb(MB0 + t * 384, 384), ma, 1.0e30, -1.0e30,
                                                          ALU.mult, ALU.add),
                 waits=[sV.at()] + (s.lnf_done[("hr", 0)] + s.lnf_done[("hl", 0)] if t == 0 else []), inc=sV)
        mb_done = [sV.at()]

        for hh in range(NDC):
            s.op("gpsimd", lambda g, hh=hh: g.dma_start(out=s.arb(s.WB0 + hh * 1024, 1024),
                                                        in_=s.wproj_b[hh * 128:(hh + 1) * 128, :]),
                 waits=[s.sG.at()] if hh == 0 else (), inc=sG, inc_n=16)
        wp_ld = [sG.at()]

        TT5 = ((0, 512), (512, 512), (1024, 512), (1536, 512), (2048, 256))
        for hh in range(NDC):
            s.op("gpsimd", lambda g, hh=hh: g.dma_start(out=s.arb(SC + 19200, 0) if False else
                                                        s.arb(s.WB0 + 8192, 3072), in_=s.wqkv_b[hh, :, :]),
                 waits=[sT.at()], inc=sG, inc_n=16)
            whh_ld = [sG.at()]
            for part, dst0 in ((0, QF), (1, KF), (2, VF)):
                for it, (c0, w) in enumerate(TT5):
                    bi = it % 2
                    pb, bw = s.bank(bi, extra=whh_ld + (s.lnf_done_all if part == 0 and it == 0 else []))
                    for dc in range(NDC):
                        s.op("tensor", lambda te, pb=pb, part=part, dc=dc, c0=c0, w=w:
                             te.matmul(pb[:, 0:w], s.arb(s.WB0 + 8192 + (part * 8 + dc) * 128, 128),
                                       s.arb(s.LNF0 + dc * FM + c0, w), start=(dc == 0), stop=(dc == NDC - 1)),
                             waits=bw if dc == 0 else (), inc=sT if dc == NDC - 1 else None)
                    s.op("scalar", lambda sc, pb=pb, dst0=dst0, c0=c0, w=w:
                         sc.copy(s.arb(dst0 + c0, w), pb[:, 0:w]), waits=[sT.at()], inc=sS)
                    s.bdone(bi, [sS.at()])
            qkv_done = [sS.at()]
            for j in range(18):
                bi = 6 + j % 2
                pb, bw = s.bank(bi, extra=qkv_done)
                s.op("tensor", lambda te, pb=pb, j=j:
                     te.transpose(pb[:, 0:64].bitcast(BF16), s.arb(VF + j * 128, 128), s.idt_b[:]),
                     waits=bw, inc=sT)
                s.op("scalar", lambda sc, pb=pb, j=j: sc.copy(s.arb(VTM + j * 128, 128),
                                                              pb[:, 0:64].bitcast(BF16)),
                     waits=[sT.at()], inc=sS)
                s.bdone(bi, [sS.at()])
            vtm_done = [sS.at()]
            for t in range(NT):
                for head in range(2):
                    po = head * 64
                    qcol = 128 + t * 128
                    pb3, bw3 = s.bank(3, extra=vtm_done + mb_done)
                    s.op("tensor", lambda te, pb3=pb3, po=po, qcol=qcol, t=t:
                         te.matmul(pb3[:, 0:384], s.arb(QF + qcol, 128)[po:po + 64, :],
                                   s.arb(KF + t * 128, 384)[po:po + 64, :], start=True, stop=True),
                         waits=bw3, inc=sT)
                    scmm = [sT.at()]
                    scb = s.arf(SCB_F, 384)
                    s.op("vector", lambda v, pb3=pb3, t=t: v.tensor_add(scb, pb3[:, 0:384],
                                                                        s.arb(MB0 + t * 384, 384)),
                         waits=scmm, inc=sV)
                    s.bdone(3, [sV.at()])
                    mx = s.csm[:, 60:61]
                    nmx = s.csm[:, 61:62]
                    sume = s.csm[:, 62:63]
                    s.op("vector", lambda v, mx=mx: v.reduce_max(mx, scb, axis=AX), waits=[sV.at()], inc=sV)
                    s.op("vector", lambda v, mx=mx, nmx=nmx: v.tensor_scalar(nmx, mx, -0.125, None, ALU.mult),
                         waits=[sV.at()], inc=sV)
                    expt = s.arf(EXPT_F, 384)
                    s.op("scalar", lambda sc, nmx=nmx, expt=expt:
                         sc.activation(expt, scb, AF.Exp, bias=nmx, scale=0.125, accum_out=sume),
                         waits=[sV.at()], inc=sS)
                    s.op("vector", lambda v, sume=sume: v.reciprocal(s.csm[:, 63:64], sume), waits=[sS.at()],
                         inc=sV)
                    att = s.arb(ATT0, 384)
                    s.op("vector", lambda v, att=att, expt=expt:
                         v.tensor_scalar(att, expt, s.csm[:, 63:64], None, ALU.mult), waits=[sV.at()], inc=sV)
                    att_done = [sV.at()]
                    for j3 in range(3):
                        bi = 6 + j3 % 2
                        pb, bw = s.bank(bi, extra=att_done)
                        s.op("tensor", lambda te, pb=pb, att=att, j3=j3:
                             te.transpose(pb[:, 0:64].bitcast(BF16), att[:, j3 * 128:(j3 + 1) * 128], s.idt_b[:]),
                             waits=bw, inc=sT)
                        s.op("scalar", lambda sc, pb=pb, j3=j3: sc.copy(s.arb(ATTT0 + j3 * 128, 128),
                                                                        pb[:, 0:64].bitcast(BF16)),
                             waits=[sT.at()], inc=sS)
                        s.bdone(bi, [sS.at()])
                    attT_done = [sS.at()]
                    pb4, bw4 = s.bank(4, extra=attT_done)
                    for j3 in range(3):
                        s.op("tensor", lambda te, pb4=pb4, j3=j3, t=t, po=po, head=head:
                             te.matmul(pb4[po:po + 64, 0:128], s.arb(VTM + (t + j3) * 128 + po, 64),
                                       s.arb(ATTT0 + j3 * 128, 128),
                                       start=(j3 == 0), stop=(j3 == 2),
                                       tile_position=(0, po)),
                             waits=bw4 if j3 == 0 else (), inc=sT if j3 == 2 else None)
                    avmm = [sT.at()]
                    if head == 1:
                        s.op("scalar", lambda sc, pb4=pb4, hh=hh, t=t:
                             sc.copy(s.arb(s.LNT0 + hh * 2048 + t * 128, 128), pb4[:, 0:128]),
                             waits=avmm, inc=sS)
                        s.bdone(4, [sS.at()])
            self.fence()
        for tg in range(4):
            banks = []
            bw_all = [sV.at(), sS.at()]
            for fj in range(NDC):
                pb, bw = s.bank(fj, extra=bw_all)
                banks.append(pb)
                for hh in range(NDC):
                    s.op("tensor", lambda te, pb=pb, fj=fj, hh=hh, tg=tg:
                         te.matmul(pb[:, 0:512], s.arb(s.WB0 + hh * 1024 + fj * 128, 128),
                                   s.arb(s.LNT0 + hh * 2048 + tg * 512, 512), start=(hh == 0), stop=(hh == NDC - 1)),
                         waits=(bw + [s.sG.at()]) if hh == 0 else (), inc=sT if hh == NDC - 1 else None)
            pr_mm = [sT.at()]
            for fj in range(NDC):
                osb = s.arf(SCB_F, 512)
                s.op("scalar", lambda sc, fj=fj, osb=osb:
                     sc.activation(osb, banks[fj][:, 0:512], AF.Identity, bias=s.bp_b[:, fj:fj + 1]),
                     waits=pr_mm if fj == 0 else [sV.at()], inc=sS)
                ev = [sS.at()]
                for i in range(4):
                    t2 = tg * 4 + i
                    pb2, bw2 = s.bank(fj, extra=ev)
                    s.op("tensor", lambda te, pb2=pb2, osb=osb, i=i:
                         te.transpose(pb2[:, 0:128], osb[:, i * 128:(i + 1) * 128], s.idt_f[:]),
                         waits=bw2, inc=sT)
                    dst = s.xb[:, t2 * DM + fj * 128:t2 * DM + (fj + 1) * 128]
                    s.op("vector", lambda v, pb2=pb2, dst=dst: v.tensor_add(dst, dst, pb2[:, 0:128]),
                         waits=[sT.at()], inc=sV)
                    s.bdone(fj, [sV.at()])
                    if fj == NDC - 1:
                        self.resid_done[t2] = [sV.at()]

    def store(self):
        s = self
        for t in range(NT):
            s.op("sync", lambda sy, t=t: sy.dma_start(out=s.y_out[t * 128:(t + 1) * 128, :],
                                                      in_=s.xb[:, t * DM:(t + 1) * DM]),
                 waits=self.resid_done[t], inc=s.sL, inc_n=16)
        s.op("sync", lambda sy: None, waits=[s.sL.at()])

    def emit(self, block):
        @block.sync
        def _(sy):
            for f in self.ops["sync"]:
                f(sy)

        @block.tensor
        def _(te):
            for f in self.ops["tensor"]:
                f(te)

        @block.vector
        def _(v):
            for f in self.ops["vector"]:
                f(v)

        @block.scalar
        def _(sc):
            for f in self.ops["scalar"]:
                f(sc)

        @block.gpsimd
        def _(g):
            for f in self.ops["gpsimd"]:
                f(g)


def build(n_sub=4):
    nc = bass.Bass("TRN2", target_bir_lowering=False)
    mk = MK(nc, n_sub)
    with ExitStack() as ctx:
        mk.declare(ctx)
        block = ctx.enter_context(nc.Block())
        mk.p0()
        mk.resid_done = [[] for _ in range(NT)]
        if n_sub >= 1:
            mk.attn_a()
        if n_sub >= 2:
            mk.moe()
        if n_sub >= 3:
            mk.attn_b()
        if n_sub >= 4:
            mk.moe()
        mk.store()
        mk.emit(block)
    return nc


def kernel(**inputs):
    maps = shard_inputs(inputs)
    nc = build(4)
    res = run_bass_kernel_spmd(nc, maps, core_ids=list(range(NCORE)))
    y = np.concatenate([r["y_out"] for r in res.results], axis=0)
    return y.reshape(2, SEQ, DM)
